# version 02
# baseline (speedup 1.0000x reference)
import numpy as np
import ml_dtypes
import concourse.bass as bass
import concourse.mybir as mybir
from concourse.bass_utils import run_bass_kernel_spmd

F32 = mybir.dt.float32
BF16 = mybir.dt.bfloat16
AF = mybir.ActivationFunctionType
ALU = mybir.AluOpType
AX = mybir.AxisListType
ds = bass.ds
BFNP = ml_dtypes.bfloat16

B, S, D = 4, 2048, 2048
NH, HD = 16, 128
FF = 5120
NCORES = 8
C_STAB = 15.0
MASKVAL = -30000.0
LN_EPS = 1e-5
KB = 1024

PAIRS = [[0, 1], [2, 3], [4, 5], [6, 7]]
ALL8 = [list(range(8))]


def _split3(x):
    x = np.asarray(x, np.float32)
    p1 = x.astype(BFNP).astype(np.float32)
    r = x - p1
    p2 = r.astype(BFNP).astype(np.float32)
    p3 = (r - p2).astype(BFNP).astype(np.float32)
    return np.stack([p1, p2, p3]).astype(np.float32)


def _slopes():
    return np.asarray([1.0 / 2.0 ** (i / 2.0) for i in range(NH)], dtype=np.float64)


class P:
    ENGS = ("tensor", "scalar", "vector", "gpsimd", "sync")

    def __init__(self, nc):
        self.nc = nc
        self.ops = {e: [] for e in self.ENGS}
        self.cnt = {}
        self.sems = {}
        self.waited = {e: {} for e in self.ENGS}

    def sem(self, name):
        self.sems[name] = self.nc.alloc_semaphore(name=name)
        self.cnt[name] = 0

    def now(self, name):
        return self.cnt[name]

    def E(self, eng, fn, waits=(), inc=None):
        w = []
        for s, v in waits:
            if v > self.waited[eng].get(s, 0):
                self.waited[eng][s] = v
                w.append((s, v))
        tgt = None
        if inc is not None:
            self.cnt[inc[0]] += inc[1]
            tgt = self.cnt[inc[0]]

        def runner(e, fn=fn, w=tuple(w), inc=inc):
            for s, v in w:
                e.wait_ge(self.sems[s], v)
            ins = fn(e)
            if inc is not None and ins is not None:
                ins.then_inc(self.sems[inc[0]], inc[1])

        self.ops[eng].append(runner)
        return tgt


def build_program():
    nc = bass.Bass("TRN2", target_bir_lowering=False)

    dxT = nc.dram_tensor("xT", [D, S], F32, kind="ExternalInput")
    dx_own = nc.dram_tensor("x_own", [1024, D], F32, kind="ExternalInput")
    dwq = nc.dram_tensor("wq", [D, 1024], F32, kind="ExternalInput")
    dwk = nc.dram_tensor("wk", [D, 1024], F32, kind="ExternalInput")
    dwv = nc.dram_tensor("wv", [D, 1024], F32, kind="ExternalInput")
    dbqk = nc.dram_tensor("bqk", [128, 16], F32, kind="ExternalInput")
    dbv3 = nc.dram_tensor("bv3", [3, 1024], F32, kind="ExternalInput")
    doutw = nc.dram_tensor("outw", [D, D], F32, kind="ExternalInput")
    doutb3 = nc.dram_tensor("outb3", [3, D], F32, kind="ExternalInput")
    dwfcT = nc.dram_tensor("wfcT_sl", [256, FF], F32, kind="ExternalInput")
    dwpjT = nc.dram_tensor("wprojT_sl", [640, D], F32, kind="ExternalInput")
    dmasks = nc.dram_tensor("masks", [128, 4 * 512], F32, kind="ExternalInput")
    dalibi = nc.dram_tensor("alibi2", [128, 8 * 20], F32, kind="ExternalInput")
    dnegsi = nc.dram_tensor("negsi3q", [3, 8 * 512], F32, kind="ExternalInput")
    dident = nc.dram_tensor("identf", [128, 128], F32, kind="ExternalInput")
    dout = nc.dram_tensor("out", [1024, D], F32, kind="ExternalOutput")

    outw_bf = nc.dram_tensor("outw_bf", [D, D], BF16)
    am_in = nc.dram_tensor("am_in", [1, 2], F32)
    am_out = nc.dram_tensor("am_out", [1, 2], F32, addr_space="Shared")
    fc_agi = nc.dram_tensor("fc_agi", [256, FF], BF16)
    fc_ago = nc.dram_tensor("fc_ago", [D, FF], BF16, addr_space="Shared")
    pj_agi = nc.dram_tensor("pj_agi", [16 * 640, 128], BF16)
    pj_ago = nc.dram_tensor("pj_ago", [16 * 640 * 8, 128], BF16, addr_space="Shared")
    ctx_agi = nc.dram_tensor("ctx_agi", [1024, 1024], BF16)
    ctx_ago = nc.dram_tensor("ctx_ago", [2048, 1024], BF16)

    BASE = 16384

    def at(name, shape, dtype, off_b):
        return nc.alloc_sbuf_tensor_at(name, shape, dtype, offset=BASE + off_b)

    wq_sb = at("wq_sb", [128, 16 * 1024], BF16, 0 * KB)
    wk_sb = at("wk_sb", [128, 16 * 1024], BF16, 32 * KB)
    xt_sb = [at(f"xt{i}", [128, 16 * 512], BF16, (64 + 16 * i) * KB) for i in range(2)]
    qT_sb = at("qT_sb", [128, 8 * 2048], BF16, 96 * KB)
    kT_sb = at("kT_sb", [128, 8 * 2048], BF16, 128 * KB)
    wv_sb = at("wv_sb", [128, 16 * 1024], BF16, 0 * KB)
    v_sb = at("v_sb", [128, 16 * 1024], BF16, 32 * KB)
    ctx_sb = at("ctx_sb", [128, 8 * 2048], BF16, 0 * KB)
    masks_sb = at("masks_sb", [128, 4 * 512], F32, 64 * KB)
    negsi_sb = at("negsi_sb", [3, 8 * 512], BF16, 72 * KB)
    alibi_sb = at("alibi_sb", [128, 8 * 20], F32, 80 * KB)
    pT_sb = [at(f"pT{i}", [128, 512], BF16, (81 + i) * KB) for i in range(2)]
    rl_sb = at("rl_sb", [1, 512], F32, 83 * KB)
    bcst_sb = at("bcst_sb", [128, 512], F32, 85 * KB)
    ctx2_sb = at("ctx2_sb", [128, 8 * 1024], BF16, 32 * KB)
    ctxC_sb = at("ctxC_sb", [128, 8 * 1024], BF16, 48 * KB)
    ow_sb = [at(f"ow{i}", [128, 2048], BF16, (64 + 4 * i) * KB) for i in range(2)]
    x_sb = at("x_sb", [128, 8 * 2048], F32, 96 * KB)
    h_sb = at("h_sb", [128, 40 * 512], BF16, 0 * KB)
    fcs_sb = [at(f"fcs{i}", [128, 16 * 512], BF16, (40 + 16 * i) * KB) for i in range(2)]
    pjs_sb = [at(f"pjs{i}", [128, 40 * 128], BF16, (40 + 10 * i) * KB) for i in range(2)]
    xlnT_sb = at("xlnT_sb", [128, 16 * 512], BF16, 72 * KB)
    relu_sb = at("relu_sb", [128, 512], F32, 88 * KB)
    o2_sb = at("o2_sb", [128, 512], F32, 90 * KB)
    lnsq_sb = at("lnsq_sb", [128, 512], F32, 93 * KB)
    lncol_sb = at("lncol_sb", [128, 64], F32, 95 * KB)
    ln2col_sb = at("ln2col_sb", [128, 64], F32, 95 * KB + 256)
    tin_sb = [at(f"tin{i}", [128, 512], F32, (161 + 2 * i) * KB) for i in range(2)]
    ts1_sb = at("ts1_sb", [128, 512], F32, 165 * KB)
    tout_sb = at("tout_sb", [128, 512], BF16, 167 * KB)
    cur = [168 * KB]

    def cst(name, shape, dtype):
        nonlocal_off = cur[0]
        t = at(name, shape, dtype, nonlocal_off)
        nbytes = int(np.prod(shape[1:])) * (2 if dtype == BF16 else 4)
        cur[0] += (nbytes + 31) // 32 * 32
        return t

    bqk_sb = cst("bqk_sb", [128, 16], F32)
    absacc_sb = cst("absacc_sb", [128, 44], F32)
    amg_sb = cst("amg_sb", [1, 8], F32)
    thr_sb = cst("thr_sb", [128, 6], F32)
    thrrow_sb = cst("thrrow_sb", [1, 6], F32)
    onescol_sb = cst("onescol_sb", [128, 1], F32)
    ones3_sb = cst("ones3_sb", [3, 128], BF16)
    ones128_sb = cst("ones128_sb", [128, 1], BF16)
    bv3_sb = cst("bv3_sb", [3, 1024], BF16)
    outb3_sb = cst("outb3_sb", [3, 2048], BF16)
    identf_sb = cst("identf_sb", [128, 128], F32)
    onesf_sb = cst("onesf_sb", [1, 128], F32)
    assert cur[0] <= 176 * KB, cur[0]
    assert BASE + 176 * KB <= nc.SBUF_PARTITION_SIZE_BYTES - 32

    ps = [nc.alloc_psum_tensor(f"ps{i}", [128, 512], F32) for i in range(8)]
    bank_free = [None] * 8

    p = P(nc)
    for s in ("gl", "sl", "tt", "ac", "ve", "st", "car", "cfc", "cpj", "cctx"):
        p.sem(s)
    ev = {}
    rt = {}
    p.E("vector", lambda e: rt.__setitem__("u_ve", e.partition_id() % 2))
    p.E("scalar", lambda e: rt.__setitem__("u_ac", e.partition_id() % 2))
    p.E("gpsimd", lambda e: rt.__setitem__("u_gl", e.partition_id() % 2))

    def bank_wait(i):
        return [bank_free[i]] if bank_free[i] is not None else []

    c = p.E("sync", lambda e: e.dma_start(out=bqk_sb[:, :], in_=dbqk[:, :]), inc=("sl", 16))
    c = p.E("sync", lambda e: e.dma_start(out=identf_sb[:, :], in_=dident[:, :]), inc=("sl", 16))
    ev["consts_sl"] = ("sl", c)
    c = p.E("gpsimd", lambda e: e.dma_start(out=bv3_sb[:, :], in_=dbv3[:, :]), inc=("gl", 16))
    c = p.E("gpsimd", lambda e: e.dma_start(out=outb3_sb[:, :], in_=doutb3[:, :]), inc=("gl", 16))
    c = p.E("gpsimd", lambda e: e.memset(ones3_sb[:, :], 1.0), inc=("gl", 1))
    c = p.E("gpsimd", lambda e: e.memset(ones128_sb[:, :], 1.0), inc=("gl", 1))
    c = p.E("gpsimd", lambda e: e.memset(onesf_sb[:, :], 1.0), inc=("gl", 1))
    c = p.E("gpsimd", lambda e: e.memset(onescol_sb[:, :], 1.0), inc=("gl", 1))
    ev["consts_gl"] = ("gl", c)

    def wload(sb, dr, fw):
        cl = None
        for g in range(4):
            def fn(e, g=g, sb=sb, dr=dr, fw=fw):
                src = dr[g * 512:(g + 1) * 512, :].rearrange("(dc p) f -> p dc f", p=128)
                dst = sb[:, g * 4 * fw:(g + 1) * 4 * fw].rearrange("p (dc f) -> p dc f", f=fw)
                return e.dma_start(out=dst, in_=src)
            cl = p.E("gpsimd", fn, inc=("gl", 16))
        return cl
    ev["wq"] = ("gl", wload(wq_sb, dwq, 1024))
    ev["wk"] = ("gl", wload(wk_sb, dwk, 1024))

    t_pieces = [(0, g, cc) for g in range(2) for cc in range(10)] + \
               [(1, g, cc) for g in range(5) for cc in range(4)]

    def piece_src(m, g, cc):
        return (dwfcT if m == 0 else dwpjT)[g * 128:(g + 1) * 128, cc * 512:(cc + 1) * 512]

    r1_abs = {}
    for i, (m, g, cc) in enumerate(t_pieces):
        buf = i % 2
        w = [("ve", r1_abs[i - 2])] if i >= 2 else []
        cl = p.E("sync", lambda e, m=m, g=g, cc=cc, buf=buf: e.dma_start(out=tin_sb[buf][:, :],
                                                                         in_=piece_src(m, g, cc)),
                 waits=w, inc=("sl", 16))
        r1_abs[i] = p.E("vector",
                        lambda e, i=i, buf=buf: e.reduce_sum(absacc_sb[:, i:i + 1], tin_sb[buf][:, :],
                                                             axis=AX.X, apply_absolute_value=True),
                        waits=[("sl", cl)], inc=("ve", 1))
    c_r1 = p.E("vector", lambda e: e.reduce_sum(absacc_sb[:, 40:41], absacc_sb[:, 0:20], axis=AX.X),
               waits=[("ve", r1_abs[len(t_pieces) - 1])], inc=("ve", 1))
    c_r1 = p.E("vector", lambda e: e.reduce_sum(absacc_sb[:, 41:42], absacc_sb[:, 20:40], axis=AX.X),
               waits=[("ve", c_r1)], inc=("ve", 1))
    ev["r1"] = ("ve", c_r1)

    xt_cnt = {}
    def xt_load(tc, buf, extra_w):
        cl = None
        for g in range(4):
            def fn(e, tc=tc, g=g, buf=buf):
                src = dxT[g * 512:(g + 1) * 512, tc * 512:(tc + 1) * 512].rearrange("(dc p) t -> p dc t", p=128)
                dst = xt_sb[buf][:, g * 4 * 512:(g + 1) * 4 * 512].rearrange("p (dc t) -> p dc t", t=512)
                return e.dma_start(out=dst, in_=src)
            cl = p.E("gpsimd", fn, waits=extra_w if g == 0 else (), inc=("gl", 16))
        return cl

    xt_cnt[0] = xt_load(0, 0, [])
    xt_cnt[1] = xt_load(1, 1, [])
    for tc in range(4):
        buf = tc % 2
        for ft in range(16):
            bk = (tc * 16 + ft) % 2
            wsb = wq_sb if ft < 8 else wk_sb
            waits = [("gl", xt_cnt[tc]), ("gl", ev["wq" if ft < 8 else "wk"][1])] + bank_wait(bk)
            c_mm = None
            for dc in range(16):
                def fn(e, bk=bk, wsb=wsb, ft=ft, dc=dc, buf=buf):
                    return e.matmul(ps[bk][:, :],
                                    wsb[:, dc * 1024 + (ft % 8) * 128: dc * 1024 + (ft % 8) * 128 + 128],
                                    xt_sb[buf][:, dc * 512:(dc + 1) * 512],
                                    start=(dc == 0), stop=(dc == 15))
                c_mm = p.E("tensor", fn, waits=waits if dc == 0 else (),
                           inc=("tt", 1) if dc == 15 else None)
            dst_sb = qT_sb if ft < 8 else kT_sb
            def ev_fn(e, dst_sb=dst_sb, ft=ft, tc=tc, bk=bk):
                return e.activation(dst_sb[:, (ft % 8) * 2048 + tc * 512:(ft % 8) * 2048 + tc * 512 + 512],
                                    ps[bk][:, :], AF.Identity, bias=bqk_sb[:, ft:ft + 1])
            c_ev = p.E("scalar", ev_fn, waits=[("tt", c_mm), ("sl", ev["consts_sl"][1])], inc=("ac", 1))
            bank_free[bk] = ("ac", c_ev)
        if tc + 2 < 4:
            xt_cnt[tc + 2] = xt_load(tc + 2, (tc + 2) % 2, [("tt", p.now("tt"))])
    ev["a1_mms"] = ("tt", p.now("tt"))
    ev["a1_evacs"] = ("ac", p.now("ac"))

    cl = None
    for g in range(4):
        def fn(e, g=g):
            src = dwv[g * 512:(g + 1) * 512, :].rearrange("(dc p) f -> p dc f", p=128)
            dst = wv_sb[:, g * 4 * 1024:(g + 1) * 4 * 1024].rearrange("p (dc f) -> p dc f", f=1024)
            return e.dma_start(out=dst, in_=src)
        cl = p.E("gpsimd", fn, waits=[ev["a1_mms"]] if g == 0 else (), inc=("gl", 16))
    ev["wv"] = ("gl", cl)

    xt_cnt[0] = xt_load(0, 0, [ev["a1_mms"]])
    xt_cnt[1] = xt_load(1, 1, [])
    for tc in range(4):
        buf = tc % 2
        for tl in range(4):
            tt = tc * 4 + tl
            for hc in range(2):
                bk = (tt * 2 + hc) % 2
                waits = [("gl", xt_cnt[tc]), ("gl", ev["wv"][1]), ("gl", ev["consts_gl"][1])] + bank_wait(bk)
                def fb(e, bk=bk, hc=hc):
                    return e.matmul(ps[bk][:, :], ones3_sb[:, :], bv3_sb[:, hc * 512:(hc + 1) * 512],
                                    start=True, stop=False)
                p.E("tensor", fb, waits=waits)
                c_mm = None
                for dc in range(16):
                    def fn(e, bk=bk, tl=tl, dc=dc, hc=hc, buf=buf):
                        return e.matmul(ps[bk][:, :],
                                        xt_sb[buf][:, dc * 512 + tl * 128: dc * 512 + tl * 128 + 128],
                                        wv_sb[:, dc * 1024 + hc * 512: dc * 1024 + hc * 512 + 512],
                                        start=False, stop=(dc == 15))
                    c_mm = p.E("tensor", fn, inc=("tt", 1) if dc == 15 else None)
                def ev_fn(e, tt=tt, hc=hc, bk=bk):
                    return e.tensor_copy(v_sb[:, tt * 1024 + hc * 512: tt * 1024 + hc * 512 + 512],
                                         ps[bk][:, :])
                c_ev = p.E("vector", ev_fn, waits=[("tt", c_mm), ("ve", ev["r1"][1])], inc=("ve", 1))
                bank_free[bk] = ("ve", c_ev)
        if tc + 2 < 4:
            xt_cnt[tc + 2] = xt_load(tc + 2, (tc + 2) % 2, [("tt", p.now("tt"))])
    ev["a2_mms"] = ("tt", p.now("tt"))
    ev["v_evacs"] = ("ve", p.now("ve"))

    c_owp = None
    for g in range(4):
        c_owp = p.E("gpsimd", lambda e, g=g: e.dma_start(out=outw_bf[g * 512:(g + 1) * 512, :],
                                                         in_=doutw[g * 512:(g + 1) * 512, :]),
                    inc=("gl", 16))
    ev["owp"] = ("gl", c_owp)

    c_ams = p.E("tensor",
                lambda e: e.matmul(ps[0][0:1, 0:2], onescol_sb[:, :], absacc_sb[:, 40:42],
                                   start=True, stop=True),
                waits=[ev["r1"], ("gl", ev["consts_gl"][1])] + bank_wait(0), inc=("tt", 1))
    c_amc = p.E("vector", lambda e: e.tensor_copy(amg_sb[0:1, 0:2], ps[0][0:1, 0:2]),
                waits=[("tt", c_ams)], inc=("ve", 1))
    bank_free[0] = ("ve", c_amc)
    c_ami = p.E("gpsimd", lambda e: e.dma_start(out=am_in[:, :], in_=amg_sb[0:1, 0:2]),
                waits=[("ve", c_amc)], inc=("gl", 16))
    c_ar = p.E("gpsimd", lambda e: e.collective_compute("AllReduce", ALU.add, replica_groups=ALL8,
                                                        ins=[am_in[:, :]], outs=[am_out[:, :]]),
               waits=[("gl", c_ami)], inc=("car", 1))
    c_amo = p.E("gpsimd", lambda e: e.dma_start(out=amg_sb[0:1, 2:4], in_=am_out[:, :]),
                waits=[("car", c_ar)], inc=("gl", 16))
    NEL = float(FF * D)
    c_v = p.E("vector", lambda e: e.tensor_scalar(amg_sb[0:1, 4:6], amg_sb[0:1, 2:4], 1.0 / NEL, 1.0,
                                                  ALU.mult, ALU.mult),
              waits=[("gl", c_amo)], inc=("ve", 1))
    c_v = p.E("vector", lambda e: e.tensor_scalar(amg_sb[0:1, 4:6], amg_sb[0:1, 4:6], 1e-5, 1.0,
                                                  ALU.max, ALU.mult), waits=[("ve", c_v)], inc=("ve", 1))
    c_v = p.E("vector", lambda e: e.tensor_scalar(thrrow_sb[0:1, 0:2], amg_sb[0:1, 4:6], -0.7, 1.0,
                                                  ALU.mult, ALU.mult), waits=[("ve", c_v)], inc=("ve", 1))
    c_v = p.E("vector", lambda e: e.tensor_scalar(thrrow_sb[0:1, 2:4], amg_sb[0:1, 4:6], 0.7, 1.0,
                                                  ALU.mult, ALU.mult), waits=[("ve", c_v)], inc=("ve", 1))
    c_v = p.E("vector", lambda e: e.tensor_scalar(thrrow_sb[0:1, 4:6], amg_sb[0:1, 4:6], 0.5, 1.0,
                                                  ALU.mult, ALU.mult), waits=[("ve", c_v)], inc=("ve", 1))
    c_bc = p.E("tensor", lambda e: e.matmul(ps[0][:, 0:6], onesf_sb[:, :], thrrow_sb[0:1, 0:6],
                                            start=True, stop=True),
               waits=[("ve", c_v)] + bank_wait(0), inc=("tt", 1))
    c_thr = p.E("vector", lambda e: e.tensor_copy(thr_sb[:, 0:6], ps[0][:, 0:6]),
                waits=[("tt", c_bc)], inc=("ve", 1))
    bank_free[0] = ("ve", c_thr)
    ev["thr"] = ("ve", c_thr)

    c = p.E("sync", lambda e: e.dma_start(out=masks_sb[:, :], in_=dmasks[:, :]),
            waits=[ev["a2_mms"]], inc=("sl", 16))
    c = p.E("sync", lambda e: e.dma_start(out=alibi_sb[:, :], in_=dalibi[:, :]), inc=("sl", 16))
    ev["attn_sl"] = ("sl", c)
    c = p.E("gpsimd", lambda e: e.dma_start(out=negsi_sb[:, :], in_=dnegsi[:, :]),
            waits=[ev["a2_mms"]], inc=("gl", 16))
    ev["attn_gl"] = ("gl", c)

    r2_state = {"prev_comb": None, "prev_w": None, "buf": 0}

    def emit_r2_piece(i):
        m, g, cc = t_pieces[i]
        buf = i % 2
        w = [("ve", r2_state["prev_comb"])] if r2_state["prev_comb"] else []
        if i >= 2:
            w.append(("ac", r2_state[f"s2_{i-2}"]))
        cl = p.E("sync", lambda e, m=m, g=g, cc=cc, buf=buf: e.dma_start(out=tin_sb[buf][:, :],
                                                                        in_=piece_src(m, g, cc)),
                 waits=w, inc=("sl", 16))
        c_s1 = p.E("scalar", lambda e, m=m, buf=buf: e.activation(ts1_sb[:, :], tin_sb[buf][:, :], AF.Sign,
                                                                  bias=thr_sb[:, m:m + 1]),
                   waits=[("sl", cl), ev["thr"]] +
                         ([("ve", r2_state["prev_comb"])] if r2_state["prev_comb"] else []),
                   inc=("ac", 1))
        c_s2 = p.E("scalar", lambda e, m=m, buf=buf: e.activation(tout_sb[:, :], tin_sb[buf][:, :], AF.Sign,
                                                                  bias=thr_sb[:, 2 + m:3 + m]),
                   waits=[("sl", r2_state["prev_w"])] if r2_state["prev_w"] else (),
                   inc=("ac", 1))
        r2_state[f"s2_{i}"] = c_s2
        c_c1 = p.E("vector", lambda e: e.tensor_add(ts1_sb[:, :], ts1_sb[:, :], tout_sb[:, :]),
                   waits=[("ac", c_s2), ("sl", r2_state["prev_w"])] if r2_state["prev_w"] else [("ac", c_s2)],
                   inc=("ve", 1))
        c_c2 = p.E("vector", lambda e, m=m: e.tensor_scalar(tout_sb[:, :], ts1_sb[:, :],
                                                            thr_sb[:, 4 + m:5 + m], 1.0,
                                                            ALU.mult, ALU.mult),
                   waits=[("ve", c_c1)], inc=("ve", 1))
        r2_state["prev_comb"] = c_c2
        if m == 0:
            cw = p.E("sync", lambda e, g=g, cc=cc: e.dma_start(
                out=fc_agi[g * 128:(g + 1) * 128, cc * 512:(cc + 1) * 512], in_=tout_sb[:, :]),
                waits=[("ve", c_c2)], inc=("sl", 16))
        else:
            for k in range(4):
                cw = p.E("sync", lambda e, g=g, cc=cc, k=k: e.dma_start(
                    out=pj_agi[(cc * 4 + k) * 640 + g * 128: (cc * 4 + k) * 640 + g * 128 + 128, :],
                    in_=tout_sb[:, k * 128:(k + 1) * 128]),
                    waits=[("ve", c_c2)] if k == 0 else (), inc=("sl", 16))
        r2_state["prev_w"] = cw

    band_pairs = [(v, it) for v in range(4) for it in range(v, 4)]
    ctx_done = {}
    lmm_hist = {}
    pT_guard = {0: None, 1: None}
    first_evac = [True]
    r2_next = [0]
    for h in range(8):
        for cI in range(4):
            tiles = [("off", jt) for jt in range(4 * cI)] + [("band", v, it) for (v, it) in band_pairs]
            n_t = len(tiles)
            ctx_bk = 2 + (h * 4 + cI) % 2
            l_bk = 4 + (h * 4 + cI) % 2
            ctx_w0 = bank_wait(ctx_bk)
            l_w0 = bank_wait(l_bk)
            c_lmm = None
            pend = None

            def emit_ctx_l(pend, last):
                nonlocal c_lmm
                ti, jt, width, ioff, c_exp = pend
                def fctx(e, ctx_bk=ctx_bk, h=h, jt=jt, pbuf=ti % 2, width=width, ti=ti, ioff=ioff):
                    return e.matmul(ps[ctx_bk][:, ioff:ioff + width],
                                    v_sb[:, jt * 1024 + h * 128: jt * 1024 + h * 128 + 128],
                                    pT_sb[pbuf][:, 0:width], start=(ti == 0), stop=False)
                wctx = [("ac", c_exp)]
                if ti == 0:
                    wctx += [("ve", ev["v_evacs"][1])] + ctx_w0 + l_w0
                p.E("tensor", fctx, waits=wctx)
                def fl(e, l_bk=l_bk, pbuf=ti % 2, width=width, ti=ti, ioff=ioff, last=last):
                    return e.matmul(ps[l_bk][0:1, ioff:ioff + width], ones128_sb[:, :],
                                    pT_sb[pbuf][:, 0:width], start=(ti == 0), stop=last)
                c_lmm = p.E("tensor", fl, inc=("tt", 1))
                lmm_hist[(h, cI, ti)] = c_lmm
                pT_guard[ti % 2] = ("tt", c_lmm)

            for ti, tile in enumerate(tiles):
                sbk = ti % 2
                pbuf = ti % 2
                if tile[0] == "off":
                    jt = tile[1]
                    width, qoff, var, diag = 512, cI * 512, tile[1] - 4 * cI + 16, False
                else:
                    v_, it_ = tile[1], tile[2]
                    jt = 4 * cI + v_
                    width, qoff, var, diag = 128, cI * 512 + it_ * 128, v_ - it_ + 16, (v_ == it_)
                waits = list(bank_wait(sbk))
                if ti == 0:
                    waits += [("ac", ev["a1_evacs"][1]), ("gl", ev["attn_gl"][1])]
                def mm1(e, sbk=sbk, h=h, jt=jt, qoff=qoff, width=width):
                    return e.matmul(ps[sbk][:, 0:width],
                                    kT_sb[:, h * 2048 + jt * 128: h * 2048 + jt * 128 + 128],
                                    qT_sb[:, h * 2048 + qoff: h * 2048 + qoff + width],
                                    start=True, stop=False)
                p.E("tensor", mm1, waits=waits)
                def mm2(e, sbk=sbk, h=h, width=width):
                    return e.matmul(ps[sbk][:, 0:width], ones3_sb[:, :],
                                    negsi_sb[:, h * 512: h * 512 + width], start=False, stop=True)
                c_smm = p.E("tensor", mm2, inc=("tt", 1))
                if pend is not None:
                    emit_ctx_l(pend, last=False)
                if diag:
                    c_msk = p.E("vector",
                                lambda e, sbk=sbk: e.tensor_tensor(ps[sbk][:, 0:128], ps[sbk][:, 0:128],
                                                                   masks_sb[:, 0:128], ALU.add),
                                waits=[("tt", c_smm), ("sl", ev["attn_sl"][1])], inc=("ve", 1))
                    exp_w = [("ve", c_msk)]
                else:
                    exp_w = [("tt", c_smm), ("sl", ev["attn_sl"][1])]
                if pT_guard[pbuf] is not None:
                    exp_w.append(pT_guard[pbuf])
                def fexp(e, sbk=sbk, pbuf=pbuf, width=width, h=h, var=var):
                    return e.activation(pT_sb[pbuf][:, 0:width], ps[sbk][:, 0:width], AF.Exp,
                                        bias=alibi_sb[:, h * 20 + var: h * 20 + var + 1])
                c_exp = p.E("scalar", fexp, waits=exp_w, inc=("ac", 1))
                bank_free[sbk] = ("ac", c_exp)
                pend = (ti, jt, width, qoff - cI * 512, c_exp)
            emit_ctx_l(pend, last=True)
            c_rl = p.E("vector", lambda e, l_bk=l_bk: e.reciprocal(rl_sb[0:1, :], ps[l_bk][0:1, :]),
                       waits=[("tt", c_lmm)], inc=("ve", 1))
            bank_free[l_bk] = ("ve", c_rl)
            c_bcm = p.E("tensor", lambda e: e.matmul(ps[6][:, :], onesf_sb[:, :], rl_sb[0:1, :],
                                                     start=True, stop=True),
                        waits=[("ve", c_rl)] + bank_wait(6), inc=("tt", 1))
            c_bcc = p.E("vector", lambda e: e.tensor_copy(bcst_sb[:, :], ps[6][:, :]),
                        waits=[("tt", c_bcm)], inc=("ve", 1))
            bank_free[6] = ("ve", c_bcc)
            evac_w = [("ve", c_bcc)]
            if first_evac[0]:
                evac_w.append(ev["a2_mms"])
                first_evac[0] = False
            def fevac(e, h=h, cI=cI, ctx_bk=ctx_bk):
                return e.tensor_tensor(ctx_sb[:, h * 2048 + cI * 512: h * 2048 + cI * 512 + 512],
                                       ps[ctx_bk][:, :], bcst_sb[:, :], ALU.mult)
            c_ev = p.E("vector", fevac, waits=evac_w, inc=("ve", 1))
            bank_free[ctx_bk] = ("ve", c_ev)
            ctx_done[(h, cI)] = c_ev
            for _ in range(2):
                if r2_next[0] < len(t_pieces):
                    emit_r2_piece(r2_next[0])
                    r2_next[0] += 1
    while r2_next[0] < len(t_pieces):
        emit_r2_piece(r2_next[0])
        r2_next[0] += 1
    ev["attn_mms"] = ("tt", p.now("tt"))
    ev["ctx_all"] = ("ve", max(ctx_done.values()))
    ev["tern_w"] = ("sl", p.now("sl"))

    c_agfc = p.E("gpsimd", lambda e: e.collective_compute("AllGather", ALU.bypass, replica_groups=ALL8,
                                                          ins=[fc_agi[:, :]], outs=[fc_ago[:, :]]),
                 waits=[ev["tern_w"]], inc=("cfc", 1))
    c_agpj = p.E("gpsimd", lambda e: e.collective_compute("AllGather", ALU.bypass, replica_groups=ALL8,
                                                          ins=[pj_agi[:, :]], outs=[pj_ago[:, :]]),
                 inc=("cpj", 1))

    c_xo = None
    for g in range(2):
        def fxo(e, g=g):
            src = dx_own[g * 512:(g + 1) * 512, :].rearrange("(tt p) d -> p tt d", p=128)
            dst = x_sb[:, g * 4 * 2048:(g + 1) * 4 * 2048].rearrange("p (tt d) -> p tt d", d=2048)
            return e.dma_start(out=dst, in_=src)
        c_xo = p.E("sync", fxo, waits=[ev["attn_mms"]] if g == 0 else (), inc=("sl", 16))
    ev["x_own"] = ("sl", c_xo)

    c_stp = None
    for h in range(8):
        def fcp(e, h=h):
            return e.tensor_copy(ctx2_sb[:, h * 1024:(h + 1) * 1024],
                                 ctx_sb[:, ds(h * 2048 + (1 - rt["u_ve"]) * 1024, 1024)])
        c_stp = p.E("vector", fcp, waits=[ev["ctx_all"], ev["a2_mms"]] if h == 0 else (), inc=("ve", 1))
    c_stg = None
    for h in range(8):
        def fstg(e, h=h):
            return e.dma_start(out=ctx_agi[h * 128:(h + 1) * 128, :],
                               in_=ctx2_sb[:, h * 1024:(h + 1) * 1024])
        c_stg = p.E("gpsimd", fstg, waits=[("ve", c_stp)] if h == 0 else (), inc=("gl", 16))
    c_cc = None
    for h in range(8):
        def fcc(e, h=h):
            return e.copy(ctxC_sb[:, h * 1024:(h + 1) * 1024],
                          ctx_sb[:, ds(h * 2048 + rt["u_ac"] * 1024, 1024)])
        c_cc = p.E("scalar", fcc, waits=[ev["ctx_all"], ev["a2_mms"]] if h == 0 else (), inc=("ac", 1))
    ev["ctxC"] = ("ac", c_cc)
    c_agc = p.E("gpsimd", lambda e: e.collective_compute("AllGather", ALU.bypass, replica_groups=PAIRS,
                                                         ins=[ctx_agi[:, :]], outs=[ctx_ago[:, :]]),
                waits=[("gl", c_stg)], inc=("cctx", 1))
    def fc2(e):
        prow = (1 - rt["u_gl"]) * 1024
        src = ctx_ago[ds(prow, 1024), :].rearrange("(h p) t -> p h t", p=128)
        dst = ctx2_sb[:, :].rearrange("p (h t) -> p h t", t=1024)
        return e.dma_start(out=dst, in_=src)
    c_ctx2 = p.E("gpsimd", fc2, waits=[("cctx", c_agc), ev["attn_mms"]], inc=("gl", 16))
    ev["ctx2"] = ("gl", c_ctx2)

    buf_users = {0: 0, 1: 0}
    for tp in range(4):
        c_mm = None
        for fc in range(16):
            buf = fc % 2
            w = [ev["owp"], ("tt", buf_users[buf])]
            c_ow = p.E("sync", lambda e, fc=fc, buf=buf: e.dma_start(out=ow_sb[buf][:, :],
                                                                    in_=outw_bf[fc * 128:(fc + 1) * 128, :]),
                       waits=w, inc=("sl", 16))
            for tloc in range(2):
                tt = tp * 2 + tloc
                for n4 in range(4):
                    bk = tloc * 4 + n4
                    if fc == 0:
                        w0 = list(bank_wait(bk)) + [("sl", c_ow), ev["ctx2"], ev["ctxC"],
                                                    ("gl", ev["consts_gl"][1])]
                        def fb(e, bk=bk, n4=n4):
                            return e.matmul(ps[bk][:, :], ones3_sb[:, :],
                                            outb3_sb[:, n4 * 512:(n4 + 1) * 512], start=True, stop=False)
                        p.E("tensor", fb, waits=w0)
                    def fmm(e, bk=bk, fc=fc, tt=tt, n4=n4, buf=buf):
                        if fc < 8:
                            lhsT = ctxC_sb[:, fc * 1024 + tt * 128: fc * 1024 + tt * 128 + 128]
                        else:
                            lhsT = ctx2_sb[:, (fc - 8) * 1024 + tt * 128: (fc - 8) * 1024 + tt * 128 + 128]
                        return e.matmul(ps[bk][:, :], lhsT, ow_sb[buf][:, n4 * 512:(n4 + 1) * 512],
                                        start=False, stop=(fc == 15))
                    wmm = [("sl", c_ow)] if (tloc == 0 and n4 == 0) else []
                    c_mm = p.E("tensor", fmm, waits=wmm,
                               inc=("tt", 1) if (fc == 15 or (tloc == 1 and n4 == 3)) else None)
            buf_users[buf] = p.now("tt")
        for tloc in range(2):
            tt = tp * 2 + tloc
            for n4 in range(4):
                bk = tloc * 4 + n4
                def fev(e, bk=bk, tt=tt, n4=n4):
                    sl = x_sb[:, tt * 2048 + n4 * 512: tt * 2048 + n4 * 512 + 512]
                    return e.tensor_tensor(sl, sl, ps[bk][:, :], ALU.add)
                c_ad = p.E("vector", fev, waits=[("tt", c_mm), ev["x_own"]], inc=("ve", 1))
                bank_free[bk] = ("ve", c_ad)
    ev["outproj"] = ("ve", p.now("ve"))

    def emit_ln(tt, col_sb, cb, ve_ready, extra_sq_w=()):
        sl = lambda: x_sb[:, tt * 2048:(tt + 1) * 2048]
        c1 = p.E("vector", lambda e: e.reduce_sum(col_sb[:, cb:cb + 1], sl(), axis=AX.X),
                 waits=[("ve", ve_ready)], inc=("ve", 1))
        c2 = None
        for j in range(4):
            def fsq(e, j=j):
                sj = x_sb[:, tt * 2048 + j * 512: tt * 2048 + j * 512 + 512]
                return e.scalar_tensor_tensor(lnsq_sb[:, :], sj, 1.0, sj, ALU.mult, ALU.mult,
                                              accum_out=col_sb[:, cb + 1 + j:cb + 2 + j])
            c2 = p.E("vector", fsq, waits=([("ve", ve_ready)] + list(extra_sq_w)) if j == 0 else (),
                     inc=("ve", 1))
        c3 = p.E("vector", lambda e: e.reduce_sum(col_sb[:, cb + 5:cb + 6], col_sb[:, cb + 1:cb + 5], axis=AX.X),
                 waits=[("ve", c2)], inc=("ve", 1))
        c3 = p.E("vector", lambda e: e.tensor_scalar(col_sb[:, cb:cb + 1], col_sb[:, cb:cb + 1],
                                                     1.0 / D, 1.0, ALU.mult, ALU.mult),
                 waits=[("ve", c1)], inc=("ve", 1))
        c3 = p.E("vector", lambda e: e.tensor_tensor(col_sb[:, cb + 6:cb + 7], col_sb[:, cb:cb + 1],
                                                     col_sb[:, cb:cb + 1], ALU.mult),
                 waits=[("ve", c3)], inc=("ve", 1))
        c3 = p.E("vector", lambda e: e.tensor_scalar(col_sb[:, cb + 5:cb + 6], col_sb[:, cb + 5:cb + 6],
                                                     1.0 / D, 1.0, ALU.mult, ALU.mult),
                 waits=[("ve", c3)], inc=("ve", 1))
        c3 = p.E("vector", lambda e: e.tensor_tensor(col_sb[:, cb + 5:cb + 6], col_sb[:, cb + 5:cb + 6],
                                                     col_sb[:, cb + 6:cb + 7], ALU.subtract),
                 waits=[("ve", c3)], inc=("ve", 1))
        c3 = p.E("vector", lambda e: e.tensor_scalar(col_sb[:, cb + 5:cb + 6], col_sb[:, cb + 5:cb + 6],
                                                     LN_EPS, 1.0, ALU.add, ALU.mult),
                 waits=[("ve", c3)], inc=("ve", 1))
        c4 = p.E("vector", lambda e: e.tensor_scalar(col_sb[:, cb + 7:cb + 8], col_sb[:, cb + 5:cb + 6],
                                                     0.30, 0.55, ALU.mult, ALU.add),
                 waits=[("ve", c3)], inc=("ve", 1))
        for _it in range(4):
            c4 = p.E("vector", lambda e: e.reciprocal(col_sb[:, cb + 6:cb + 7], col_sb[:, cb + 7:cb + 8]),
                     waits=[("ve", c4)], inc=("ve", 1))
            c4 = p.E("vector", lambda e: e.tensor_scalar(col_sb[:, cb + 6:cb + 7], col_sb[:, cb + 6:cb + 7],
                                                         col_sb[:, cb + 5:cb + 6], 0.5, ALU.mult, ALU.mult),
                     waits=[("ve", c4)], inc=("ve", 1))
            c4 = p.E("vector", lambda e: e.scalar_tensor_tensor(col_sb[:, cb + 7:cb + 8],
                                                                col_sb[:, cb + 7:cb + 8], 0.5,
                                                                col_sb[:, cb + 6:cb + 7], ALU.mult, ALU.add),
                     waits=[("ve", c4)], inc=("ve", 1))
        c4 = p.E("vector", lambda e: e.reciprocal(col_sb[:, cb + 7:cb + 8], col_sb[:, cb + 7:cb + 8]),
                 waits=[("ve", c4)], inc=("ve", 1))
        c5 = p.E("vector", lambda e: e.tensor_scalar(sl(), sl(), col_sb[:, cb:cb + 1],
                                                     col_sb[:, cb + 7:cb + 8], ALU.subtract, ALU.mult),
                 waits=[("ve", c4)], inc=("ve", 1))
        return c5

    ln1_done = {}
    for tt in range(8):
        ln1_done[tt] = emit_ln(tt, lncol_sb, tt * 8, ev["outproj"][1])

    fcs_free = {}
    pjs_free = {}
    st_cnt = None
    for tc2 in range(2):
        xT_done = None
        for tl in range(4):
            tt = tc2 * 4 + tl
            for dg in range(4):
                w = list(bank_wait(7)) + [("ve", ln1_done[tt]), ("sl", ev["consts_sl"][1])]
                if tc2 == 1 and tl == 0 and dg == 0:
                    w.append(fcs_free[9])
                c_tr = None
                for k in range(4):
                    dc = dg * 4 + k
                    def ftr(e, tt=tt, dc=dc, k=k):
                        return e.transpose(ps[7][:, k * 128:(k + 1) * 128],
                                           x_sb[:, tt * 2048 + dc * 128: tt * 2048 + dc * 128 + 128],
                                           identf_sb[:, :])
                    c_tr = p.E("tensor", ftr, waits=w if k == 0 else (), inc=("tt", 1) if k == 3 else None)
                def fxe(e, tl=tl, dg=dg):
                    dst = xlnT_sb[:, :].rearrange("p (dc t) -> p dc t", t=512)[:, dg * 4:(dg + 1) * 4,
                                                                              tl * 128:(tl + 1) * 128]
                    src = ps[7][:, :].rearrange("p (k t) -> p k t", t=128)
                    return e.tensor_copy(dst, src)
                xT_done = p.E("vector", fxe, waits=[("tt", c_tr)], inc=("ve", 1))
                bank_free[7] = ("ve", xT_done)
        ev[f"xlnT{tc2}"] = ("ve", xT_done)

        first_sq = True
        for sI in range(10):
            buf = sI % 2
            gi = tc2 * 10 + sI
            w = [("cfc", c_agfc)]
            if gi == 0:
                w.append(ev["outproj"])
            if tc2 == 1 and sI == 0:
                w.append(pjs_free[15])
            if gi >= 2:
                w.append(fcs_free[gi - 2])
            c_sl = None
            for g in range(4):
                def fsl(e, sI=sI, g=g, buf=buf):
                    src = fc_ago[g * 512:(g + 1) * 512, sI * 512:(sI + 1) * 512].rearrange(
                        "(dc p) f -> p dc f", p=128)
                    dst = fcs_sb[buf][:, g * 4 * 512:(g + 1) * 4 * 512].rearrange("p (dc f) -> p dc f", f=512)
                    return e.dma_start(out=dst, in_=src)
                c_sl = p.E("sync", fsl, waits=w if g == 0 else (), inc=("sl", 16))
            for hft in range(4):
                hf = sI * 4 + hft
                bk = hf % 2
                w2 = list(bank_wait(bk)) + [("sl", c_sl), ev[f"xlnT{tc2}"]]
                c_mm = None
                for dc in range(16):
                    def fmm(e, bk=bk, dc=dc, hft=hft, buf=buf):
                        return e.matmul(ps[bk][:, :],
                                        fcs_sb[buf][:, dc * 512 + hft * 128: dc * 512 + hft * 128 + 128],
                                        xlnT_sb[:, dc * 512:(dc + 1) * 512],
                                        start=(dc == 0), stop=(dc == 15))
                    c_mm = p.E("tensor", fmm, waits=w2 if dc == 0 else (), inc=("tt", 1) if dc == 15 else None)
                c_r = p.E("vector", lambda e, bk=bk: e.tensor_scalar(relu_sb[:, :], ps[bk][:, :], 0.0, 1.0,
                                                                    ALU.max, ALU.mult),
                          waits=[("tt", c_mm)], inc=("ve", 1))
                sq_w = [("ve", c_r)]
                if first_sq and tc2 == 1:
                    sq_w.append(pjs_free[15])
                first_sq = False
                c_q = p.E("vector", lambda e, hf=hf: e.tensor_mul(h_sb[:, hf * 512:(hf + 1) * 512],
                                                                 relu_sb[:, :], relu_sb[:, :]),
                          waits=sq_w, inc=("ve", 1))
                bank_free[bk] = ("ve", c_r)
            fcs_free[gi] = ("tt", p.now("tt"))
        ev[f"h{tc2}"] = ("ve", p.now("ve"))

        for d2t in range(16):
            buf = d2t % 2
            gi = tc2 * 16 + d2t
            w = [("cpj", c_agpj), fcs_free[tc2 * 10 + 9]]
            if gi >= 2:
                w.append(pjs_free[gi - 2])
            c_pl = None
            for c8 in range(8):
                def fpl(e, d2t=d2t, c8=c8, buf=buf):
                    src = pj_ago[c8 * 10240 + d2t * 640: c8 * 10240 + d2t * 640 + 640, :].rearrange(
                        "(hc p) f -> p hc f", p=128)
                    dst = pjs_sb[buf][:, c8 * 5 * 128:(c8 * 5 + 5) * 128].rearrange("p (hc f) -> p hc f", f=128)
                    return e.dma_start(out=dst, in_=src)
                c_pl = p.E("sync", fpl, waits=w if c8 == 0 else (), inc=("sl", 16))
            bk = 2 + d2t % 2
            w2 = list(bank_wait(bk)) + [("sl", c_pl), ev[f"h{tc2}"]]
            c_mm = None
            for hc in range(40):
                def fmm(e, bk=bk, hc=hc, buf=buf):
                    return e.matmul(ps[bk][:, :], pjs_sb[buf][:, hc * 128:(hc + 1) * 128],
                                    h_sb[:, hc * 512:(hc + 1) * 512],
                                    start=(hc == 0), stop=(hc == 39))
                c_mm = p.E("tensor", fmm, waits=w2 if hc == 0 else (), inc=("tt", 1) if hc == 39 else None)
            pjs_free[gi] = ("tt", p.now("tt"))
            c_o2 = p.E("scalar", lambda e, bk=bk: e.activation(o2_sb[:, :], ps[bk][:, :], AF.Copy),
                       waits=[("tt", c_mm)], inc=("ac", 1))
            bank_free[bk] = ("ac", c_o2)
            w3 = list(bank_wait(6)) + [("ac", c_o2)]
            c_tr = None
            for k in range(4):
                def ftr(e, k=k):
                    return e.transpose(ps[6][:, k * 128:(k + 1) * 128], o2_sb[:, k * 128:(k + 1) * 128],
                                       identf_sb[:, :])
                c_tr = p.E("tensor", ftr, waits=w3 if k == 0 else (), inc=("tt", 1) if k == 3 else None)
            c_ad = None
            for k in range(4):
                tt = tc2 * 4 + k
                def fadd(e, tt=tt, k=k, d2t=d2t):
                    sl = x_sb[:, tt * 2048 + d2t * 128: tt * 2048 + d2t * 128 + 128]
                    return e.tensor_tensor(sl, sl, ps[6][:, k * 128:(k + 1) * 128], ALU.add)
                c_ad = p.E("vector", fadd, waits=[("tt", c_tr)] if k == 0 else (), inc=("ve", 1))
            bank_free[6] = ("ve", c_ad)
        for tl in range(4):
            tt = tc2 * 4 + tl
            c_ln = emit_ln(tt, ln2col_sb, tt * 8, p.now("ve"))
            def fst(e, tt=tt):
                return e.dma_start(out=dout[tt * 128:(tt + 1) * 128, :],
                                   in_=x_sb[:, tt * 2048:(tt + 1) * 2048])
            st_cnt = p.E("sync", fst, waits=[("ve", c_ln)], inc=("st", 16))

    p.E("sync", lambda e: e.nop(), waits=[("st", p.now("st"))])

    with nc.Block() as block:
        block.tensor(lambda e: [f(e) for f in p.ops["tensor"]])
        block.scalar(lambda e: [f(e) for f in p.ops["scalar"]])
        block.vector(lambda e: [f(e) for f in p.ops["vector"]])
        block.gpsimd(lambda e: [f(e) for f in p.ops["gpsimd"]])
        block.sync(lambda e: [f(e) for f in p.ops["sync"]])

    return nc


def _stage_inputs(L0_attn_x, L0_attn_qkv_w, L0_attn_qkv_b, L0_attn_out_w, L0_attn_out_b,
                  L0_ffn_w_fc, L0_ffn_w_proj):
    x = np.ascontiguousarray(np.asarray(L0_attn_x, np.float32))
    qkv_w = np.asarray(L0_attn_qkv_w, np.float32)
    qkv_b = np.asarray(L0_attn_qkv_b, np.float32)
    out_w = np.asarray(L0_attn_out_w, np.float32)
    out_b = np.asarray(L0_attn_out_b, np.float32)
    wfcT = np.ascontiguousarray(np.asarray(L0_ffn_w_fc, np.float32).T)
    wpjT = np.ascontiguousarray(np.asarray(L0_ffn_w_proj, np.float32).T)

    slopes = _slopes()
    masks = np.zeros((128, 4, 512), np.float32)
    pp = np.arange(128)[:, None]
    qq = np.arange(512)[None, :]
    for v in range(4):
        masks[:, v, :] = np.where(128 * v + pp <= qq, 0.0, MASKVAL)
    masks = np.ascontiguousarray(masks.reshape(128, 4 * 512))
    ident = np.eye(128, dtype=np.float32)

    in_maps = []
    for c in range(NCORES):
        b, h2 = c // 2, c % 2
        sl = slopes[h2 * 8:(h2 + 1) * 8]
        rel = np.arange(-16, 4)
        alibi = (sl[None, :, None] * (128.0 * rel[None, None, :] + np.arange(128)[:, None, None])
                 - C_STAB).astype(np.float32)
        nq = (-sl[:, None] * np.arange(512)[None, :]).reshape(8 * 512)
        in_maps.append({
            "xT": np.ascontiguousarray(x[b].T),
            "x_own": np.ascontiguousarray(x[b, h2 * 1024:(h2 + 1) * 1024]),
            "wq": np.ascontiguousarray(qkv_w[:, h2 * 1024:(h2 + 1) * 1024]),
            "wk": np.ascontiguousarray(qkv_w[:, 2048 + h2 * 1024: 2048 + (h2 + 1) * 1024]),
            "wv": np.ascontiguousarray(qkv_w[:, 4096 + h2 * 1024: 4096 + (h2 + 1) * 1024]),
            "bqk": np.ascontiguousarray(np.concatenate([
                qkv_b[h2 * 1024:(h2 + 1) * 1024].reshape(8, 128).T,
                qkv_b[2048 + h2 * 1024: 2048 + (h2 + 1) * 1024].reshape(8, 128).T], axis=1)),
            "bv3": np.ascontiguousarray(
                _split3(qkv_b[4096 + h2 * 1024: 4096 + (h2 + 1) * 1024][None, :]).reshape(3, 1024)),
            "outw": np.ascontiguousarray(np.concatenate([
                out_w[h2 * 1024:(h2 + 1) * 1024], out_w[(1 - h2) * 1024:(2 - h2) * 1024]], axis=0)),
            "outb3": np.ascontiguousarray(_split3(out_b[None, :]).reshape(3, 2048)),
            "wfcT_sl": np.ascontiguousarray(wfcT[c * 256:(c + 1) * 256]),
            "wprojT_sl": np.ascontiguousarray(wpjT[c * 640:(c + 1) * 640]),
            "masks": masks,
            "alibi2": np.ascontiguousarray(alibi.reshape(128, 160)),
            "negsi3q": np.ascontiguousarray(_split3(nq[None, :]).reshape(3, 8 * 512)),
            "identf": ident,
        })
    return in_maps


_cached = {}


def kernel(L0_attn_x, L0_attn_qkv_w, L0_attn_qkv_b, L0_attn_out_w, L0_attn_out_b,
           L0_ffn_w_fc, L0_ffn_w_proj):
    in_maps = _stage_inputs(L0_attn_x, L0_attn_qkv_w, L0_attn_qkv_b, L0_attn_out_w,
                            L0_attn_out_b, L0_ffn_w_fc, L0_ffn_w_proj)
    if "nc" not in _cached:
        _cached["nc"] = build_program()
    nc = _cached["nc"]
    res = run_bass_kernel_spmd(nc, in_maps, core_ids=list(range(NCORES)))
    out = np.empty((B, S, D), np.float32)
    for c in range(NCORES):
        b, h2 = c // 2, c % 2
        out[b, h2 * 1024:(h2 + 1) * 1024] = res.results[c]["out"]
    return out


# version 04
# speedup vs baseline: 1.0777x; 1.0777x over previous
import numpy as np
import ml_dtypes
import concourse.bass as bass
import concourse.mybir as mybir
from concourse.bass_utils import run_bass_kernel_spmd

F32 = mybir.dt.float32
BF16 = mybir.dt.bfloat16
AF = mybir.ActivationFunctionType
ALU = mybir.AluOpType
AX = mybir.AxisListType
ds = bass.ds
BFNP = ml_dtypes.bfloat16

B, S, D = 4, 2048, 2048
NH, HD = 16, 128
FF = 5120
NCORES = 8
C_STAB = 15.0
MASKVAL = -30000.0
LN_EPS = 1e-5
KB = 1024

PAIRS = [[0, 1], [2, 3], [4, 5], [6, 7]]
ALL8 = [list(range(8))]


def _split3(x):
    x = np.asarray(x, np.float32)
    p1 = x.astype(BFNP).astype(np.float32)
    r = x - p1
    p2 = r.astype(BFNP).astype(np.float32)
    p3 = (r - p2).astype(BFNP).astype(np.float32)
    return np.stack([p1, p2, p3]).astype(np.float32)


def _slopes():
    return np.asarray([1.0 / 2.0 ** (i / 2.0) for i in range(NH)], dtype=np.float64)


class P:
    ENGS = ("tensor", "scalar", "vector", "gpsimd", "sync")

    def __init__(self, nc):
        self.nc = nc
        self.ops = {e: [] for e in self.ENGS}
        self.cnt = {}
        self.sems = {}
        self.waited = {e: {} for e in self.ENGS}

    def sem(self, name):
        self.sems[name] = self.nc.alloc_semaphore(name=name)
        self.cnt[name] = 0

    def now(self, name):
        return self.cnt[name]

    def E(self, eng, fn, waits=(), inc=None):
        w = []
        for s, v in waits:
            if v > self.waited[eng].get(s, 0):
                self.waited[eng][s] = v
                w.append((s, v))
        tgt = None
        if inc is not None:
            self.cnt[inc[0]] += inc[1]
            tgt = self.cnt[inc[0]]

        def runner(e, fn=fn, w=tuple(w), inc=inc):
            for s, v in w:
                e.wait_ge(self.sems[s], v)
            ins = fn(e)
            if inc is not None and ins is not None:
                ins.then_inc(self.sems[inc[0]], inc[1])

        self.ops[eng].append(runner)
        return tgt


def build_program():
    nc = bass.Bass("TRN2", target_bir_lowering=False)

    dxT = nc.dram_tensor("xT", [D, S], F32, kind="ExternalInput")
    dx_own = nc.dram_tensor("x_own", [1024, D], F32, kind="ExternalInput")
    dwq = nc.dram_tensor("wq", [D, 1024], F32, kind="ExternalInput")
    dwk = nc.dram_tensor("wk", [D, 1024], F32, kind="ExternalInput")
    dwv = nc.dram_tensor("wv", [D, 1024], F32, kind="ExternalInput")
    dbqk = nc.dram_tensor("bqk", [128, 16], F32, kind="ExternalInput")
    dbv3 = nc.dram_tensor("bv3", [3, 1024], F32, kind="ExternalInput")
    doutw = nc.dram_tensor("outw", [D, D], F32, kind="ExternalInput")
    doutb3 = nc.dram_tensor("outb3", [3, D], F32, kind="ExternalInput")
    dwfcT = nc.dram_tensor("wfcT_sl", [256, FF], F32, kind="ExternalInput")
    dwpjT = nc.dram_tensor("wprojT_sl", [640, D], F32, kind="ExternalInput")
    dmasks = nc.dram_tensor("masks", [128, 4 * 512], F32, kind="ExternalInput")
    dalibi = nc.dram_tensor("alibi2", [128, 8 * 20], F32, kind="ExternalInput")
    dnegsi = nc.dram_tensor("negsi3q", [3, 8 * 512], F32, kind="ExternalInput")
    dident = nc.dram_tensor("identf", [128, 128], F32, kind="ExternalInput")
    dout = nc.dram_tensor("out", [1024, D], F32, kind="ExternalOutput")

    outw_bf = nc.dram_tensor("outw_bf", [D, D], BF16)
    am_in = nc.dram_tensor("am_in", [1, 2], F32)
    am_out = nc.dram_tensor("am_out", [1, 2], F32, addr_space="Shared")
    fc_agi = nc.dram_tensor("fc_agi", [256, FF], BF16)
    fc_ago = nc.dram_tensor("fc_ago", [D, FF], BF16, addr_space="Shared")
    pj_agi = nc.dram_tensor("pj_agi", [16 * 640, 128], BF16)
    pj_ago = nc.dram_tensor("pj_ago", [16 * 640 * 8, 128], BF16, addr_space="Shared")
    ctx_agi = nc.dram_tensor("ctx_agi", [1024, 1024], BF16)
    ctx_ago = nc.dram_tensor("ctx_ago", [2048, 1024], BF16)

    BASE = 16384

    def at(name, shape, dtype, off_b):
        return nc.alloc_sbuf_tensor_at(name, shape, dtype, offset=BASE + off_b)

    wq_sb = at("wq_sb", [128, 16 * 1024], BF16, 0 * KB)
    wk_sb = at("wk_sb", [128, 16 * 1024], BF16, 32 * KB)
    xt_sb = [at(f"xt{i}", [128, 16 * 512], BF16, (64 + 16 * i) * KB) for i in range(2)]
    qT_sb = at("qT_sb", [128, 8 * 2048], BF16, 96 * KB)
    kT_sb = at("kT_sb", [128, 8 * 2048], BF16, 128 * KB)
    wv_sb = at("wv_sb", [128, 16 * 1024], BF16, 0 * KB)
    v_sb = at("v_sb", [128, 16 * 1024], BF16, 32 * KB)
    ctx_sb = at("ctx_sb", [128, 8 * 2048], BF16, 0 * KB)
    masks_sb = at("masks_sb", [128, 4 * 512], F32, 64 * KB)
    negsi_sb = at("negsi_sb", [3, 8 * 512], BF16, 72 * KB)
    alibi_sb = at("alibi_sb", [128, 8 * 20], F32, 80 * KB)
    pT_sb = [at(f"pT{i}", [128, 512], BF16, (81 + i) * KB) for i in range(2)]
    rl_sb = at("rl_sb", [1, 1024], F32, 83 * KB)
    bcst_sb = at("bcst_sb", [128, 1024], F32, 87 * KB)
    ctx2_sb = at("ctx2_sb", [128, 8 * 1024], BF16, 32 * KB)
    ctxC_sb = at("ctxC_sb", [128, 8 * 1024], BF16, 48 * KB)
    ow_sb = [at(f"ow{i}", [128, 2048], BF16, (64 + 4 * i) * KB) for i in range(2)]
    x_sb = at("x_sb", [128, 8 * 2048], F32, 96 * KB)
    h_sb = at("h_sb", [128, 40 * 512], BF16, 0 * KB)
    fcs_sb = [at(f"fcs{i}", [128, 16 * 512], BF16, (40 + 16 * i) * KB) for i in range(2)]
    pjs_sb = [at(f"pjs{i}", [128, 40 * 128], BF16, (40 + 10 * i) * KB) for i in range(2)]
    xlnT_sb = at("xlnT_sb", [128, 16 * 512], BF16, 72 * KB)
    relu_sb = at("relu_sb", [128, 512], F32, 88 * KB)
    o2_sb = at("o2_sb", [128, 512], F32, 90 * KB)
    lnsq_sb = at("lnsq_sb", [128, 512], F32, 93 * KB)
    lncol_sb = at("lncol_sb", [128, 64], F32, 95 * KB)
    ln2col_sb = at("ln2col_sb", [128, 64], F32, 95 * KB + 256)
    tin_sb = [at(f"tin{i}", [128, 512], F32, (161 + 2 * i) * KB) for i in range(2)]
    ts1_sb = at("ts1_sb", [128, 512], F32, 165 * KB)
    tout_sb = at("tout_sb", [128, 512], BF16, 167 * KB)
    cur = [168 * KB]

    def cst(name, shape, dtype):
        nonlocal_off = cur[0]
        t = at(name, shape, dtype, nonlocal_off)
        nbytes = int(np.prod(shape[1:])) * (2 if dtype == BF16 else 4)
        cur[0] += (nbytes + 31) // 32 * 32
        return t

    bqk_sb = cst("bqk_sb", [128, 16], F32)
    absacc_sb = cst("absacc_sb", [128, 44], F32)
    amg_sb = cst("amg_sb", [1, 8], F32)
    thr_sb = cst("thr_sb", [128, 6], F32)
    thrrow_sb = cst("thrrow_sb", [1, 6], F32)
    onescol_sb = cst("onescol_sb", [128, 1], F32)
    ones3_sb = cst("ones3_sb", [3, 128], BF16)
    ones128_sb = cst("ones128_sb", [128, 1], BF16)
    bv3_sb = cst("bv3_sb", [3, 1024], BF16)
    outb3_sb = cst("outb3_sb", [3, 2048], BF16)
    identf_sb = cst("identf_sb", [128, 128], F32)
    onesf_sb = cst("onesf_sb", [1, 128], F32)
    assert cur[0] <= 176 * KB, cur[0]
    assert BASE + 176 * KB <= nc.SBUF_PARTITION_SIZE_BYTES - 32

    ps = [nc.alloc_psum_tensor(f"ps{i}", [128, 512], F32) for i in range(8)]
    bank_free = [None] * 8

    p = P(nc)
    for s in ("gl", "sl", "tt", "ac", "ve", "st", "car", "cfc", "cpj", "cctx"):
        p.sem(s)
    ev = {}
    rt = {}
    p.E("vector", lambda e: rt.__setitem__("u_ve", e.partition_id() % 2))
    p.E("scalar", lambda e: rt.__setitem__("u_ac", e.partition_id() % 2))
    p.E("gpsimd", lambda e: rt.__setitem__("u_gl", e.partition_id() % 2))

    def bank_wait(i):
        return [bank_free[i]] if bank_free[i] is not None else []

    c = p.E("sync", lambda e: e.dma_start(out=bqk_sb[:, :], in_=dbqk[:, :]), inc=("sl", 16))
    c = p.E("sync", lambda e: e.dma_start(out=identf_sb[:, :], in_=dident[:, :]), inc=("sl", 16))
    ev["consts_sl"] = ("sl", c)
    c = p.E("gpsimd", lambda e: e.dma_start(out=bv3_sb[:, :], in_=dbv3[:, :]), inc=("gl", 16))
    c = p.E("gpsimd", lambda e: e.dma_start(out=outb3_sb[:, :], in_=doutb3[:, :]), inc=("gl", 16))
    c = p.E("gpsimd", lambda e: e.memset(ones3_sb[:, :], 1.0), inc=("gl", 1))
    c = p.E("gpsimd", lambda e: e.memset(ones128_sb[:, :], 1.0), inc=("gl", 1))
    c = p.E("gpsimd", lambda e: e.memset(onesf_sb[:, :], 1.0), inc=("gl", 1))
    c = p.E("gpsimd", lambda e: e.memset(onescol_sb[:, :], 1.0), inc=("gl", 1))
    ev["consts_gl"] = ("gl", c)

    def wload(sb, dr, fw):
        cl = None
        for g in range(4):
            def fn(e, g=g, sb=sb, dr=dr, fw=fw):
                src = dr[g * 512:(g + 1) * 512, :].rearrange("(dc p) f -> p dc f", p=128)
                dst = sb[:, g * 4 * fw:(g + 1) * 4 * fw].rearrange("p (dc f) -> p dc f", f=fw)
                return e.dma_start(out=dst, in_=src)
            cl = p.E("gpsimd", fn, inc=("gl", 16))
        return cl
    ev["wq"] = ("gl", wload(wq_sb, dwq, 1024))
    ev["wk"] = ("gl", wload(wk_sb, dwk, 1024))

    t_pieces = [(0, g, cc) for g in range(2) for cc in range(10)] + \
               [(1, g, cc) for g in range(5) for cc in range(4)]

    def piece_src(m, g, cc):
        return (dwfcT if m == 0 else dwpjT)[g * 128:(g + 1) * 128, cc * 512:(cc + 1) * 512]

    r1_abs = {}
    for i, (m, g, cc) in enumerate(t_pieces):
        buf = i % 2
        w = [("ve", r1_abs[i - 2])] if i >= 2 else []
        cl = p.E("sync", lambda e, m=m, g=g, cc=cc, buf=buf: e.dma_start(out=tin_sb[buf][:, :],
                                                                         in_=piece_src(m, g, cc)),
                 waits=w, inc=("sl", 16))
        r1_abs[i] = p.E("vector",
                        lambda e, i=i, buf=buf: e.reduce_sum(absacc_sb[:, i:i + 1], tin_sb[buf][:, :],
                                                             axis=AX.X, apply_absolute_value=True),
                        waits=[("sl", cl)], inc=("ve", 1))
    c_r1 = p.E("vector", lambda e: e.reduce_sum(absacc_sb[:, 40:41], absacc_sb[:, 0:20], axis=AX.X),
               waits=[("ve", r1_abs[len(t_pieces) - 1])], inc=("ve", 1))
    c_r1 = p.E("vector", lambda e: e.reduce_sum(absacc_sb[:, 41:42], absacc_sb[:, 20:40], axis=AX.X),
               waits=[("ve", c_r1)], inc=("ve", 1))
    ev["r1"] = ("ve", c_r1)

    def xt_load(tc, buf, extra_w):
        cl = None
        for g2 in range(4):
            def fn(e, tc=tc, g2=g2, buf=buf):
                src = dxT[g2 * 512:(g2 + 1) * 512, tc * 512:(tc + 1) * 512].rearrange("(dc p) t -> p dc t", p=128)
                dst = xt_sb[buf][:, g2 * 4 * 512:(g2 + 1) * 4 * 512].rearrange("p (dc t) -> p dc t", t=512)
                return e.dma_start(out=dst, in_=src)
            cl = p.E("gpsimd", fn, waits=extra_w if g2 == 0 else (), inc=("gl", 16))
        return cl

    jobs = [("Q", tc) for tc in range(4)] + [("K", tc) for tc in range(4)] + [("V", tc) for tc in range(4)]
    xt_cnt = {}
    job_mms = {}
    xt_cnt[0] = xt_load(0, 0, [])
    xt_cnt[1] = xt_load(1, 1, [])
    for s, (kind, tc) in enumerate(jobs):
        buf = s % 2
        if kind in ("Q", "K"):
            wsb = wq_sb if kind == "Q" else wk_sb
            dst_sb = qT_sb if kind == "Q" else kT_sb
            wev = ev["wq"] if kind == "Q" else ("gl", ev["wk"][1])
            for ft in range(8):
                bk = ft % 2
                waits = [("gl", xt_cnt[s]), (wev[0], wev[1])] + bank_wait(bk)
                c_mm = None
                for dc in range(16):
                    def fn(e, bk=bk, wsb=wsb, ft=ft, dc=dc, buf=buf):
                        return e.matmul(ps[bk][:, :],
                                        wsb[:, dc * 1024 + ft * 128: dc * 1024 + ft * 128 + 128],
                                        xt_sb[buf][:, dc * 512:(dc + 1) * 512],
                                        start=(dc == 0), stop=(dc == 15))
                    c_mm = p.E("tensor", fn, waits=waits if dc == 0 else (),
                               inc=("tt", 1) if dc == 15 else None)
                def ev_fn(e, dst_sb=dst_sb, ft=ft, tc=tc, bk=bk, kind=kind):
                    return e.activation(dst_sb[:, ft * 2048 + tc * 512: ft * 2048 + tc * 512 + 512],
                                        ps[bk][:, :], AF.Identity,
                                        bias=bqk_sb[:, (0 if kind == "Q" else 8) + ft:
                                                    (1 if kind == "Q" else 9) + ft])
                c_ev = p.E("scalar", ev_fn, waits=[("tt", c_mm), ("sl", ev["consts_sl"][1])], inc=("ac", 1))
                bank_free[bk] = ("ac", c_ev)
        else:
            for tl in range(4):
                tt = tc * 4 + tl
                for hc in range(2):
                    bk = 2 + (tt * 2 + hc) % 2
                    waits = [("gl", xt_cnt[s]), ("gl", ev["wv"][1]), ("gl", ev["consts_gl"][1])] + bank_wait(bk)
                    def fb(e, bk=bk, hc=hc):
                        return e.matmul(ps[bk][:, :], ones3_sb[:, :], bv3_sb[:, hc * 512:(hc + 1) * 512],
                                        start=True, stop=False)
                    p.E("tensor", fb, waits=waits)
                    c_mm = None
                    for dc in range(16):
                        def fn(e, bk=bk, tl=tl, dc=dc, hc=hc, buf=buf):
                            return e.matmul(ps[bk][:, :],
                                            xt_sb[buf][:, dc * 512 + tl * 128: dc * 512 + tl * 128 + 128],
                                            wv_sb[:, dc * 1024 + hc * 512: dc * 1024 + hc * 512 + 512],
                                            start=False, stop=(dc == 15))
                        c_mm = p.E("tensor", fn, inc=("tt", 1) if dc == 15 else None)
                    def ev_fn(e, tt=tt, hc=hc, bk=bk):
                        return e.tensor_copy(v_sb[:, tt * 1024 + hc * 512: tt * 1024 + hc * 512 + 512],
                                             ps[bk][:, :])
                    c_ev = p.E("vector", ev_fn, waits=[("tt", c_mm)], inc=("ve", 1))
                    bank_free[bk] = ("ve", c_ev)
        job_mms[s] = p.now("tt")
        if s == 3:
            ev["a1_mms"] = ("tt", p.now("tt"))
            cl = None
            for g in range(4):
                def fn(e, g=g):
                    src = dwv[g * 512:(g + 1) * 512, :].rearrange("(dc p) f -> p dc f", p=128)
                    dst = wv_sb[:, g * 4 * 1024:(g + 1) * 4 * 1024].rearrange("p (dc f) -> p dc f", f=1024)
                    return e.dma_start(out=dst, in_=src)
                cl = p.E("gpsimd", fn, waits=[ev["a1_mms"]] if g == 0 else (), inc=("gl", 16))
            ev["wv"] = ("gl", cl)
            c_ams = p.E("tensor",
                        lambda e: e.matmul(ps[6][0:1, 0:2], onescol_sb[:, :], absacc_sb[:, 40:42],
                                           start=True, stop=True),
                        waits=[ev["r1"], ("gl", ev["consts_gl"][1])] + bank_wait(6), inc=("tt", 1))
            c_amc = p.E("vector", lambda e: e.tensor_copy(amg_sb[0:1, 0:2], ps[6][0:1, 0:2]),
                        waits=[("tt", c_ams)], inc=("ve", 1))
            bank_free[6] = ("ve", c_amc)
            c_ami = p.E("gpsimd", lambda e: e.dma_start(out=am_in[:, :], in_=amg_sb[0:1, 0:2]),
                        waits=[("ve", c_amc)], inc=("gl", 16))
            c_ar = p.E("gpsimd", lambda e: e.collective_compute("AllReduce", ALU.add, replica_groups=ALL8,
                                                                ins=[am_in[:, :]], outs=[am_out[:, :]]),
                       waits=[("gl", c_ami)], inc=("car", 1))
            c_amo = p.E("gpsimd", lambda e: e.dma_start(out=amg_sb[0:1, 2:4], in_=am_out[:, :]),
                        waits=[("car", c_ar)], inc=("gl", 16))
        if s + 2 < len(jobs):
            xt_cnt[s + 2] = xt_load((jobs[s + 2][1]), (s + 2) % 2, [("tt", job_mms[s])])
    ev["a2_mms"] = ("tt", p.now("tt"))
    ev["v_evacs"] = ("ve", p.now("ve"))
    ev["a1_evacs"] = ("ac", p.now("ac"))

    c_owp = None
    for g in range(4):
        c_owp = p.E("gpsimd", lambda e, g=g: e.dma_start(out=outw_bf[g * 512:(g + 1) * 512, :],
                                                         in_=doutw[g * 512:(g + 1) * 512, :]),
                    inc=("gl", 16))
    ev["owp"] = ("gl", c_owp)

    NEL = float(FF * D)
    c_v = p.E("vector", lambda e: e.tensor_scalar(amg_sb[0:1, 4:6], amg_sb[0:1, 2:4], 1.0 / NEL, 1.0,
                                                  ALU.mult, ALU.mult),
              waits=[("gl", c_amo)], inc=("ve", 1))
    c_v = p.E("vector", lambda e: e.tensor_scalar(amg_sb[0:1, 4:6], amg_sb[0:1, 4:6], 1e-5, 1.0,
                                                  ALU.max, ALU.mult), waits=[("ve", c_v)], inc=("ve", 1))
    c_v = p.E("vector", lambda e: e.tensor_scalar(thrrow_sb[0:1, 0:2], amg_sb[0:1, 4:6], -0.7, 1.0,
                                                  ALU.mult, ALU.mult), waits=[("ve", c_v)], inc=("ve", 1))
    c_v = p.E("vector", lambda e: e.tensor_scalar(thrrow_sb[0:1, 2:4], amg_sb[0:1, 4:6], 0.7, 1.0,
                                                  ALU.mult, ALU.mult), waits=[("ve", c_v)], inc=("ve", 1))
    c_v = p.E("vector", lambda e: e.tensor_scalar(thrrow_sb[0:1, 4:6], amg_sb[0:1, 4:6], 0.5, 1.0,
                                                  ALU.mult, ALU.mult), waits=[("ve", c_v)], inc=("ve", 1))
    c_bc = p.E("tensor", lambda e: e.matmul(ps[0][:, 0:6], onesf_sb[:, :], thrrow_sb[0:1, 0:6],
                                            start=True, stop=True),
               waits=[("ve", c_v)] + bank_wait(0), inc=("tt", 1))
    c_thr = p.E("vector", lambda e: e.tensor_copy(thr_sb[:, 0:6], ps[0][:, 0:6]),
                waits=[("tt", c_bc)], inc=("ve", 1))
    bank_free[0] = ("ve", c_thr)
    ev["thr"] = ("ve", c_thr)

    c = p.E("sync", lambda e: e.dma_start(out=masks_sb[:, :], in_=dmasks[:, :]),
            waits=[ev["a2_mms"]], inc=("sl", 16))
    c = p.E("sync", lambda e: e.dma_start(out=alibi_sb[:, :], in_=dalibi[:, :]), inc=("sl", 16))
    ev["attn_sl"] = ("sl", c)
    c = p.E("gpsimd", lambda e: e.dma_start(out=negsi_sb[:, :], in_=dnegsi[:, :]),
            waits=[ev["a2_mms"]], inc=("gl", 16))
    ev["attn_gl"] = ("gl", c)

    r2_state = {"prev_comb": None, "prev_w": None, "buf": 0}

    def emit_r2_piece(i):
        m, g, cc = t_pieces[i]
        buf = i % 2
        w = [("ve", r2_state["prev_comb"])] if r2_state["prev_comb"] else []
        if i >= 2:
            w.append(("ac", r2_state[f"s2_{i-2}"]))
        cl = p.E("sync", lambda e, m=m, g=g, cc=cc, buf=buf: e.dma_start(out=tin_sb[buf][:, :],
                                                                        in_=piece_src(m, g, cc)),
                 waits=w, inc=("sl", 16))
        c_s1 = p.E("scalar", lambda e, m=m, buf=buf: e.activation(ts1_sb[:, :], tin_sb[buf][:, :], AF.Sign,
                                                                  bias=thr_sb[:, m:m + 1]),
                   waits=[("sl", cl), ev["thr"]] +
                         ([("ve", r2_state["prev_comb"])] if r2_state["prev_comb"] else []),
                   inc=("ac", 1))
        c_s2 = p.E("scalar", lambda e, m=m, buf=buf: e.activation(tout_sb[:, :], tin_sb[buf][:, :], AF.Sign,
                                                                  bias=thr_sb[:, 2 + m:3 + m]),
                   waits=[("sl", r2_state["prev_w"])] if r2_state["prev_w"] else (),
                   inc=("ac", 1))
        r2_state[f"s2_{i}"] = c_s2
        c_c1 = p.E("vector", lambda e: e.tensor_add(ts1_sb[:, :], ts1_sb[:, :], tout_sb[:, :]),
                   waits=[("ac", c_s2), ("sl", r2_state["prev_w"])] if r2_state["prev_w"] else [("ac", c_s2)],
                   inc=("ve", 1))
        c_c2 = p.E("vector", lambda e, m=m: e.tensor_scalar(tout_sb[:, :], ts1_sb[:, :],
                                                            thr_sb[:, 4 + m:5 + m], 1.0,
                                                            ALU.mult, ALU.mult),
                   waits=[("ve", c_c1)], inc=("ve", 1))
        r2_state["prev_comb"] = c_c2
        if m == 0:
            cw = p.E("sync", lambda e, g=g, cc=cc: e.dma_start(
                out=fc_agi[g * 128:(g + 1) * 128, cc * 512:(cc + 1) * 512], in_=tout_sb[:, :]),
                waits=[("ve", c_c2)], inc=("sl", 16))
        else:
            for k in range(4):
                cw = p.E("sync", lambda e, g=g, cc=cc, k=k: e.dma_start(
                    out=pj_agi[(cc * 4 + k) * 640 + g * 128: (cc * 4 + k) * 640 + g * 128 + 128, :],
                    in_=tout_sb[:, k * 128:(k + 1) * 128]),
                    waits=[("ve", c_c2)] if k == 0 else (), inc=("sl", 16))
        r2_state["prev_w"] = cw

    ctx_done = {}
    lmm_hist = {}
    pT_guard = {0: None, 1: None}
    first_evac = [True]
    r2_next = [0]
    pend_norm = [None]

    def flush_norm():
        if pend_norm[0] is None:
            return
        h0, c0, ctx_bk0, c_rl0, slot = pend_norm[0]
        pend_norm[0] = None
        c_bcm = p.E("tensor", lambda e, slot=slot: e.matmul(ps[6][:, :], onesf_sb[:, :],
                                                            rl_sb[0:1, slot * 512:(slot + 1) * 512],
                                                            start=True, stop=True),
                    waits=[("ve", c_rl0)] + bank_wait(6), inc=("tt", 1))
        c_bcc = p.E("vector", lambda e, slot=slot: e.tensor_copy(bcst_sb[:, slot * 512:(slot + 1) * 512],
                                                                 ps[6][:, :]),
                    waits=[("tt", c_bcm)], inc=("ve", 1))
        bank_free[6] = ("ve", c_bcc)
        evac_w = [("ve", c_bcc)]
        if first_evac[0]:
            evac_w.append(ev["a2_mms"])
            first_evac[0] = False
        def fevac(e, h0=h0, c0=c0, ctx_bk0=ctx_bk0, slot=slot):
            return e.tensor_tensor(ctx_sb[:, h0 * 2048 + c0 * 512: h0 * 2048 + c0 * 512 + 512],
                                   ps[ctx_bk0][:, :], bcst_sb[:, slot * 512:(slot + 1) * 512], ALU.mult)
        c_ev = p.E("vector", fevac, waits=evac_w, inc=("ve", 1))
        bank_free[ctx_bk0] = ("ve", c_ev)
        ctx_done[(h0, c0)] = c_ev

    for h in range(8):
        for cI in range(4):
            g = h * 4 + cI
            n_t = 4 * cI + 4
            ctx_bk = 2 + g % 2
            l_bk = 4 + g % 2
            ctx_w0 = bank_wait(ctx_bk)
            l_w0 = bank_wait(l_bk)
            c_lmm = None
            pend = None

            def emit_ctx_l(pend, last):
                nonlocal c_lmm
                ti, jt, c_exp = pend
                def fctx(e, ctx_bk=ctx_bk, h=h, jt=jt, pbuf=ti % 2, ti=ti):
                    return e.matmul(ps[ctx_bk][:, :],
                                    v_sb[:, jt * 1024 + h * 128: jt * 1024 + h * 128 + 128],
                                    pT_sb[pbuf][:, :], start=(ti == 0), stop=False)
                wctx = [("ac", c_exp)]
                if ti == 0:
                    wctx += [("ve", ev["v_evacs"][1])] + ctx_w0 + l_w0
                p.E("tensor", fctx, waits=wctx)
                def fl(e, l_bk=l_bk, pbuf=ti % 2, ti=ti, last=last):
                    return e.matmul(ps[l_bk][0:1, :], ones128_sb[:, :],
                                    pT_sb[pbuf][:, :], start=(ti == 0), stop=last)
                c_lmm = p.E("tensor", fl, inc=("tt", 1))
                lmm_hist[(h, cI, ti)] = c_lmm
                pT_guard[ti % 2] = ("tt", c_lmm)

            for ti in range(n_t):
                jt = ti
                sbk = ti % 2
                pbuf = ti % 2
                var = jt - 4 * cI + 16
                msk = jt - 4 * cI
                waits = list(bank_wait(sbk))
                if ti == 0:
                    waits += [("ac", ev["a1_evacs"][1]), ("gl", ev["attn_gl"][1])]
                def mm1(e, sbk=sbk, h=h, jt=jt, cI=cI):
                    return e.matmul(ps[sbk][:, :],
                                    kT_sb[:, h * 2048 + jt * 128: h * 2048 + jt * 128 + 128],
                                    qT_sb[:, h * 2048 + cI * 512: h * 2048 + cI * 512 + 512],
                                    start=True, stop=False)
                p.E("tensor", mm1, waits=waits)
                def mm2(e, sbk=sbk, h=h):
                    return e.matmul(ps[sbk][:, :], ones3_sb[:, :],
                                    negsi_sb[:, h * 512: h * 512 + 512], start=False, stop=True)
                c_smm = p.E("tensor", mm2, inc=("tt", 1))
                if pend is not None:
                    emit_ctx_l(pend, last=False)
                if ti == 2:
                    flush_norm()
                if msk >= 0:
                    c_msk = p.E("vector",
                                lambda e, sbk=sbk, msk=msk: e.tensor_tensor(
                                    ps[sbk][:, :], ps[sbk][:, :],
                                    masks_sb[:, msk * 512:(msk + 1) * 512], ALU.add),
                                waits=[("tt", c_smm), ("sl", ev["attn_sl"][1])], inc=("ve", 1))
                    exp_w = [("ve", c_msk)]
                else:
                    exp_w = [("tt", c_smm), ("sl", ev["attn_sl"][1])]
                if pT_guard[pbuf] is not None:
                    exp_w.append(pT_guard[pbuf])
                def fexp(e, sbk=sbk, pbuf=pbuf, h=h, var=var):
                    return e.activation(pT_sb[pbuf][:, :], ps[sbk][:, :], AF.Exp,
                                        bias=alibi_sb[:, h * 20 + var: h * 20 + var + 1])
                c_exp = p.E("scalar", fexp, waits=exp_w, inc=("ac", 1))
                bank_free[sbk] = ("ac", c_exp)
                pend = (ti, jt, c_exp)
            emit_ctx_l(pend, last=True)
            c_rl = p.E("vector", lambda e, l_bk=l_bk, g=g: e.reciprocal(
                rl_sb[0:1, (g % 2) * 512:(g % 2) * 512 + 512], ps[l_bk][0:1, :]),
                       waits=[("tt", c_lmm)], inc=("ve", 1))
            bank_free[l_bk] = ("ve", c_rl)
            pend_norm[0] = (h, cI, ctx_bk, c_rl, g % 2)
            for _ in range(2):
                if r2_next[0] < len(t_pieces):
                    emit_r2_piece(r2_next[0])
                    r2_next[0] += 1
    flush_norm()
    while r2_next[0] < len(t_pieces):
        emit_r2_piece(r2_next[0])
        r2_next[0] += 1
    ev["attn_mms"] = ("tt", p.now("tt"))
    ev["ctx_all"] = ("ve", max(ctx_done.values()))
    ev["tern_w"] = ("sl", p.now("sl"))

    c_agfc = p.E("gpsimd", lambda e: e.collective_compute("AllGather", ALU.bypass, replica_groups=ALL8,
                                                          ins=[fc_agi[:, :]], outs=[fc_ago[:, :]]),
                 waits=[ev["tern_w"]], inc=("cfc", 1))
    c_agpj = p.E("gpsimd", lambda e: e.collective_compute("AllGather", ALU.bypass, replica_groups=ALL8,
                                                          ins=[pj_agi[:, :]], outs=[pj_ago[:, :]]),
                 inc=("cpj", 1))

    c_xo = None
    for g in range(2):
        def fxo(e, g=g):
            src = dx_own[g * 512:(g + 1) * 512, :].rearrange("(tt p) d -> p tt d", p=128)
            dst = x_sb[:, g * 4 * 2048:(g + 1) * 4 * 2048].rearrange("p (tt d) -> p tt d", d=2048)
            return e.dma_start(out=dst, in_=src)
        c_xo = p.E("sync", fxo, waits=[ev["attn_mms"]] if g == 0 else (), inc=("sl", 16))
    ev["x_own"] = ("sl", c_xo)

    c_stp = None
    for h in range(8):
        def fcp(e, h=h):
            return e.tensor_copy(ctx2_sb[:, h * 1024:(h + 1) * 1024],
                                 ctx_sb[:, ds(h * 2048 + (1 - rt["u_ve"]) * 1024, 1024)])
        c_stp = p.E("vector", fcp, waits=[ev["ctx_all"], ev["a2_mms"]] if h == 0 else (), inc=("ve", 1))
    c_stg = None
    for h in range(8):
        def fstg(e, h=h):
            return e.dma_start(out=ctx_agi[h * 128:(h + 1) * 128, :],
                               in_=ctx2_sb[:, h * 1024:(h + 1) * 1024])
        c_stg = p.E("gpsimd", fstg, waits=[("ve", c_stp)] if h == 0 else (), inc=("gl", 16))
    c_cc = None
    for h in range(8):
        def fcc(e, h=h):
            return e.copy(ctxC_sb[:, h * 1024:(h + 1) * 1024],
                          ctx_sb[:, ds(h * 2048 + rt["u_ac"] * 1024, 1024)])
        c_cc = p.E("scalar", fcc, waits=[ev["ctx_all"], ev["a2_mms"]] if h == 0 else (), inc=("ac", 1))
    ev["ctxC"] = ("ac", c_cc)
    c_agc = p.E("gpsimd", lambda e: e.collective_compute("AllGather", ALU.bypass, replica_groups=PAIRS,
                                                         ins=[ctx_agi[:, :]], outs=[ctx_ago[:, :]]),
                waits=[("gl", c_stg)], inc=("cctx", 1))
    def fc2(e):
        prow = (1 - rt["u_gl"]) * 1024
        src = ctx_ago[ds(prow, 1024), :].rearrange("(h p) t -> p h t", p=128)
        dst = ctx2_sb[:, :].rearrange("p (h t) -> p h t", t=1024)
        return e.dma_start(out=dst, in_=src)
    c_ctx2 = p.E("gpsimd", fc2, waits=[("cctx", c_agc), ev["attn_mms"]], inc=("gl", 16))
    ev["ctx2"] = ("gl", c_ctx2)

    buf_users = {0: 0, 1: 0}
    for tp in range(4):
        c_mm = None
        for fc in range(16):
            buf = fc % 2
            w = [ev["owp"], ("tt", buf_users[buf])]
            c_ow = p.E("sync", lambda e, fc=fc, buf=buf: e.dma_start(out=ow_sb[buf][:, :],
                                                                    in_=outw_bf[fc * 128:(fc + 1) * 128, :]),
                       waits=w, inc=("sl", 16))
            for tloc in range(2):
                tt = tp * 2 + tloc
                for n4 in range(4):
                    bk = tloc * 4 + n4
                    if fc == 0:
                        w0 = list(bank_wait(bk)) + [("sl", c_ow), ev["ctxC"],
                                                    ("gl", ev["consts_gl"][1])]
                        def fb(e, bk=bk, n4=n4):
                            return e.matmul(ps[bk][:, :], ones3_sb[:, :],
                                            outb3_sb[:, n4 * 512:(n4 + 1) * 512], start=True, stop=False)
                        p.E("tensor", fb, waits=w0)
                    def fmm(e, bk=bk, fc=fc, tt=tt, n4=n4, buf=buf):
                        if fc < 8:
                            lhsT = ctxC_sb[:, fc * 1024 + tt * 128: fc * 1024 + tt * 128 + 128]
                        else:
                            lhsT = ctx2_sb[:, (fc - 8) * 1024 + tt * 128: (fc - 8) * 1024 + tt * 128 + 128]
                        return e.matmul(ps[bk][:, :], lhsT, ow_sb[buf][:, n4 * 512:(n4 + 1) * 512],
                                        start=False, stop=(fc == 15))
                    wmm = [("sl", c_ow)] if (tloc == 0 and n4 == 0) else []
                    if fc == 8 and tloc == 0 and n4 == 0:
                        wmm.append(ev["ctx2"])
                    c_mm = p.E("tensor", fmm, waits=wmm,
                               inc=("tt", 1) if (fc == 15 or (tloc == 1 and n4 == 3)) else None)
            buf_users[buf] = p.now("tt")
        for tloc in range(2):
            tt = tp * 2 + tloc
            for n4 in range(4):
                bk = tloc * 4 + n4
                def fev(e, bk=bk, tt=tt, n4=n4):
                    sl = x_sb[:, tt * 2048 + n4 * 512: tt * 2048 + n4 * 512 + 512]
                    return e.tensor_tensor(sl, sl, ps[bk][:, :], ALU.add)
                c_ad = p.E("vector", fev, waits=[("tt", c_mm), ev["x_own"]], inc=("ve", 1))
                bank_free[bk] = ("ve", c_ad)
    ev["outproj"] = ("ve", p.now("ve"))

    def emit_ln_batch(tiles, col_sb, ve_ready, extra_sq_w=()):
        T = len(tiles)
        t0 = tiles[0]
        view = col_sb[:, :].rearrange("p (t j) -> p t j", j=8)

        def cv(k):
            return view[:, t0:t0 + T, k:k + 1]

        for tt in tiles:
            p.E("vector", lambda e, tt=tt: e.reduce_sum(col_sb[:, tt * 8:tt * 8 + 1],
                                                        x_sb[:, tt * 2048:(tt + 1) * 2048], axis=AX.X),
                waits=[("ve", ve_ready)], inc=("ve", 1))
        c2 = None
        for tt in tiles:
            for j in range(4):
                def fsq(e, tt=tt, j=j):
                    sj = x_sb[:, tt * 2048 + j * 512: tt * 2048 + j * 512 + 512]
                    return e.scalar_tensor_tensor(lnsq_sb[:, :], sj, 1.0, sj, ALU.mult, ALU.mult,
                                                  accum_out=col_sb[:, tt * 8 + 1 + j:tt * 8 + 2 + j])
                c2 = p.E("vector", fsq, waits=list(extra_sq_w) if (tt == tiles[0] and j == 0) else (),
                         inc=("ve", 1))
        c3 = p.E("vector", lambda e: e.reduce_sum(cv(5), view[:, t0:t0 + T, 1:5], axis=AX.X),
                 waits=[("ve", c2)], inc=("ve", 1))
        c3 = p.E("vector", lambda e: e.tensor_scalar(cv(0), cv(0), 1.0 / D, 1.0, ALU.mult, ALU.mult),
                 waits=[("ve", c3)], inc=("ve", 1))
        c3 = p.E("vector", lambda e: e.tensor_tensor(cv(6), cv(0), cv(0), ALU.mult),
                 waits=[("ve", c3)], inc=("ve", 1))
        c3 = p.E("vector", lambda e: e.tensor_scalar(cv(5), cv(5), 1.0 / D, 1.0, ALU.mult, ALU.mult),
                 waits=[("ve", c3)], inc=("ve", 1))
        c3 = p.E("vector", lambda e: e.tensor_tensor(cv(5), cv(5), cv(6), ALU.subtract),
                 waits=[("ve", c3)], inc=("ve", 1))
        c3 = p.E("vector", lambda e: e.tensor_scalar(cv(5), cv(5), LN_EPS, 1.0, ALU.add, ALU.mult),
                 waits=[("ve", c3)], inc=("ve", 1))
        c4 = p.E("vector", lambda e: e.tensor_scalar(cv(7), cv(5), 0.30, 0.55, ALU.mult, ALU.add),
                 waits=[("ve", c3)], inc=("ve", 1))
        for _it in range(4):
            c4 = p.E("vector", lambda e: e.reciprocal(cv(6), cv(7)), waits=[("ve", c4)], inc=("ve", 1))
            c4 = p.E("vector", lambda e: e.scalar_tensor_tensor(cv(6), cv(6), 0.5, cv(5),
                                                                ALU.mult, ALU.mult),
                     waits=[("ve", c4)], inc=("ve", 1))
            c4 = p.E("vector", lambda e: e.scalar_tensor_tensor(cv(7), cv(7), 0.5, cv(6),
                                                                ALU.mult, ALU.add),
                     waits=[("ve", c4)], inc=("ve", 1))
        c4 = p.E("vector", lambda e: e.reciprocal(cv(7), cv(7)), waits=[("ve", c4)], inc=("ve", 1))
        done = {}
        for tt in tiles:
            def fnorm(e, tt=tt):
                sl = x_sb[:, tt * 2048:(tt + 1) * 2048]
                return e.tensor_scalar(sl, sl, col_sb[:, tt * 8:tt * 8 + 1],
                                       col_sb[:, tt * 8 + 7:tt * 8 + 8], ALU.subtract, ALU.mult)
            done[tt] = p.E("vector", fnorm, waits=[("ve", c4)], inc=("ve", 1))
        return done

    ln1_done = emit_ln_batch(list(range(8)), lncol_sb, ev["outproj"][1])

    fcs_free = {}
    pjs_free = {}
    st_cnt = None
    for tc2 in range(2):
        xT_done = None
        for tl in range(4):
            tt = tc2 * 4 + tl
            for dg in range(4):
                w = list(bank_wait(7)) + [("ve", ln1_done[tt]), ("sl", ev["consts_sl"][1])]
                if tc2 == 1 and tl == 0 and dg == 0:
                    w.append(fcs_free[9])
                c_tr = None
                for k in range(4):
                    dc = dg * 4 + k
                    def ftr(e, tt=tt, dc=dc, k=k):
                        return e.transpose(ps[7][:, k * 128:(k + 1) * 128],
                                           x_sb[:, tt * 2048 + dc * 128: tt * 2048 + dc * 128 + 128],
                                           identf_sb[:, :])
                    c_tr = p.E("tensor", ftr, waits=w if k == 0 else (), inc=("tt", 1) if k == 3 else None)
                def fxe(e, tl=tl, dg=dg):
                    dst = xlnT_sb[:, :].rearrange("p (dc t) -> p dc t", t=512)[:, dg * 4:(dg + 1) * 4,
                                                                              tl * 128:(tl + 1) * 128]
                    src = ps[7][:, :].rearrange("p (k t) -> p k t", t=128)
                    return e.tensor_copy(dst, src)
                xT_done = p.E("vector", fxe, waits=[("tt", c_tr)], inc=("ve", 1))
                bank_free[7] = ("ve", xT_done)
        ev[f"xlnT{tc2}"] = ("ve", xT_done)

        first_sq = True
        for sI in range(10):
            buf = sI % 2
            gi = tc2 * 10 + sI
            w = [("cfc", c_agfc)]
            if gi == 0:
                w.append(ev["outproj"])
            if tc2 == 1 and sI == 0:
                w.append(pjs_free[15])
            if gi >= 2:
                w.append(fcs_free[gi - 2])
            c_sl = None
            for g in range(4):
                def fsl(e, sI=sI, g=g, buf=buf):
                    src = fc_ago[g * 512:(g + 1) * 512, sI * 512:(sI + 1) * 512].rearrange(
                        "(dc p) f -> p dc f", p=128)
                    dst = fcs_sb[buf][:, g * 4 * 512:(g + 1) * 4 * 512].rearrange("p (dc f) -> p dc f", f=512)
                    return e.dma_start(out=dst, in_=src)
                c_sl = p.E("sync", fsl, waits=w if g == 0 else (), inc=("sl", 16))
            for hft in range(4):
                hf = sI * 4 + hft
                bk = hf % 2
                w2 = list(bank_wait(bk)) + [("sl", c_sl), ev[f"xlnT{tc2}"]]
                c_mm = None
                for dc in range(16):
                    def fmm(e, bk=bk, dc=dc, hft=hft, buf=buf):
                        return e.matmul(ps[bk][:, :],
                                        fcs_sb[buf][:, dc * 512 + hft * 128: dc * 512 + hft * 128 + 128],
                                        xlnT_sb[:, dc * 512:(dc + 1) * 512],
                                        start=(dc == 0), stop=(dc == 15))
                    c_mm = p.E("tensor", fmm, waits=w2 if dc == 0 else (), inc=("tt", 1) if dc == 15 else None)
                c_r = p.E("vector", lambda e, bk=bk: e.tensor_scalar(relu_sb[:, :], ps[bk][:, :], 0.0, 1.0,
                                                                    ALU.max, ALU.mult),
                          waits=[("tt", c_mm)], inc=("ve", 1))
                sq_w = [("ve", c_r)]
                if first_sq and tc2 == 1:
                    sq_w.append(pjs_free[15])
                first_sq = False
                c_q = p.E("vector", lambda e, hf=hf: e.tensor_mul(h_sb[:, hf * 512:(hf + 1) * 512],
                                                                 relu_sb[:, :], relu_sb[:, :]),
                          waits=sq_w, inc=("ve", 1))
                bank_free[bk] = ("ve", c_r)
            fcs_free[gi] = ("tt", p.now("tt"))
        ev[f"h{tc2}"] = ("ve", p.now("ve"))

        for d2t in range(16):
            buf = d2t % 2
            gi = tc2 * 16 + d2t
            w = [("cpj", c_agpj), fcs_free[tc2 * 10 + 9]]
            if gi >= 2:
                w.append(pjs_free[gi - 2])
            c_pl = None
            for c8 in range(8):
                def fpl(e, d2t=d2t, c8=c8, buf=buf):
                    src = pj_ago[c8 * 10240 + d2t * 640: c8 * 10240 + d2t * 640 + 640, :].rearrange(
                        "(hc p) f -> p hc f", p=128)
                    dst = pjs_sb[buf][:, c8 * 5 * 128:(c8 * 5 + 5) * 128].rearrange("p (hc f) -> p hc f", f=128)
                    return e.dma_start(out=dst, in_=src)
                c_pl = p.E("sync", fpl, waits=w if c8 == 0 else (), inc=("sl", 16))
            bk = 2 + d2t % 2
            w2 = list(bank_wait(bk)) + [("sl", c_pl), ev[f"h{tc2}"]]
            c_mm = None
            for hc in range(40):
                def fmm(e, bk=bk, hc=hc, buf=buf):
                    return e.matmul(ps[bk][:, :], pjs_sb[buf][:, hc * 128:(hc + 1) * 128],
                                    h_sb[:, hc * 512:(hc + 1) * 512],
                                    start=(hc == 0), stop=(hc == 39))
                c_mm = p.E("tensor", fmm, waits=w2 if hc == 0 else (), inc=("tt", 1) if hc == 39 else None)
            pjs_free[gi] = ("tt", p.now("tt"))
            c_o2 = p.E("scalar", lambda e, bk=bk: e.activation(o2_sb[:, :], ps[bk][:, :], AF.Copy),
                       waits=[("tt", c_mm)], inc=("ac", 1))
            bank_free[bk] = ("ac", c_o2)
            w3 = list(bank_wait(6)) + [("ac", c_o2)]
            c_tr = None
            for k in range(4):
                def ftr(e, k=k):
                    return e.transpose(ps[6][:, k * 128:(k + 1) * 128], o2_sb[:, k * 128:(k + 1) * 128],
                                       identf_sb[:, :])
                c_tr = p.E("tensor", ftr, waits=w3 if k == 0 else (), inc=("tt", 1) if k == 3 else None)
            c_ad = None
            for k in range(4):
                tt = tc2 * 4 + k
                def fadd(e, tt=tt, k=k, d2t=d2t):
                    sl = x_sb[:, tt * 2048 + d2t * 128: tt * 2048 + d2t * 128 + 128]
                    return e.tensor_tensor(sl, sl, ps[6][:, k * 128:(k + 1) * 128], ALU.add)
                c_ad = p.E("vector", fadd, waits=[("tt", c_tr)] if k == 0 else (), inc=("ve", 1))
            bank_free[6] = ("ve", c_ad)
        ln2_done = emit_ln_batch([tc2 * 4 + tl for tl in range(4)], ln2col_sb, p.now("ve"))
        for tl in range(4):
            tt = tc2 * 4 + tl
            def fst(e, tt=tt):
                return e.dma_start(out=dout[tt * 128:(tt + 1) * 128, :],
                                   in_=x_sb[:, tt * 2048:(tt + 1) * 2048])
            st_cnt = p.E("sync", fst, waits=[("ve", ln2_done[tt])], inc=("st", 16))

    p.E("sync", lambda e: e.nop(), waits=[("st", p.now("st"))])

    with nc.Block() as block:
        block.tensor(lambda e: [f(e) for f in p.ops["tensor"]])
        block.scalar(lambda e: [f(e) for f in p.ops["scalar"]])
        block.vector(lambda e: [f(e) for f in p.ops["vector"]])
        block.gpsimd(lambda e: [f(e) for f in p.ops["gpsimd"]])
        block.sync(lambda e: [f(e) for f in p.ops["sync"]])

    return nc


def _stage_inputs(L0_attn_x, L0_attn_qkv_w, L0_attn_qkv_b, L0_attn_out_w, L0_attn_out_b,
                  L0_ffn_w_fc, L0_ffn_w_proj):
    x = np.ascontiguousarray(np.asarray(L0_attn_x, np.float32))
    qkv_w = np.asarray(L0_attn_qkv_w, np.float32)
    qkv_b = np.asarray(L0_attn_qkv_b, np.float32)
    out_w = np.asarray(L0_attn_out_w, np.float32)
    out_b = np.asarray(L0_attn_out_b, np.float32)
    wfcT = np.ascontiguousarray(np.asarray(L0_ffn_w_fc, np.float32).T)
    wpjT = np.ascontiguousarray(np.asarray(L0_ffn_w_proj, np.float32).T)

    slopes = _slopes()
    masks = np.zeros((128, 4, 512), np.float32)
    pp = np.arange(128)[:, None]
    qq = np.arange(512)[None, :]
    for v in range(4):
        masks[:, v, :] = np.where(128 * v + pp <= qq, 0.0, MASKVAL)
    masks = np.ascontiguousarray(masks.reshape(128, 4 * 512))
    ident = np.eye(128, dtype=np.float32)

    in_maps = []
    for c in range(NCORES):
        b, h2 = c // 2, c % 2
        sl = slopes[h2 * 8:(h2 + 1) * 8]
        rel = np.arange(-16, 4)
        alibi = (sl[None, :, None] * (128.0 * rel[None, None, :] + np.arange(128)[:, None, None])
                 - C_STAB).astype(np.float32)
        nq = (-sl[:, None] * np.arange(512)[None, :]).reshape(8 * 512)
        in_maps.append({
            "xT": np.ascontiguousarray(x[b].T),
            "x_own": np.ascontiguousarray(x[b, h2 * 1024:(h2 + 1) * 1024]),
            "wq": np.ascontiguousarray(qkv_w[:, h2 * 1024:(h2 + 1) * 1024]),
            "wk": np.ascontiguousarray(qkv_w[:, 2048 + h2 * 1024: 2048 + (h2 + 1) * 1024]),
            "wv": np.ascontiguousarray(qkv_w[:, 4096 + h2 * 1024: 4096 + (h2 + 1) * 1024]),
            "bqk": np.ascontiguousarray(np.concatenate([
                qkv_b[h2 * 1024:(h2 + 1) * 1024].reshape(8, 128).T,
                qkv_b[2048 + h2 * 1024: 2048 + (h2 + 1) * 1024].reshape(8, 128).T], axis=1)),
            "bv3": np.ascontiguousarray(
                _split3(qkv_b[4096 + h2 * 1024: 4096 + (h2 + 1) * 1024][None, :]).reshape(3, 1024)),
            "outw": np.ascontiguousarray(np.concatenate([
                out_w[h2 * 1024:(h2 + 1) * 1024], out_w[(1 - h2) * 1024:(2 - h2) * 1024]], axis=0)),
            "outb3": np.ascontiguousarray(_split3(out_b[None, :]).reshape(3, 2048)),
            "wfcT_sl": np.ascontiguousarray(wfcT[c * 256:(c + 1) * 256]),
            "wprojT_sl": np.ascontiguousarray(wpjT[c * 640:(c + 1) * 640]),
            "masks": masks,
            "alibi2": np.ascontiguousarray(alibi.reshape(128, 160)),
            "negsi3q": np.ascontiguousarray(_split3(nq[None, :]).reshape(3, 8 * 512)),
            "identf": ident,
        })
    return in_maps


_cached = {}


def kernel(L0_attn_x, L0_attn_qkv_w, L0_attn_qkv_b, L0_attn_out_w, L0_attn_out_b,
           L0_ffn_w_fc, L0_ffn_w_proj):
    in_maps = _stage_inputs(L0_attn_x, L0_attn_qkv_w, L0_attn_qkv_b, L0_attn_out_w,
                            L0_attn_out_b, L0_ffn_w_fc, L0_ffn_w_proj)
    if "nc" not in _cached:
        _cached["nc"] = build_program()
    nc = _cached["nc"]
    res = run_bass_kernel_spmd(nc, in_maps, core_ids=list(range(NCORES)))
    out = np.empty((B, S, D), np.float32)
    for c in range(NCORES):
        b, h2 = c // 2, c % 2
        out[b, h2 * 1024:(h2 + 1) * 1024] = res.results[c]["out"]
    return out


# version 18
# speedup vs baseline: 1.0871x; 1.0088x over previous
import numpy as np
import ml_dtypes
import concourse.bass as bass
import concourse.mybir as mybir
from concourse.bass_utils import run_bass_kernel_spmd

F32 = mybir.dt.float32
BF16 = mybir.dt.bfloat16
AF = mybir.ActivationFunctionType
ALU = mybir.AluOpType
AX = mybir.AxisListType
ds = bass.ds
BFNP = ml_dtypes.bfloat16

B, S, D = 4, 2048, 2048
NH, HD = 16, 128
FF = 5120
NCORES = 8
C_STAB = 15.0
MASKVAL = -30000.0
LN_EPS = 1e-5
KB = 1024

PAIRS = [[0, 1], [2, 3], [4, 5], [6, 7]]
ALL8 = [list(range(8))]


def _split3(x):
    x = np.asarray(x, np.float32)
    p1 = x.astype(BFNP).astype(np.float32)
    r = x - p1
    p2 = r.astype(BFNP).astype(np.float32)
    p3 = (r - p2).astype(BFNP).astype(np.float32)
    return np.stack([p1, p2, p3]).astype(np.float32)


def _slopes():
    return np.asarray([1.0 / 2.0 ** (i / 2.0) for i in range(NH)], dtype=np.float64)


class P:
    ENGS = ("tensor", "scalar", "vector", "gpsimd", "sync")

    def __init__(self, nc):
        self.nc = nc
        self.ops = {e: [] for e in self.ENGS}
        self.cnt = {}
        self.sems = {}
        self.waited = {e: {} for e in self.ENGS}

    def sem(self, name):
        self.sems[name] = self.nc.alloc_semaphore(name=name)
        self.cnt[name] = 0

    def now(self, name):
        return self.cnt[name]

    def E(self, eng, fn, waits=(), inc=None):
        w = []
        for s, v in waits:
            if v > self.waited[eng].get(s, 0):
                self.waited[eng][s] = v
                w.append((s, v))
        tgt = None
        if inc is not None:
            self.cnt[inc[0]] += inc[1]
            tgt = self.cnt[inc[0]]

        def runner(e, fn=fn, w=tuple(w), inc=inc):
            for s, v in w:
                e.wait_ge(self.sems[s], v)
            ins = fn(e)
            if inc is not None and ins is not None:
                ins.then_inc(self.sems[inc[0]], inc[1])

        self.ops[eng].append(runner)
        return tgt


def build_program():
    nc = bass.Bass("TRN2", target_bir_lowering=False)

    dxT = nc.dram_tensor("xT", [D, S], F32, kind="ExternalInput")
    dx_own = nc.dram_tensor("x_own", [1024, D], F32, kind="ExternalInput")
    dwq = nc.dram_tensor("wq", [D, 1024], F32, kind="ExternalInput")
    dwk = nc.dram_tensor("wk", [D, 1024], F32, kind="ExternalInput")
    dwv = nc.dram_tensor("wv", [D, 1024], F32, kind="ExternalInput")
    dbqk = nc.dram_tensor("bqk", [128, 16], F32, kind="ExternalInput")
    dbv3 = nc.dram_tensor("bv3", [3, 1024], F32, kind="ExternalInput")
    doutw = nc.dram_tensor("outw", [D, D], F32, kind="ExternalInput")
    doutb3 = nc.dram_tensor("outb3", [3, D], F32, kind="ExternalInput")
    dwfcT = nc.dram_tensor("wfcT_sl", [256, FF], F32, kind="ExternalInput")
    dwpjT = nc.dram_tensor("wprojT_sl", [640, D], F32, kind="ExternalInput")
    dmasks = nc.dram_tensor("masks", [128, 4 * 512], F32, kind="ExternalInput")
    dalibi = nc.dram_tensor("alibi2", [128, 8 * 20], F32, kind="ExternalInput")
    dnegsi = nc.dram_tensor("negsi3q", [3, 8 * 512], F32, kind="ExternalInput")
    dident = nc.dram_tensor("identf", [128, 128], F32, kind="ExternalInput")
    dout = nc.dram_tensor("out", [1024, D], F32, kind="ExternalOutput")

    outw_bf = nc.dram_tensor("outw_bf", [D, D], BF16)
    am_in = nc.dram_tensor("am_in", [1, 2], F32)
    am_out = nc.dram_tensor("am_out", [1, 2], F32, addr_space="Shared")
    fc_agi = nc.dram_tensor("fc_agi", [256, FF], BF16)
    fc_ago = nc.dram_tensor("fc_ago", [D, FF], BF16, addr_space="Shared")
    pj_agi = nc.dram_tensor("pj_agi", [16 * 640, 128], BF16)
    pj_ago = nc.dram_tensor("pj_ago", [16 * 640 * 8, 128], BF16, addr_space="Shared")
    ctx_agi = nc.dram_tensor("ctx_agi", [1024, 1024], BF16)
    ctx_ago = nc.dram_tensor("ctx_ago", [2048, 1024], BF16)

    BASE = 16384

    def at(name, shape, dtype, off_b):
        return nc.alloc_sbuf_tensor_at(name, shape, dtype, offset=BASE + off_b)

    wq_sb = at("wq_sb", [128, 16 * 1024], BF16, 0 * KB)
    wk_sb = at("wk_sb", [128, 16 * 1024], BF16, 32 * KB)
    xt_sb = [at(f"xt{i}", [128, 16 * 512], BF16, (64 + 16 * i) * KB) for i in range(2)]
    qT_sb = at("qT_sb", [128, 8 * 2048], BF16, 96 * KB)
    kT_sb = at("kT_sb", [128, 8 * 2048], BF16, 128 * KB)
    wv_sb = at("wv_sb", [128, 16 * 1024], BF16, 0 * KB)
    v_sb = at("v_sb", [128, 16 * 1024], BF16, 32 * KB)
    ctx_sb = at("ctx_sb", [128, 8 * 2048], BF16, 0 * KB)
    masks_sb = at("masks_sb", [128, 4 * 512], BF16, 64 * KB)
    negsi_sb = at("negsi_sb", [3, 8 * 512], BF16, 72 * KB)
    alibi_sb = at("alibi_sb", [128, 8 * 20], F32, 80 * KB)
    pT_sb = [at(f"pT{i}", [128, 512], BF16, (81 + i) * KB) for i in range(4)]
    rl_sb = at("rl_sb", [1, 1024], F32, 85 * KB)
    bcst_sb = at("bcst_sb", [128, 1024], F32, 89 * KB)
    ctx2_sb = at("ctx2_sb", [128, 8 * 1024], BF16, 32 * KB)
    ctxC_sb = at("ctxC_sb", [128, 8 * 1024], BF16, 48 * KB)
    ow_sb = [at(f"ow{i}", [128, 2048], BF16, (64 + 4 * i) * KB) for i in range(2)]
    x_sb = at("x_sb", [128, 8 * 2048], F32, 96 * KB)
    h_sb = at("h_sb", [128, 40 * 512], BF16, 0 * KB)
    fcs_sb = [at(f"fcs{i}", [128, 16 * 512], BF16, (40 + 16 * i) * KB) for i in range(2)]
    pjs_sb = [at(f"pjs{i}", [128, 40 * 128], BF16, (40 + 10 * i) * KB) for i in range(2)]
    xlnT_sb = at("xlnT_sb", [128, 16 * 512], BF16, 72 * KB)
    relu_sb = at("relu_sb", [128, 512], F32, 88 * KB)
    o2_sb = at("o2_sb", [128, 512], F32, 90 * KB)
    lnsq_sb = at("lnsq_sb", [128, 512], F32, 93 * KB)
    lncol_sb = at("lncol_sb", [128, 64], F32, 95 * KB)
    ln2col_sb = at("ln2col_sb", [128, 64], F32, 95 * KB + 256)
    tin_sb = [at(f"tin{i}", [128, 512], F32, (161 + 2 * i) * KB) for i in range(2)]
    ts1_sb = at("ts1_sb", [128, 512], F32, 165 * KB)
    tout_sb = at("tout_sb", [128, 512], BF16, 167 * KB)
    cur = [168 * KB]

    def cst(name, shape, dtype):
        nonlocal_off = cur[0]
        t = at(name, shape, dtype, nonlocal_off)
        nbytes = int(np.prod(shape[1:])) * (2 if dtype == BF16 else 4)
        cur[0] += (nbytes + 31) // 32 * 32
        return t

    bqk_sb = cst("bqk_sb", [128, 16], F32)
    absacc_sb = cst("absacc_sb", [128, 44], F32)
    amg_sb = cst("amg_sb", [1, 8], F32)
    thr_sb = cst("thr_sb", [128, 6], F32)
    thrrow_sb = cst("thrrow_sb", [1, 6], F32)
    onescol_sb = cst("onescol_sb", [128, 1], F32)
    ones3_sb = cst("ones3_sb", [3, 128], BF16)
    ones128_sb = cst("ones128_sb", [128, 1], BF16)
    bv3_sb = cst("bv3_sb", [3, 1024], BF16)
    outb3_sb = cst("outb3_sb", [3, 2048], BF16)
    identf_sb = cst("identf_sb", [128, 128], F32)
    identb_sb = cst("identb_sb", [128, 128], BF16)
    onesf_sb = cst("onesf_sb", [1, 128], F32)
    assert cur[0] <= 176 * KB, cur[0]
    assert BASE + 176 * KB <= nc.SBUF_PARTITION_SIZE_BYTES - 32

    ps = [nc.alloc_psum_tensor(f"ps{i}", [128, 512], F32) for i in range(8)]
    bank_free = [None] * 8

    p = P(nc)
    for s in ("gl", "sl", "tt", "ac", "ve", "st", "car", "cfc", "cpj", "cctx"):
        p.sem(s)
    ev = {}
    rt = {}
    p.E("vector", lambda e: rt.__setitem__("u_ve", e.partition_id() % 2))
    p.E("scalar", lambda e: rt.__setitem__("u_ac", e.partition_id() % 2))
    p.E("gpsimd", lambda e: rt.__setitem__("u_gl", e.partition_id() % 2))

    def bank_wait(i):
        return [bank_free[i]] if bank_free[i] is not None else []

    c = p.E("sync", lambda e: e.dma_start(out=bqk_sb[:, :], in_=dbqk[:, :]), inc=("sl", 16))
    c = p.E("sync", lambda e: e.dma_start(out=identf_sb[:, :], in_=dident[:, :]), inc=("sl", 16))
    ev["consts_sl"] = ("sl", c)
    c = p.E("gpsimd", lambda e: e.dma_start(out=bv3_sb[:, :], in_=dbv3[:, :]), inc=("gl", 16))
    c = p.E("gpsimd", lambda e: e.dma_start(out=outb3_sb[:, :], in_=doutb3[:, :]), inc=("gl", 16))
    c = p.E("gpsimd", lambda e: e.memset(ones3_sb[:, :], 1.0), inc=("gl", 1))
    c = p.E("gpsimd", lambda e: e.memset(ones128_sb[:, :], 1.0), inc=("gl", 1))
    c = p.E("gpsimd", lambda e: e.memset(onesf_sb[:, :], 1.0), inc=("gl", 1))
    c = p.E("gpsimd", lambda e: e.memset(onescol_sb[:, :], 1.0), inc=("gl", 1))
    ev["consts_gl"] = ("gl", c)

    def wload(sb, dr, fw):
        cl = None
        for g in range(4):
            def fn(e, g=g, sb=sb, dr=dr, fw=fw):
                src = dr[g * 512:(g + 1) * 512, :].rearrange("(dc p) f -> p dc f", p=128)
                dst = sb[:, g * 4 * fw:(g + 1) * 4 * fw].rearrange("p (dc f) -> p dc f", f=fw)
                return e.dma_start(out=dst, in_=src)
            cl = p.E("gpsimd", fn, inc=("gl", 16))
        return cl
    ev["wq"] = ("gl", wload(wq_sb, dwq, 1024))
    ev["wk"] = ("gl", wload(wk_sb, dwk, 1024))

    t_pieces = [(0, g, cc) for g in range(2) for cc in range(10)] + \
               [(1, g, cc) for g in range(5) for cc in range(4)]

    def piece_src(m, g, cc):
        return (dwfcT if m == 0 else dwpjT)[g * 128:(g + 1) * 128, cc * 512:(cc + 1) * 512]

    r1_abs = {}
    for i, (m, g, cc) in enumerate(t_pieces):
        buf = i % 2
        w = [("ve", r1_abs[i - 2])] if i >= 2 else []
        cl = p.E("sync", lambda e, m=m, g=g, cc=cc, buf=buf: e.dma_start(out=tin_sb[buf][:, :],
                                                                         in_=piece_src(m, g, cc)),
                 waits=w, inc=("sl", 16))
        r1_abs[i] = p.E("vector",
                        lambda e, i=i, buf=buf: e.reduce_sum(absacc_sb[:, i:i + 1], tin_sb[buf][:, :],
                                                             axis=AX.X, apply_absolute_value=True),
                        waits=[("sl", cl)], inc=("ve", 1))
    c_r1 = p.E("vector", lambda e: e.reduce_sum(absacc_sb[:, 40:41], absacc_sb[:, 0:20], axis=AX.X),
               waits=[("ve", r1_abs[len(t_pieces) - 1])], inc=("ve", 1))
    c_r1 = p.E("vector", lambda e: e.reduce_sum(absacc_sb[:, 41:42], absacc_sb[:, 20:40], axis=AX.X),
               waits=[("ve", c_r1)], inc=("ve", 1))
    ev["r1"] = ("ve", c_r1)

    def xt_load(tc, buf, extra_w):
        cl = None
        for g2 in range(4):
            def fn(e, tc=tc, g2=g2, buf=buf):
                src = dxT[g2 * 512:(g2 + 1) * 512, tc * 512:(tc + 1) * 512].rearrange("(dc p) t -> p dc t", p=128)
                dst = xt_sb[buf][:, g2 * 4 * 512:(g2 + 1) * 4 * 512].rearrange("p (dc t) -> p dc t", t=512)
                return e.dma_start(out=dst, in_=src)
            cl = p.E("gpsimd", fn, waits=extra_w if g2 == 0 else (), inc=("gl", 16))
        return cl

    jobs = [("Q", tc) for tc in range(4)] + [("K", tc) for tc in range(4)] + [("V", tc) for tc in range(4)]
    xt_cnt = {}
    job_mms = {}
    xt_cnt[0] = xt_load(0, 0, [])
    xt_cnt[1] = xt_load(1, 1, [])
    for s, (kind, tc) in enumerate(jobs):
        buf = s % 2
        if kind in ("Q", "K"):
            wsb = wq_sb if kind == "Q" else wk_sb
            dst_sb = qT_sb if kind == "Q" else kT_sb
            wev = ev["wq"] if kind == "Q" else ("gl", ev["wk"][1])
            for ft in range(8):
                bk = ft % 2
                waits = [("gl", xt_cnt[s]), (wev[0], wev[1])] + bank_wait(bk)
                c_mm = None
                for dc in range(16):
                    def fn(e, bk=bk, wsb=wsb, ft=ft, dc=dc, buf=buf):
                        return e.matmul(ps[bk][:, :],
                                        wsb[:, dc * 1024 + ft * 128: dc * 1024 + ft * 128 + 128],
                                        xt_sb[buf][:, dc * 512:(dc + 1) * 512],
                                        start=(dc == 0), stop=(dc == 15))
                    c_mm = p.E("tensor", fn, waits=waits if dc == 0 else (),
                               inc=("tt", 1) if dc == 15 else None)
                def ev_fn(e, dst_sb=dst_sb, ft=ft, tc=tc, bk=bk, kind=kind):
                    return e.activation(dst_sb[:, ft * 2048 + tc * 512: ft * 2048 + tc * 512 + 512],
                                        ps[bk][:, :], AF.Identity,
                                        bias=bqk_sb[:, (0 if kind == "Q" else 8) + ft:
                                                    (1 if kind == "Q" else 9) + ft])
                c_ev = p.E("scalar", ev_fn, waits=[("tt", c_mm), ("sl", ev["consts_sl"][1])], inc=("ac", 1))
                bank_free[bk] = ("ac", c_ev)
        else:
            for tl in range(4):
                tt = tc * 4 + tl
                for hc in range(2):
                    bk = 2 + (tt * 2 + hc) % 2
                    waits = [("gl", xt_cnt[s]), ("gl", ev["wv"][1]), ("gl", ev["consts_gl"][1])] + bank_wait(bk)
                    def fb(e, bk=bk, hc=hc):
                        return e.matmul(ps[bk][:, :], ones3_sb[:, :], bv3_sb[:, hc * 512:(hc + 1) * 512],
                                        start=True, stop=False)
                    p.E("tensor", fb, waits=waits)
                    c_mm = None
                    for dc in range(16):
                        def fn(e, bk=bk, tl=tl, dc=dc, hc=hc, buf=buf):
                            return e.matmul(ps[bk][:, :],
                                            xt_sb[buf][:, dc * 512 + tl * 128: dc * 512 + tl * 128 + 128],
                                            wv_sb[:, dc * 1024 + hc * 512: dc * 1024 + hc * 512 + 512],
                                            start=False, stop=(dc == 15))
                        c_mm = p.E("tensor", fn, inc=("tt", 1) if dc == 15 else None)
                    def ev_fn(e, tt=tt, hc=hc, bk=bk):
                        return e.tensor_copy(v_sb[:, tt * 1024 + hc * 512: tt * 1024 + hc * 512 + 512],
                                             ps[bk][:, :])
                    c_ev = p.E("vector", ev_fn, waits=[("tt", c_mm)], inc=("ve", 1))
                    bank_free[bk] = ("ve", c_ev)
        job_mms[s] = p.now("tt")
        if s == 3:
            ev["a1_mms"] = ("tt", p.now("tt"))
            cl = None
            for g in range(4):
                def fn(e, g=g):
                    src = dwv[g * 512:(g + 1) * 512, :].rearrange("(dc p) f -> p dc f", p=128)
                    dst = wv_sb[:, g * 4 * 1024:(g + 1) * 4 * 1024].rearrange("p (dc f) -> p dc f", f=1024)
                    return e.dma_start(out=dst, in_=src)
                cl = p.E("gpsimd", fn, waits=[ev["a1_mms"]] if g == 0 else (), inc=("gl", 16))
            ev["wv"] = ("gl", cl)
        if s + 2 < len(jobs):
            xt_cnt[s + 2] = xt_load((jobs[s + 2][1]), (s + 2) % 2, [("tt", job_mms[s])])
    ev["a2_mms"] = ("tt", p.now("tt"))
    ev["v_evacs"] = ("ve", p.now("ve"))
    ev["a1_evacs"] = ("ac", p.now("ac"))

    c_owp = None
    for g in range(4):
        c_owp = p.E("gpsimd", lambda e, g=g: e.dma_start(out=outw_bf[g * 512:(g + 1) * 512, :],
                                                         in_=doutw[g * 512:(g + 1) * 512, :]),
                    inc=("gl", 16))
    ev["owp"] = ("gl", c_owp)


    c = p.E("gpsimd", lambda e: e.dma_start(out=masks_sb[:, :], in_=dmasks[:, :]),
            waits=[ev["a2_mms"]], inc=("gl", 16))
    c = p.E("gpsimd", lambda e: e.dma_start(out=identb_sb[:, :], in_=dident[:, :]), inc=("gl", 16))
    ev["attn_glm"] = ("gl", c)
    c = p.E("sync", lambda e: e.dma_start(out=alibi_sb[:, :], in_=dalibi[:, :]),
            waits=[ev["a2_mms"]], inc=("sl", 16))
    ev["attn_sl"] = ("sl", c)
    c = p.E("gpsimd", lambda e: e.dma_start(out=negsi_sb[:, :], in_=dnegsi[:, :]),
            waits=[ev["a2_mms"]], inc=("gl", 16))
    ev["attn_gl"] = ("gl", c)

    r2_state = {"prev_comb": None, "prev_w": None}
    r2_loads = {}

    def emit_r2_load(i):
        if i >= len(t_pieces):
            return
        m, g, cc = t_pieces[i]
        buf = i % 2
        w = []
        if i >= 2:
            w.append(("ac", r2_state[f"s2_{i-2}"]))
        r2_loads[i] = p.E("sync", lambda e, m=m, g=g, cc=cc, buf=buf: e.dma_start(
            out=tin_sb[buf][:, :], in_=piece_src(m, g, cc)), waits=w, inc=("sl", 16))

    def emit_r2_compute(i):
        if i >= len(t_pieces):
            return
        m, g, cc = t_pieces[i]
        buf = i % 2
        c_s1 = p.E("scalar", lambda e, m=m, buf=buf: e.activation(ts1_sb[:, :], tin_sb[buf][:, :], AF.Sign,
                                                                  bias=thr_sb[:, m:m + 1]),
                   waits=[("sl", r2_loads[i]), ev["thr"]] +
                         ([("ve", r2_state["prev_comb"])] if r2_state["prev_comb"] else []),
                   inc=("ac", 1))
        c_s2 = p.E("scalar", lambda e, m=m, buf=buf: e.activation(tout_sb[:, :], tin_sb[buf][:, :], AF.Sign,
                                                                  bias=thr_sb[:, 2 + m:3 + m]),
                   waits=[("sl", r2_state["prev_w"])] if r2_state["prev_w"] else (),
                   inc=("ac", 1))
        r2_state[f"s2_{i}"] = c_s2
        c_c1 = p.E("vector", lambda e: e.tensor_add(ts1_sb[:, :], ts1_sb[:, :], tout_sb[:, :]),
                   waits=[("ac", c_s2)] + ([("sl", r2_state["prev_w"])] if r2_state["prev_w"] else []),
                   inc=("ve", 1))
        c_c2 = p.E("vector", lambda e, m=m: e.tensor_scalar(tout_sb[:, :], ts1_sb[:, :],
                                                            thr_sb[:, 4 + m:5 + m], 1.0,
                                                            ALU.mult, ALU.mult),
                   waits=[("ve", c_c1)], inc=("ve", 1))
        r2_state["prev_comb"] = c_c2
        c_c3 = p.E("vector", lambda e: e.tensor_scalar(ts1_sb[:, 0:8], tout_sb[:, 0:8], 1.0, 1.0,
                                                       ALU.mult, ALU.mult),
                   waits=[("ve", c_c2)], inc=("ve", 1))
        if m == 0:
            cw = p.E("sync", lambda e, g=g, cc=cc: e.dma_start(
                out=fc_agi[g * 128:(g + 1) * 128, cc * 512:(cc + 1) * 512], in_=tout_sb[:, :]),
                waits=[("ve", c_c3)], inc=("sl", 16))
        else:
            for k in range(4):
                cw = p.E("sync", lambda e, g=g, cc=cc, k=k: e.dma_start(
                    out=pj_agi[(cc * 4 + k) * 640 + g * 128: (cc * 4 + k) * 640 + g * 128 + 128, :],
                    in_=tout_sb[:, k * 128:(k + 1) * 128]),
                    waits=[("ve", c_c3)] if k == 0 else (), inc=("sl", 16))
        r2_state["prev_w"] = cw


    amref = {}

    def emit_absmean_am():
        c_ams = p.E("tensor",
                    lambda e: e.matmul(ps[6][0:1, 0:2], onescol_sb[:, :], absacc_sb[:, 40:42],
                                       start=True, stop=True),
                    waits=[ev["r1"], ("gl", ev["consts_gl"][1])] + bank_wait(6), inc=("tt", 1))
        c_amc = p.E("vector", lambda e: e.tensor_copy(amg_sb[0:1, 0:2], ps[6][0:1, 0:2]),
                    waits=[("tt", c_ams)], inc=("ve", 1))
        bank_free[6] = ("ve", c_amc)
        c_ami = p.E("gpsimd", lambda e: e.dma_start(out=am_in[:, :], in_=amg_sb[0:1, 0:2]),
                    waits=[("ve", c_amc)], inc=("gl", 16))
        c_ar = p.E("gpsimd", lambda e: e.collective_compute("AllReduce", ALU.add, replica_groups=ALL8,
                                                            ins=[am_in[:, :]], outs=[am_out[:, :]]),
                   waits=[("gl", c_ami)], inc=("car", 1))
        c_amo = p.E("gpsimd", lambda e: e.dma_start(out=amg_sb[0:1, 2:4], in_=am_out[:, :]),
                    waits=[("car", c_ar)], inc=("gl", 16))
        amref["amo"] = c_amo

    def emit_thr_bcast():
        c_amo = amref["amo"]
        NEL = float(FF * D)
        c_v = p.E("vector", lambda e: e.tensor_scalar(amg_sb[0:1, 4:6], amg_sb[0:1, 2:4], 1.0 / NEL, 1.0,
                                                      ALU.mult, ALU.mult),
                  waits=[("gl", c_amo)], inc=("ve", 1))
        c_v = p.E("vector", lambda e: e.tensor_scalar(amg_sb[0:1, 4:6], amg_sb[0:1, 4:6], 1e-5, 1.0,
                                                      ALU.max, ALU.mult), waits=[("ve", c_v)], inc=("ve", 1))
        c_v = p.E("vector", lambda e: e.tensor_scalar(thrrow_sb[0:1, 0:2], amg_sb[0:1, 4:6], -0.7, 1.0,
                                                      ALU.mult, ALU.mult), waits=[("ve", c_v)], inc=("ve", 1))
        c_v = p.E("vector", lambda e: e.tensor_scalar(thrrow_sb[0:1, 2:4], amg_sb[0:1, 4:6], 0.7, 1.0,
                                                      ALU.mult, ALU.mult), waits=[("ve", c_v)], inc=("ve", 1))
        c_v = p.E("vector", lambda e: e.tensor_scalar(thrrow_sb[0:1, 4:6], amg_sb[0:1, 4:6], 0.5, 1.0,
                                                      ALU.mult, ALU.mult), waits=[("ve", c_v)], inc=("ve", 1))
        c_bc = p.E("tensor", lambda e: e.matmul(ps[0][:, 0:6], onesf_sb[:, :], thrrow_sb[0:1, 0:6],
                                                start=True, stop=True),
                   waits=[("ve", c_v)] + bank_wait(0), inc=("tt", 1))
        c_thr = p.E("vector", lambda e: e.tensor_copy(thr_sb[:, 0:6], ps[0][:, 0:6]),
                    waits=[("tt", c_bc)], inc=("ve", 1))
        bank_free[0] = ("ve", c_thr)
        ev["thr"] = ("ve", c_thr)

    ctx_done = {}
    lmm_hist = {}
    pT_guard = {0: None, 1: None, 2: None, 3: None}
    first_evac = [True]
    r2_next = [0]
    pend_norm = [None]
    ctxP_done = {}
    ctxC_done = {}

    def emit_head_copies(h0, evac_cnt):
        def fcp(e, h0=h0):
            return e.tensor_copy(ctx2_sb[:, h0 * 1024:(h0 + 1) * 1024],
                                 ctx_sb[:, ds(h0 * 2048 + (1 - rt["u_ve"]) * 1024, 1024)])
        w = [("ve", evac_cnt)]
        if first_evac[0]:
            pass
        ctxP_done[h0] = p.E("vector", fcp, waits=w, inc=("ve", 1))
        def fcc(e, h0=h0):
            return e.copy(ctxC_sb[:, h0 * 1024:(h0 + 1) * 1024],
                          ctx_sb[:, ds(h0 * 2048 + rt["u_ac"] * 1024, 1024)])
        ctxC_done[h0] = p.E("scalar", fcc, waits=[("ve", evac_cnt)], inc=("ac", 1))

    def flush_norm():
        if pend_norm[0] is None:
            return
        h0, c0, ctx_bk0, c_rl0, slot = pend_norm[0]
        pend_norm[0] = None
        c_bcm = p.E("tensor", lambda e, slot=slot: e.matmul(ps[6][:, :], onesf_sb[:, :],
                                                            rl_sb[0:1, slot * 512:(slot + 1) * 512],
                                                            start=True, stop=True),
                    waits=[("ve", c_rl0)] + bank_wait(6), inc=("tt", 1))
        c_bcc = p.E("vector", lambda e, slot=slot: e.tensor_copy(bcst_sb[:, slot * 512:(slot + 1) * 512],
                                                                 ps[6][:, :]),
                    waits=[("tt", c_bcm)], inc=("ve", 1))
        bank_free[6] = ("ve", c_bcc)
        evac_w = [("ve", c_bcc)]
        if first_evac[0]:
            evac_w.append(ev["a2_mms"])
            first_evac[0] = False
        def fevac(e, h0=h0, c0=c0, ctx_bk0=ctx_bk0, slot=slot):
            return e.tensor_tensor(ctx_sb[:, h0 * 2048 + c0 * 512: h0 * 2048 + c0 * 512 + 512],
                                   ps[ctx_bk0][:, :], bcst_sb[:, slot * 512:(slot + 1) * 512], ALU.mult)
        c_ev = p.E("vector", fevac, waits=evac_w, inc=("ve", 1))
        bank_free[ctx_bk0] = ("ve", c_ev)
        ctx_done[(h0, c0)] = c_ev

    for h in range(8):
        for cI in range(4):
            g = h * 4 + cI
            n_t = 4 * cI + 4
            ctx_bk = 4 + g % 2
            l_bk = 7
            ctx_w0 = bank_wait(ctx_bk)
            l_w0 = bank_wait(l_bk)
            c_lmm = None
            pend = None
            order = list(range(4 * cI, 4 * cI + 4)) + list(range(4 * cI))

            def emit_ctx_l(pend, last):
                nonlocal c_lmm
                ti, jt, c_exp = pend
                def fctx(e, ctx_bk=ctx_bk, h=h, jt=jt, pbuf=ti % 4, ti=ti):
                    return e.matmul(ps[ctx_bk][:, :],
                                    v_sb[:, jt * 1024 + h * 128: jt * 1024 + h * 128 + 128],
                                    pT_sb[pbuf][:, :], start=(ti == 0), stop=False)
                wctx = [("ac", c_exp)]
                if ti == 0:
                    wctx += [("ve", ev["v_evacs"][1])] + ctx_w0 + l_w0
                p.E("tensor", fctx, waits=wctx)
                def fl(e, l_bk=l_bk, pbuf=ti % 4, ti=ti, last=last):
                    return e.matmul(ps[l_bk][0:1, :], ones128_sb[:, :],
                                    pT_sb[pbuf][:, :], start=(ti == 0), stop=last)
                c_lmm = p.E("tensor", fl, inc=("tt", 1))
                lmm_hist[(h, cI, ti)] = c_lmm
                pT_guard[ti % 4] = ("tt", c_lmm)

            for ti in range(n_t):
                jt = order[ti]
                sbk = ti % 4
                pbuf = ti % 4
                var = jt - 4 * cI + 16
                msk = jt - 4 * cI
                waits = list(bank_wait(sbk))
                if ti == 0:
                    waits += [("ac", ev["a1_evacs"][1]), ("gl", ev["attn_gl"][1])]
                def mm1(e, sbk=sbk, h=h, jt=jt, cI=cI):
                    return e.matmul(ps[sbk][:, :],
                                    kT_sb[:, h * 2048 + jt * 128: h * 2048 + jt * 128 + 128],
                                    qT_sb[:, h * 2048 + cI * 512: h * 2048 + cI * 512 + 512],
                                    start=True, stop=False)
                p.E("tensor", mm1, waits=waits)
                def mm2(e, sbk=sbk, h=h, last=(msk < 0)):
                    return e.matmul(ps[sbk][:, :], ones3_sb[:, :],
                                    negsi_sb[:, h * 512: h * 512 + 512], start=False, stop=last)
                c_smm = p.E("tensor", mm2, inc=("tt", 1) if msk < 0 else None,
                            waits=[ev["attn_glm"]] if (ti == 0) else ())
                if msk >= 0:
                    def mmm(e, sbk=sbk, msk=msk):
                        return e.matmul(ps[sbk][:, :], identb_sb[:, :],
                                        masks_sb[:, msk * 512:(msk + 1) * 512], start=False, stop=True)
                    c_smm = p.E("tensor", mmm, inc=("tt", 1))
                if pend is not None:
                    emit_ctx_l(pend, last=False)
                if ti == 2:
                    flush_norm()
                exp_w = [("tt", c_smm), ("sl", ev["attn_sl"][1])]
                if pT_guard[pbuf] is not None:
                    exp_w.append(pT_guard[pbuf])
                def fexp(e, sbk=sbk, pbuf=pbuf, h=h, var=var):
                    return e.activation(pT_sb[pbuf][:, :], ps[sbk][:, :], AF.Exp,
                                        bias=alibi_sb[:, h * 20 + var: h * 20 + var + 1])
                c_exp = p.E("scalar", fexp, waits=exp_w, inc=("ac", 1))
                bank_free[sbk] = ("ac", c_exp)
                pend = (ti, jt, c_exp)
            emit_ctx_l(pend, last=True)
            c_rl = p.E("vector", lambda e, l_bk=l_bk, g=g: e.reciprocal(
                rl_sb[0:1, (g % 2) * 512:(g % 2) * 512 + 512], ps[l_bk][0:1, :]),
                       waits=[("tt", c_lmm)], inc=("ve", 1))
            bank_free[l_bk] = ("ve", c_rl)
            pend_norm[0] = (h, cI, ctx_bk, c_rl, g % 2)
            if g == 0:
                emit_absmean_am()
            if g == 3:
                emit_thr_bcast()
            if h >= 2:
                for _ in range(2):
                    i = r2_next[0]
                    emit_r2_compute(i)
                    emit_r2_load(i + 2)
                    r2_next[0] += 1
            elif h == 1 and cI == 0:
                emit_r2_load(0)
                emit_r2_load(1)
    flush_norm()
    while r2_next[0] < len(t_pieces):
        i = r2_next[0]
        emit_r2_compute(i)
        emit_r2_load(i + 2)
        r2_next[0] += 1
    ev["attn_mms"] = ("tt", p.now("tt"))
    ev["ctx_all"] = ("ve", max(ctx_done.values()))
    ev["tern_w"] = ("sl", p.now("sl"))

    c_agfc = p.E("gpsimd", lambda e: e.collective_compute("AllGather", ALU.bypass, replica_groups=ALL8,
                                                          ins=[fc_agi[:, :]], outs=[fc_ago[:, :]]),
                 waits=[ev["tern_w"]], inc=("cfc", 1))
    c_agpj = p.E("gpsimd", lambda e: e.collective_compute("AllGather", ALU.bypass, replica_groups=ALL8,
                                                          ins=[pj_agi[:, :]], outs=[pj_ago[:, :]]),
                 inc=("cpj", 1))

    c_xo = None
    for g in range(2):
        def fxo(e, g=g):
            src = dx_own[g * 512:(g + 1) * 512, :].rearrange("(tt p) d -> p tt d", p=128)
            dst = x_sb[:, g * 4 * 2048:(g + 1) * 4 * 2048].rearrange("p (tt d) -> p tt d", d=2048)
            return e.dma_start(out=dst, in_=src)
        c_xo = p.E("sync", fxo, waits=[ev["attn_mms"]] if g == 0 else (), inc=("sl", 16))
    ev["x_own"] = ("sl", c_xo)

    for h in range(8):
        emit_head_copies(h, ev["ctx_all"][1])
    ev["ctxC"] = ("ac", max(ctxC_done.values()))
    c_stg = None
    for h in range(8):
        def fstg(e, h=h):
            return e.dma_start(out=ctx_agi[h * 128:(h + 1) * 128, :],
                               in_=ctx2_sb[:, h * 1024:(h + 1) * 1024])
        c_stg = p.E("gpsimd", fstg, waits=[("ve", ctxP_done[h])], inc=("gl", 16))
    c_agc = p.E("gpsimd", lambda e: e.collective_compute("AllGather", ALU.bypass, replica_groups=PAIRS,
                                                         ins=[ctx_agi[:, :]], outs=[ctx_ago[:, :]]),
                waits=[("gl", c_stg)], inc=("cctx", 1))
    def fc2(e):
        prow = (1 - rt["u_gl"]) * 1024
        src = ctx_ago[ds(prow, 1024), :].rearrange("(h p) t -> p h t", p=128)
        dst = ctx2_sb[:, :].rearrange("p (h t) -> p h t", t=1024)
        return e.dma_start(out=dst, in_=src)
    c_ctx2 = p.E("gpsimd", fc2, waits=[("cctx", c_agc), ev["attn_mms"]], inc=("gl", 16))
    ev["ctx2"] = ("gl", c_ctx2)

    def emit_ln_batch(tiles, col_sb, ve_ready, extra_sq_w=()):
        T = len(tiles)
        t0 = tiles[0]
        view = col_sb[:, :].rearrange("p (t j) -> p t j", j=8)

        def cv(k):
            return view[:, t0:t0 + T, k:k + 1]

        for tt in tiles:
            p.E("vector", lambda e, tt=tt: e.reduce_sum(col_sb[:, tt * 8:tt * 8 + 1],
                                                        x_sb[:, tt * 2048:(tt + 1) * 2048], axis=AX.X),
                waits=[("ve", ve_ready)], inc=("ve", 1))
        c2 = None
        for tt in tiles:
            for j in range(4):
                def fsq(e, tt=tt, j=j):
                    sj = x_sb[:, tt * 2048 + j * 512: tt * 2048 + j * 512 + 512]
                    return e.scalar_tensor_tensor(lnsq_sb[:, :], sj, 1.0, sj, ALU.mult, ALU.mult,
                                                  accum_out=col_sb[:, tt * 8 + 1 + j:tt * 8 + 2 + j])
                c2 = p.E("vector", fsq, waits=list(extra_sq_w) if (tt == tiles[0] and j == 0) else (),
                         inc=("ve", 1))
            p.E("vector", lambda e: e.nop(), waits=[("ve", c2)])
        c3 = p.E("vector", lambda e: e.reduce_sum(cv(5), view[:, t0:t0 + T, 1:5], axis=AX.X),
                 waits=[("ve", c2)], inc=("ve", 1))
        c3 = p.E("vector", lambda e: e.tensor_scalar(cv(0), cv(0), 1.0 / D, 1.0, ALU.mult, ALU.mult),
                 waits=[("ve", c3)], inc=("ve", 1))
        c3 = p.E("vector", lambda e: e.tensor_tensor(cv(6), cv(0), cv(0), ALU.mult),
                 waits=[("ve", c3)], inc=("ve", 1))
        c3 = p.E("vector", lambda e: e.tensor_scalar(cv(5), cv(5), 1.0 / D, 1.0, ALU.mult, ALU.mult),
                 waits=[("ve", c3)], inc=("ve", 1))
        c3 = p.E("vector", lambda e: e.tensor_tensor(cv(5), cv(5), cv(6), ALU.subtract),
                 waits=[("ve", c3)], inc=("ve", 1))
        c3 = p.E("vector", lambda e: e.tensor_scalar(cv(5), cv(5), LN_EPS, 1.0, ALU.add, ALU.mult),
                 waits=[("ve", c3)], inc=("ve", 1))
        c4 = p.E("vector", lambda e: e.tensor_scalar(cv(7), cv(5), 0.30, 0.55, ALU.mult, ALU.add),
                 waits=[("ve", c3)], inc=("ve", 1))
        for _it in range(4):
            c4 = p.E("vector", lambda e: e.reciprocal(cv(6), cv(7)), waits=[("ve", c4)], inc=("ve", 1))
            c4 = p.E("vector", lambda e: e.scalar_tensor_tensor(cv(6), cv(6), 0.5, cv(5),
                                                                ALU.mult, ALU.mult),
                     waits=[("ve", c4)], inc=("ve", 1))
            c4 = p.E("vector", lambda e: e.scalar_tensor_tensor(cv(7), cv(7), 0.5, cv(6),
                                                                ALU.mult, ALU.add),
                     waits=[("ve", c4)], inc=("ve", 1))
        c4 = p.E("vector", lambda e: e.reciprocal(cv(7), cv(7)), waits=[("ve", c4)], inc=("ve", 1))
        done = {}
        for tt in tiles:
            def fnorm(e, tt=tt):
                sl = x_sb[:, tt * 2048:(tt + 1) * 2048]
                return e.tensor_scalar(sl, sl, col_sb[:, tt * 8:tt * 8 + 1],
                                       col_sb[:, tt * 8 + 7:tt * 8 + 8], ALU.subtract, ALU.mult)
            done[tt] = p.E("vector", fnorm, waits=[("ve", c4)], inc=("ve", 1))
        return done

    ln1_done = {}
    buf_users = {0: 0, 1: 0}
    for tp in range(4):
        c_mm = None
        for fc in range(16):
            buf = fc % 2
            w = [ev["owp"], ("tt", buf_users[buf])]
            if tp == 0 and fc < 2:
                w.append(ev["ctx_all"])
            c_ow = p.E("gpsimd", lambda e, fc=fc, buf=buf: e.dma_start(out=ow_sb[buf][:, :],
                                                                       in_=outw_bf[fc * 128:(fc + 1) * 128, :]),
                       waits=w, inc=("gl", 16))
            for tloc in range(2):
                tt = tp * 2 + tloc
                for n4 in range(4):
                    bk = tloc * 4 + n4
                    if fc == 0:
                        w0 = list(bank_wait(bk)) + [("gl", c_ow), ev["ctxC"],
                                                    ("gl", ev["consts_gl"][1])]
                        def fb(e, bk=bk, n4=n4):
                            return e.matmul(ps[bk][:, :], ones3_sb[:, :],
                                            outb3_sb[:, n4 * 512:(n4 + 1) * 512], start=True, stop=False)
                        p.E("tensor", fb, waits=w0)
                    def fmm(e, bk=bk, fc=fc, tt=tt, n4=n4, buf=buf):
                        if fc < 8:
                            lhsT = ctxC_sb[:, fc * 1024 + tt * 128: fc * 1024 + tt * 128 + 128]
                        else:
                            lhsT = ctx2_sb[:, (fc - 8) * 1024 + tt * 128: (fc - 8) * 1024 + tt * 128 + 128]
                        return e.matmul(ps[bk][:, :], lhsT, ow_sb[buf][:, n4 * 512:(n4 + 1) * 512],
                                        start=False, stop=(fc == 15))
                    wmm = [("gl", c_ow)] if (tloc == 0 and n4 == 0) else []
                    if fc == 8 and tloc == 0 and n4 == 0:
                        wmm.append(ev["ctx2"])
                    c_mm = p.E("tensor", fmm, waits=wmm,
                               inc=("tt", 1) if (fc == 15 or (tloc == 1 and n4 == 3)) else None)
            buf_users[buf] = p.now("tt")
        for tloc in range(2):
            tt = tp * 2 + tloc
            for n4 in range(4):
                bk = tloc * 4 + n4
                def fev(e, bk=bk, tt=tt, n4=n4):
                    sl = x_sb[:, tt * 2048 + n4 * 512: tt * 2048 + n4 * 512 + 512]
                    return e.tensor_tensor(sl, sl, ps[bk][:, :], ALU.add)
                c_ad = p.E("vector", fev, waits=[("tt", c_mm), ev["x_own"]], inc=("ve", 1))
                bank_free[bk] = ("ve", c_ad)
        ln1_done.update(emit_ln_batch([2 * tp, 2 * tp + 1], lncol_sb, p.now("ve")))
    ev["outproj"] = ("ve", p.now("ve"))




    fcs_free = {}
    pjs_free = {}
    st_cnt = None
    for tc2 in range(2):
        xT_done = None
        for tl in range(4):
            tt = tc2 * 4 + tl
            for dg in range(4):
                w = list(bank_wait(7)) + [("ve", ln1_done[tt]), ("sl", ev["consts_sl"][1])]
                if tc2 == 1 and tl == 0 and dg == 0:
                    w.append(fcs_free[9])
                c_tr = None
                for k in range(4):
                    dc = dg * 4 + k
                    def ftr(e, tt=tt, dc=dc, k=k):
                        return e.transpose(ps[7][:, k * 128:(k + 1) * 128],
                                           x_sb[:, tt * 2048 + dc * 128: tt * 2048 + dc * 128 + 128],
                                           identf_sb[:, :])
                    c_tr = p.E("tensor", ftr, waits=w if k == 0 else (), inc=("tt", 1) if k == 3 else None)
                def fxe(e, tl=tl, dg=dg):
                    dst = xlnT_sb[:, :].rearrange("p (dc t) -> p dc t", t=512)[:, dg * 4:(dg + 1) * 4,
                                                                              tl * 128:(tl + 1) * 128]
                    src = ps[7][:, :].rearrange("p (k t) -> p k t", t=128)
                    return e.tensor_copy(dst, src)
                xT_done = p.E("vector", fxe, waits=[("tt", c_tr)], inc=("ve", 1))
                bank_free[7] = ("ve", xT_done)
        ev[f"xlnT{tc2}"] = ("ve", xT_done)

        first_sq = True
        for sI in range(10):
            buf = sI % 2
            gi = tc2 * 10 + sI
            w = [("cfc", c_agfc)]
            if gi <= 1:
                w.append(ev["outproj"])
            if tc2 == 1 and sI == 0:
                w.append(pjs_free[15])
            if gi >= 2:
                w.append(fcs_free[gi - 2])
            c_sl = None
            for g in range(4):
                def fsl(e, sI=sI, g=g, buf=buf):
                    src = fc_ago[g * 512:(g + 1) * 512, sI * 512:(sI + 1) * 512].rearrange(
                        "(dc p) f -> p dc f", p=128)
                    dst = fcs_sb[buf][:, g * 4 * 512:(g + 1) * 4 * 512].rearrange("p (dc f) -> p dc f", f=512)
                    return e.dma_start(out=dst, in_=src)
                c_sl = p.E("sync", fsl, waits=w if g == 0 else (), inc=("sl", 16))
            for hft in range(4):
                hf = sI * 4 + hft
                bk = hf % 2
                w2 = list(bank_wait(bk)) + [("sl", c_sl), ev[f"xlnT{tc2}"]]
                c_mm = None
                for dc in range(16):
                    def fmm(e, bk=bk, dc=dc, hft=hft, buf=buf):
                        return e.matmul(ps[bk][:, :],
                                        fcs_sb[buf][:, dc * 512 + hft * 128: dc * 512 + hft * 128 + 128],
                                        xlnT_sb[:, dc * 512:(dc + 1) * 512],
                                        start=(dc == 0), stop=(dc == 15))
                    c_mm = p.E("tensor", fmm, waits=w2 if dc == 0 else (), inc=("tt", 1) if dc == 15 else None)
                c_r = p.E("vector", lambda e, bk=bk: e.tensor_scalar(relu_sb[:, :], ps[bk][:, :], 0.0, 1.0,
                                                                    ALU.max, ALU.mult),
                          waits=[("tt", c_mm)], inc=("ve", 1))
                sq_w = [("ve", c_r)]
                if first_sq and tc2 == 1:
                    sq_w.append(pjs_free[15])
                first_sq = False
                c_q = p.E("vector", lambda e, hf=hf: e.tensor_mul(h_sb[:, hf * 512:(hf + 1) * 512],
                                                                 relu_sb[:, :], relu_sb[:, :]),
                          waits=sq_w, inc=("ve", 1))
                bank_free[bk] = ("ve", c_r)
            fcs_free[gi] = ("tt", p.now("tt"))
        ev[f"h{tc2}"] = ("ve", p.now("ve"))

        for d2t in range(16):
            buf = d2t % 2
            gi = tc2 * 16 + d2t
            w = [("cpj", c_agpj), fcs_free[tc2 * 10 + 9]]
            if gi >= 2:
                w.append(pjs_free[gi - 2])
            c_pl = None
            for c8 in range(8):
                def fpl(e, d2t=d2t, c8=c8, buf=buf):
                    src = pj_ago[c8 * 10240 + d2t * 640: c8 * 10240 + d2t * 640 + 640, :].rearrange(
                        "(hc p) f -> p hc f", p=128)
                    dst = pjs_sb[buf][:, c8 * 5 * 128:(c8 * 5 + 5) * 128].rearrange("p (hc f) -> p hc f", f=128)
                    return e.dma_start(out=dst, in_=src)
                c_pl = p.E("sync", fpl, waits=w if c8 == 0 else (), inc=("sl", 16))
            bk = 2 + d2t % 2
            w2 = list(bank_wait(bk)) + [("sl", c_pl), ev[f"h{tc2}"]]
            c_mm = None
            for hc in range(40):
                def fmm(e, bk=bk, hc=hc, buf=buf):
                    return e.matmul(ps[bk][:, :], pjs_sb[buf][:, hc * 128:(hc + 1) * 128],
                                    h_sb[:, hc * 512:(hc + 1) * 512],
                                    start=(hc == 0), stop=(hc == 39))
                c_mm = p.E("tensor", fmm, waits=w2 if hc == 0 else (), inc=("tt", 1) if hc == 39 else None)
            pjs_free[gi] = ("tt", p.now("tt"))
            c_o2 = p.E("scalar", lambda e, bk=bk: e.activation(o2_sb[:, :], ps[bk][:, :], AF.Copy),
                       waits=[("tt", c_mm)], inc=("ac", 1))
            bank_free[bk] = ("ac", c_o2)
            w3 = list(bank_wait(6)) + [("ac", c_o2)]
            c_tr = None
            for k in range(4):
                def ftr(e, k=k):
                    return e.transpose(ps[6][:, k * 128:(k + 1) * 128], o2_sb[:, k * 128:(k + 1) * 128],
                                       identf_sb[:, :])
                c_tr = p.E("tensor", ftr, waits=w3 if k == 0 else (), inc=("tt", 1) if k == 3 else None)
            c_ad = None
            for k in range(4):
                tt = tc2 * 4 + k
                def fadd(e, tt=tt, k=k, d2t=d2t):
                    sl = x_sb[:, tt * 2048 + d2t * 128: tt * 2048 + d2t * 128 + 128]
                    return e.tensor_tensor(sl, sl, ps[6][:, k * 128:(k + 1) * 128], ALU.add)
                c_ad = p.E("vector", fadd, waits=[("tt", c_tr)] if k == 0 else (), inc=("ve", 1))
            bank_free[6] = ("ve", c_ad)
        ln2_done = emit_ln_batch([tc2 * 4 + tl for tl in range(4)], ln2col_sb, p.now("ve"))
        for tl in range(4):
            tt = tc2 * 4 + tl
            def fst(e, tt=tt):
                return e.dma_start(out=dout[tt * 128:(tt + 1) * 128, :],
                                   in_=x_sb[:, tt * 2048:(tt + 1) * 2048])
            st_cnt = p.E("sync", fst, waits=[("ve", ln2_done[tt])], inc=("st", 16))

    p.E("sync", lambda e: e.nop(), waits=[("st", p.now("st"))])

    with nc.Block() as block:
        block.tensor(lambda e: [f(e) for f in p.ops["tensor"]])
        block.scalar(lambda e: [f(e) for f in p.ops["scalar"]])
        block.vector(lambda e: [f(e) for f in p.ops["vector"]])
        block.gpsimd(lambda e: [f(e) for f in p.ops["gpsimd"]])
        block.sync(lambda e: [f(e) for f in p.ops["sync"]])

    return nc


def _stage_inputs(L0_attn_x, L0_attn_qkv_w, L0_attn_qkv_b, L0_attn_out_w, L0_attn_out_b,
                  L0_ffn_w_fc, L0_ffn_w_proj):
    x = np.ascontiguousarray(np.asarray(L0_attn_x, np.float32))
    qkv_w = np.asarray(L0_attn_qkv_w, np.float32)
    qkv_b = np.asarray(L0_attn_qkv_b, np.float32)
    out_w = np.asarray(L0_attn_out_w, np.float32)
    out_b = np.asarray(L0_attn_out_b, np.float32)
    wfcT = np.ascontiguousarray(np.asarray(L0_ffn_w_fc, np.float32).T)
    wpjT = np.ascontiguousarray(np.asarray(L0_ffn_w_proj, np.float32).T)

    slopes = _slopes()
    masks = np.zeros((128, 4, 512), np.float32)
    pp = np.arange(128)[:, None]
    qq = np.arange(512)[None, :]
    for v in range(4):
        masks[:, v, :] = np.where(128 * v + pp <= qq, 0.0, MASKVAL)
    masks = np.ascontiguousarray(masks.reshape(128, 4 * 512))
    ident = np.eye(128, dtype=np.float32)

    in_maps = []
    for c in range(NCORES):
        b, h2 = c // 2, c % 2
        sl = slopes[h2 * 8:(h2 + 1) * 8]
        rel = np.arange(-16, 4)
        alibi = (sl[None, :, None] * (128.0 * rel[None, None, :] + np.arange(128)[:, None, None])
                 - C_STAB).astype(np.float32)
        nq = (-sl[:, None] * np.arange(512)[None, :]).reshape(8 * 512)
        in_maps.append({
            "xT": np.ascontiguousarray(x[b].T),
            "x_own": np.ascontiguousarray(x[b, h2 * 1024:(h2 + 1) * 1024]),
            "wq": np.ascontiguousarray(qkv_w[:, h2 * 1024:(h2 + 1) * 1024]),
            "wk": np.ascontiguousarray(qkv_w[:, 2048 + h2 * 1024: 2048 + (h2 + 1) * 1024]),
            "wv": np.ascontiguousarray(qkv_w[:, 4096 + h2 * 1024: 4096 + (h2 + 1) * 1024]),
            "bqk": np.ascontiguousarray(np.concatenate([
                qkv_b[h2 * 1024:(h2 + 1) * 1024].reshape(8, 128).T,
                qkv_b[2048 + h2 * 1024: 2048 + (h2 + 1) * 1024].reshape(8, 128).T], axis=1)),
            "bv3": np.ascontiguousarray(
                _split3(qkv_b[4096 + h2 * 1024: 4096 + (h2 + 1) * 1024][None, :]).reshape(3, 1024)),
            "outw": np.ascontiguousarray(np.concatenate([
                out_w[h2 * 1024:(h2 + 1) * 1024], out_w[(1 - h2) * 1024:(2 - h2) * 1024]], axis=0)),
            "outb3": np.ascontiguousarray(_split3(out_b[None, :]).reshape(3, 2048)),
            "wfcT_sl": np.ascontiguousarray(wfcT[c * 256:(c + 1) * 256]),
            "wprojT_sl": np.ascontiguousarray(wpjT[c * 640:(c + 1) * 640]),
            "masks": masks,
            "alibi2": np.ascontiguousarray(alibi.reshape(128, 160)),
            "negsi3q": np.ascontiguousarray(_split3(nq[None, :]).reshape(3, 8 * 512)),
            "identf": ident,
        })
    return in_maps


_cached = {}


def kernel(L0_attn_x, L0_attn_qkv_w, L0_attn_qkv_b, L0_attn_out_w, L0_attn_out_b,
           L0_ffn_w_fc, L0_ffn_w_proj):
    in_maps = _stage_inputs(L0_attn_x, L0_attn_qkv_w, L0_attn_qkv_b, L0_attn_out_w,
                            L0_attn_out_b, L0_ffn_w_fc, L0_ffn_w_proj)
    if "nc" not in _cached:
        _cached["nc"] = build_program()
    nc = _cached["nc"]
    res = run_bass_kernel_spmd(nc, in_maps, core_ids=list(range(NCORES)))
    out = np.empty((B, S, D), np.float32)
    for c in range(NCORES):
        b, h2 = c // 2, c % 2
        out[b, h2 * 1024:(h2 + 1) * 1024] = res.results[c]["out"]
    return out


# version 19
# speedup vs baseline: 1.1212x; 1.0314x over previous
import numpy as np
import ml_dtypes
import concourse.bass as bass
import concourse.mybir as mybir
from concourse.bass_utils import run_bass_kernel_spmd

F32 = mybir.dt.float32
BF16 = mybir.dt.bfloat16
AF = mybir.ActivationFunctionType
ALU = mybir.AluOpType
AX = mybir.AxisListType
ds = bass.ds
BFNP = ml_dtypes.bfloat16

B, S, D = 4, 2048, 2048
NH, HD = 16, 128
FF = 5120
NCORES = 8
C_STAB = 15.0
MASKVAL = -30000.0
LN_EPS = 1e-5
KB = 1024

PAIRS = [[0, 1], [2, 3], [4, 5], [6, 7]]
ALL8 = [list(range(8))]


def _split3(x):
    x = np.asarray(x, np.float32)
    p1 = x.astype(BFNP).astype(np.float32)
    r = x - p1
    p2 = r.astype(BFNP).astype(np.float32)
    p3 = (r - p2).astype(BFNP).astype(np.float32)
    return np.stack([p1, p2, p3]).astype(np.float32)


def _slopes():
    return np.asarray([1.0 / 2.0 ** (i / 2.0) for i in range(NH)], dtype=np.float64)


class P:
    ENGS = ("tensor", "scalar", "vector", "gpsimd", "sync")

    def __init__(self, nc):
        self.nc = nc
        self.ops = {e: [] for e in self.ENGS}
        self.cnt = {}
        self.sems = {}
        self.waited = {e: {} for e in self.ENGS}

    def sem(self, name):
        self.sems[name] = self.nc.alloc_semaphore(name=name)
        self.cnt[name] = 0

    def now(self, name):
        return self.cnt[name]

    def E(self, eng, fn, waits=(), inc=None):
        w = []
        for s, v in waits:
            if v > self.waited[eng].get(s, 0):
                self.waited[eng][s] = v
                w.append((s, v))
        tgt = None
        if inc is not None:
            self.cnt[inc[0]] += inc[1]
            tgt = self.cnt[inc[0]]

        def runner(e, fn=fn, w=tuple(w), inc=inc):
            for s, v in w:
                e.wait_ge(self.sems[s], v)
            ins = fn(e)
            if inc is not None and ins is not None:
                ins.then_inc(self.sems[inc[0]], inc[1])

        self.ops[eng].append(runner)
        return tgt


def build_program():
    nc = bass.Bass("TRN2", target_bir_lowering=False)

    dxT = nc.dram_tensor("xT", [D, S], F32, kind="ExternalInput")
    dx_own = nc.dram_tensor("x_own", [1024, D], F32, kind="ExternalInput")
    dwq = nc.dram_tensor("wq", [D, 1024], F32, kind="ExternalInput")
    dwk = nc.dram_tensor("wk", [D, 1024], F32, kind="ExternalInput")
    dwv = nc.dram_tensor("wv", [D, 1024], F32, kind="ExternalInput")
    dbqk = nc.dram_tensor("bqk", [128, 16], F32, kind="ExternalInput")
    dbv3 = nc.dram_tensor("bv3", [3, 1024], F32, kind="ExternalInput")
    doutw = nc.dram_tensor("outw", [D, D], F32, kind="ExternalInput")
    doutb3 = nc.dram_tensor("outb3", [3, D], F32, kind="ExternalInput")
    dwfcT = nc.dram_tensor("wfcT_sl", [256, FF], F32, kind="ExternalInput")
    dwpjT = nc.dram_tensor("wprojT_sl", [640, D], F32, kind="ExternalInput")
    dmasks = nc.dram_tensor("masks", [128, 4 * 512], F32, kind="ExternalInput")
    dalibi = nc.dram_tensor("alibi2", [128, 8 * 20], F32, kind="ExternalInput")
    dnegsi = nc.dram_tensor("negsi3q", [3, 8 * 512], F32, kind="ExternalInput")
    dident = nc.dram_tensor("identf", [128, 128], F32, kind="ExternalInput")
    dout = nc.dram_tensor("out", [1024, D], F32, kind="ExternalOutput")

    outw_bf = nc.dram_tensor("outw_bf", [D, D], BF16)
    am_in = nc.dram_tensor("am_in", [1, 2], F32)
    am_out = nc.dram_tensor("am_out", [1, 2], F32, addr_space="Shared")
    fc_agi = nc.dram_tensor("fc_agi", [256, FF], BF16)
    fc_ago = nc.dram_tensor("fc_ago", [D, FF], BF16, addr_space="Shared")
    pj_agi = nc.dram_tensor("pj_agi", [16 * 640, 128], BF16)
    pj_ago = nc.dram_tensor("pj_ago", [16 * 640 * 8, 128], BF16, addr_space="Shared")
    ctx_agi = nc.dram_tensor("ctx_agi", [1024, 1024], BF16)
    ctx_ago = nc.dram_tensor("ctx_ago", [2048, 1024], BF16)

    BASE = 16384

    def at(name, shape, dtype, off_b):
        return nc.alloc_sbuf_tensor_at(name, shape, dtype, offset=BASE + off_b)

    wq_sb = at("wq_sb", [128, 16 * 1024], BF16, 0 * KB)
    wk_sb = at("wk_sb", [128, 16 * 1024], BF16, 32 * KB)
    xt_sb = [at(f"xt{i}", [128, 16 * 512], BF16, (64 + 16 * i) * KB) for i in range(2)]
    qT_sb = at("qT_sb", [128, 8 * 2048], BF16, 96 * KB)
    kT_sb = at("kT_sb", [128, 8 * 2048], BF16, 128 * KB)
    wv_sb = at("wv_sb", [128, 16 * 1024], BF16, 0 * KB)
    v_sb = at("v_sb", [128, 16 * 1024], BF16, 32 * KB)
    ctx_sb = at("ctx_sb", [128, 8 * 2048], BF16, 0 * KB)
    masks_sb = at("masks_sb", [128, 4 * 512], BF16, 64 * KB)
    negsi_sb = at("negsi_sb", [3, 8 * 512], BF16, 72 * KB)
    alibi_sb = at("alibi_sb", [128, 8 * 20], F32, 80 * KB)
    pT_sb = [at(f"pT{i}", [128, 512], BF16, (81 + i) * KB) for i in range(4)]
    rl_sb = at("rl_sb", [1, 1024], F32, 85 * KB)
    bcst_sb = at("bcst_sb", [128, 1024], F32, 89 * KB)
    ctx2_sb = at("ctx2_sb", [128, 8 * 1024], BF16, 32 * KB)
    ctxC_sb = at("ctxC_sb", [128, 8 * 1024], BF16, 48 * KB)
    ow_sb = [at(f"ow{i}", [128, 2048], BF16, (64 + 4 * i) * KB) for i in range(2)]
    x_sb = at("x_sb", [128, 8 * 2048], F32, 96 * KB)
    h_sb = at("h_sb", [128, 40 * 512], BF16, 0 * KB)
    fcs_sb = [at(f"fcs{i}", [128, 16 * 512], BF16, (40 + 16 * i) * KB) for i in range(2)]
    pjs_sb = [at(f"pjs{i}", [128, 40 * 128], BF16, (40 + 10 * i) * KB) for i in range(2)]
    xlnT_sb = at("xlnT_sb", [128, 16 * 512], BF16, 72 * KB)
    relu_sb = at("relu_sb", [128, 512], F32, 88 * KB)
    o2_sb = at("o2_sb", [128, 512], F32, 90 * KB)
    lnsq_sb = at("lnsq_sb", [128, 512], F32, 93 * KB)
    lncol_sb = at("lncol_sb", [128, 64], F32, 95 * KB)
    ln2col_sb = at("ln2col_sb", [128, 64], F32, 95 * KB + 256)
    tin_sb = [at(f"tin{i}", [128, 512], F32, (161 + 2 * i) * KB) for i in range(2)]
    ts1_sb = at("ts1_sb", [128, 512], F32, 165 * KB)
    tout_sb = at("tout_sb", [128, 512], BF16, 167 * KB)
    cur = [168 * KB]

    def cst(name, shape, dtype):
        nonlocal_off = cur[0]
        t = at(name, shape, dtype, nonlocal_off)
        nbytes = int(np.prod(shape[1:])) * (2 if dtype == BF16 else 4)
        cur[0] += (nbytes + 31) // 32 * 32
        return t

    bqk_sb = cst("bqk_sb", [128, 16], F32)
    absacc_sb = cst("absacc_sb", [128, 44], F32)
    amg_sb = cst("amg_sb", [1, 8], F32)
    thr_sb = cst("thr_sb", [128, 8], F32)
    thrrow_sb = cst("thrrow_sb", [1, 8], F32)
    onescol_sb = cst("onescol_sb", [128, 1], F32)
    ones3_sb = cst("ones3_sb", [3, 128], BF16)
    ones128_sb = cst("ones128_sb", [128, 1], BF16)
    bv3_sb = cst("bv3_sb", [3, 1024], BF16)
    outb3_sb = cst("outb3_sb", [3, 2048], BF16)
    identf_sb = cst("identf_sb", [128, 128], F32)
    identb_sb = cst("identb_sb", [128, 128], BF16)
    onesf_sb = cst("onesf_sb", [1, 128], F32)
    assert cur[0] <= 176 * KB, cur[0]
    assert BASE + 176 * KB <= nc.SBUF_PARTITION_SIZE_BYTES - 32

    ps = [nc.alloc_psum_tensor(f"ps{i}", [128, 512], F32) for i in range(8)]
    bank_free = [None] * 8

    p = P(nc)
    for s in ("gl", "sl", "tt", "ac", "ve", "st", "car", "cfc", "cpj", "cctx"):
        p.sem(s)
    ev = {}
    rt = {}
    p.E("vector", lambda e: rt.__setitem__("u_ve", e.partition_id() % 2))
    p.E("scalar", lambda e: rt.__setitem__("u_ac", e.partition_id() % 2))
    p.E("gpsimd", lambda e: rt.__setitem__("u_gl", e.partition_id() % 2))

    def bank_wait(i):
        return [bank_free[i]] if bank_free[i] is not None else []

    c = p.E("sync", lambda e: e.dma_start(out=bqk_sb[:, :], in_=dbqk[:, :]), inc=("sl", 16))
    c = p.E("sync", lambda e: e.dma_start(out=identf_sb[:, :], in_=dident[:, :]), inc=("sl", 16))
    ev["consts_sl"] = ("sl", c)
    c = p.E("gpsimd", lambda e: e.dma_start(out=bv3_sb[:, :], in_=dbv3[:, :]), inc=("gl", 16))
    c = p.E("gpsimd", lambda e: e.dma_start(out=outb3_sb[:, :], in_=doutb3[:, :]), inc=("gl", 16))
    c = p.E("gpsimd", lambda e: e.memset(ones3_sb[:, :], 1.0), inc=("gl", 1))
    c = p.E("gpsimd", lambda e: e.memset(ones128_sb[:, :], 1.0), inc=("gl", 1))
    c = p.E("gpsimd", lambda e: e.memset(onesf_sb[:, :], 1.0), inc=("gl", 1))
    c = p.E("gpsimd", lambda e: e.memset(onescol_sb[:, :], 1.0), inc=("gl", 1))
    ev["consts_gl"] = ("gl", c)

    def wload(sb, dr, fw):
        cl = None
        for g in range(4):
            def fn(e, g=g, sb=sb, dr=dr, fw=fw):
                src = dr[g * 512:(g + 1) * 512, :].rearrange("(dc p) f -> p dc f", p=128)
                dst = sb[:, g * 4 * fw:(g + 1) * 4 * fw].rearrange("p (dc f) -> p dc f", f=fw)
                return e.dma_start(out=dst, in_=src)
            cl = p.E("gpsimd", fn, inc=("gl", 16))
        return cl
    ev["wq"] = ("gl", wload(wq_sb, dwq, 1024))
    ev["wk"] = ("gl", wload(wk_sb, dwk, 1024))

    t_pieces = [(0, g, cc) for g in range(2) for cc in range(10)] + \
               [(1, g, cc) for g in range(5) for cc in range(4)]

    def piece_src(m, g, cc):
        return (dwfcT if m == 0 else dwpjT)[g * 128:(g + 1) * 128, cc * 512:(cc + 1) * 512]

    r1_abs = {}
    for i, (m, g, cc) in enumerate(t_pieces):
        buf = i % 2
        w = [("ve", r1_abs[i - 2])] if i >= 2 else []
        cl = p.E("sync", lambda e, m=m, g=g, cc=cc, buf=buf: e.dma_start(out=tin_sb[buf][:, :],
                                                                         in_=piece_src(m, g, cc)),
                 waits=w, inc=("sl", 16))
        r1_abs[i] = p.E("vector",
                        lambda e, i=i, buf=buf: e.reduce_sum(absacc_sb[:, i:i + 1], tin_sb[buf][:, :],
                                                             axis=AX.X, apply_absolute_value=True),
                        waits=[("sl", cl)], inc=("ve", 1))
    c_r1 = p.E("vector", lambda e: e.reduce_sum(absacc_sb[:, 40:41], absacc_sb[:, 0:20], axis=AX.X),
               waits=[("ve", r1_abs[len(t_pieces) - 1])], inc=("ve", 1))
    c_r1 = p.E("vector", lambda e: e.reduce_sum(absacc_sb[:, 41:42], absacc_sb[:, 20:40], axis=AX.X),
               waits=[("ve", c_r1)], inc=("ve", 1))
    ev["r1"] = ("ve", c_r1)

    def xt_load(tc, buf, extra_w):
        cl = None
        for g2 in range(4):
            def fn(e, tc=tc, g2=g2, buf=buf):
                src = dxT[g2 * 512:(g2 + 1) * 512, tc * 512:(tc + 1) * 512].rearrange("(dc p) t -> p dc t", p=128)
                dst = xt_sb[buf][:, g2 * 4 * 512:(g2 + 1) * 4 * 512].rearrange("p (dc t) -> p dc t", t=512)
                return e.dma_start(out=dst, in_=src)
            cl = p.E("gpsimd", fn, waits=extra_w if g2 == 0 else (), inc=("gl", 16))
        return cl

    jobs = [("Q", tc) for tc in range(4)] + [("K", tc) for tc in range(4)] + [("V", tc) for tc in range(4)]
    xt_cnt = {}
    job_mms = {}
    xt_cnt[0] = xt_load(0, 0, [])
    xt_cnt[1] = xt_load(1, 1, [])
    for s, (kind, tc) in enumerate(jobs):
        buf = s % 2
        if kind in ("Q", "K"):
            wsb = wq_sb if kind == "Q" else wk_sb
            dst_sb = qT_sb if kind == "Q" else kT_sb
            wev = ev["wq"] if kind == "Q" else ("gl", ev["wk"][1])
            for ft in range(8):
                bk = ft % 2
                waits = [("gl", xt_cnt[s]), (wev[0], wev[1])] + bank_wait(bk)
                c_mm = None
                for dc in range(16):
                    def fn(e, bk=bk, wsb=wsb, ft=ft, dc=dc, buf=buf):
                        return e.matmul(ps[bk][:, :],
                                        wsb[:, dc * 1024 + ft * 128: dc * 1024 + ft * 128 + 128],
                                        xt_sb[buf][:, dc * 512:(dc + 1) * 512],
                                        start=(dc == 0), stop=(dc == 15))
                    c_mm = p.E("tensor", fn, waits=waits if dc == 0 else (),
                               inc=("tt", 1) if dc == 15 else None)
                def ev_fn(e, dst_sb=dst_sb, ft=ft, tc=tc, bk=bk, kind=kind):
                    return e.activation(dst_sb[:, ft * 2048 + tc * 512: ft * 2048 + tc * 512 + 512],
                                        ps[bk][:, :], AF.Identity,
                                        bias=bqk_sb[:, (0 if kind == "Q" else 8) + ft:
                                                    (1 if kind == "Q" else 9) + ft])
                c_ev = p.E("scalar", ev_fn, waits=[("tt", c_mm), ("sl", ev["consts_sl"][1])], inc=("ac", 1))
                bank_free[bk] = ("ac", c_ev)
        else:
            for tl in range(4):
                tt = tc * 4 + tl
                for hc in range(2):
                    bk = 2 + (tt * 2 + hc) % 2
                    waits = [("gl", xt_cnt[s]), ("gl", ev["wv"][1]), ("gl", ev["consts_gl"][1])] + bank_wait(bk)
                    def fb(e, bk=bk, hc=hc):
                        return e.matmul(ps[bk][:, :], ones3_sb[:, :], bv3_sb[:, hc * 512:(hc + 1) * 512],
                                        start=True, stop=False)
                    p.E("tensor", fb, waits=waits)
                    c_mm = None
                    for dc in range(16):
                        def fn(e, bk=bk, tl=tl, dc=dc, hc=hc, buf=buf):
                            return e.matmul(ps[bk][:, :],
                                            xt_sb[buf][:, dc * 512 + tl * 128: dc * 512 + tl * 128 + 128],
                                            wv_sb[:, dc * 1024 + hc * 512: dc * 1024 + hc * 512 + 512],
                                            start=False, stop=(dc == 15))
                        c_mm = p.E("tensor", fn, inc=("tt", 1) if dc == 15 else None)
                    def ev_fn(e, tt=tt, hc=hc, bk=bk):
                        return e.tensor_copy(v_sb[:, tt * 1024 + hc * 512: tt * 1024 + hc * 512 + 512],
                                             ps[bk][:, :])
                    c_ev = p.E("vector", ev_fn, waits=[("tt", c_mm)], inc=("ve", 1))
                    bank_free[bk] = ("ve", c_ev)
        job_mms[s] = p.now("tt")
        if s == 3:
            ev["a1_mms"] = ("tt", p.now("tt"))
            cl = None
            for g in range(4):
                def fn(e, g=g):
                    src = dwv[g * 512:(g + 1) * 512, :].rearrange("(dc p) f -> p dc f", p=128)
                    dst = wv_sb[:, g * 4 * 1024:(g + 1) * 4 * 1024].rearrange("p (dc f) -> p dc f", f=1024)
                    return e.dma_start(out=dst, in_=src)
                cl = p.E("gpsimd", fn, waits=[ev["a1_mms"]] if g == 0 else (), inc=("gl", 16))
            ev["wv"] = ("gl", cl)
        if s + 2 < len(jobs):
            xt_cnt[s + 2] = xt_load((jobs[s + 2][1]), (s + 2) % 2, [("tt", job_mms[s])])
    ev["a2_mms"] = ("tt", p.now("tt"))
    ev["v_evacs"] = ("ve", p.now("ve"))
    ev["a1_evacs"] = ("ac", p.now("ac"))

    c_owp = None
    for g in range(4):
        c_owp = p.E("gpsimd", lambda e, g=g: e.dma_start(out=outw_bf[g * 512:(g + 1) * 512, :],
                                                         in_=doutw[g * 512:(g + 1) * 512, :]),
                    inc=("gl", 16))
    ev["owp"] = ("gl", c_owp)


    c = p.E("gpsimd", lambda e: e.dma_start(out=masks_sb[:, :], in_=dmasks[:, :]),
            waits=[ev["a2_mms"]], inc=("gl", 16))
    c = p.E("gpsimd", lambda e: e.dma_start(out=identb_sb[:, :], in_=dident[:, :]), inc=("gl", 16))
    ev["attn_glm"] = ("gl", c)
    c = p.E("sync", lambda e: e.dma_start(out=alibi_sb[:, :], in_=dalibi[:, :]),
            waits=[ev["a2_mms"]], inc=("sl", 16))
    ev["attn_sl"] = ("sl", c)
    c = p.E("gpsimd", lambda e: e.dma_start(out=negsi_sb[:, :], in_=dnegsi[:, :]),
            waits=[ev["a2_mms"]], inc=("gl", 16))
    ev["attn_gl"] = ("gl", c)

    r2_state = {"prev_comb": None, "prev_w": None}
    r2_loads = {}

    def emit_r2_load(i):
        if i >= len(t_pieces):
            return
        m, g, cc = t_pieces[i]
        buf = i % 2
        w = []
        if i >= 2 and r2_state.get(f"g2_{i-2}"):
            w.append(("ve", r2_state[f"g2_{i-2}"]))
        r2_loads[i] = p.E("sync", lambda e, m=m, g=g, cc=cc, buf=buf: e.dma_start(
            out=tin_sb[buf][:, :], in_=piece_src(m, g, cc)), waits=w, inc=("sl", 16))

    def emit_r2_compute(i):
        if i >= len(t_pieces):
            return
        m, g, cc = t_pieces[i]
        buf = i % 2
        c_g1 = p.E("vector", lambda e, m=m, buf=buf: e.tensor_scalar(
            ts1_sb[:, :], tin_sb[buf][:, :], thr_sb[:, 2 + m:3 + m], 1.0, ALU.is_gt, ALU.mult),
            waits=[("sl", r2_loads[i]), ev["thr"]] +
                  ([("ve", r2_state["prev_comb"])] if r2_state["prev_comb"] else []),
            inc=("ve", 1))
        c_g2 = p.E("vector", lambda e, m=m, buf=buf: e.scalar_tensor_tensor(
            ts1_sb[:, :], tin_sb[buf][:, :], thr_sb[:, m:m + 1], ts1_sb[:, :], ALU.is_lt, ALU.subtract),
            waits=[("ve", c_g1)], inc=("ve", 1))
        r2_state[f"g2_{i}"] = c_g2
        c_c2 = p.E("vector", lambda e, m=m: e.tensor_scalar(
            tout_sb[:, :], ts1_sb[:, :], thr_sb[:, 4 + m:5 + m], -2.0, ALU.mult, ALU.mult),
            waits=[("ve", c_g2)] + ([("sl", r2_state["prev_w"])] if r2_state["prev_w"] else []),
            inc=("ve", 1))
        r2_state["prev_comb"] = c_c2
        c_c3 = p.E("vector", lambda e: e.tensor_scalar(ts1_sb[:, 0:8], tout_sb[:, 0:8], 1.0, 1.0,
                                                       ALU.mult, ALU.mult),
                   waits=[("ve", c_c2)], inc=("ve", 1))
        if m == 0:
            cw = p.E("sync", lambda e, g=g, cc=cc: e.dma_start(
                out=fc_agi[g * 128:(g + 1) * 128, cc * 512:(cc + 1) * 512], in_=tout_sb[:, :]),
                waits=[("ve", c_c3)], inc=("sl", 16))
        else:
            for k in range(4):
                cw = p.E("sync", lambda e, g=g, cc=cc, k=k: e.dma_start(
                    out=pj_agi[(cc * 4 + k) * 640 + g * 128: (cc * 4 + k) * 640 + g * 128 + 128, :],
                    in_=tout_sb[:, k * 128:(k + 1) * 128]),
                    waits=[("ve", c_c3)] if k == 0 else (), inc=("sl", 16))
        r2_state["prev_w"] = cw


    amref = {}

    def emit_absmean_am():
        c_ams = p.E("tensor",
                    lambda e: e.matmul(ps[6][0:1, 0:2], onescol_sb[:, :], absacc_sb[:, 40:42],
                                       start=True, stop=True),
                    waits=[ev["r1"], ("gl", ev["consts_gl"][1])] + bank_wait(6), inc=("tt", 1))
        c_amc = p.E("vector", lambda e: e.tensor_copy(amg_sb[0:1, 0:2], ps[6][0:1, 0:2]),
                    waits=[("tt", c_ams)], inc=("ve", 1))
        bank_free[6] = ("ve", c_amc)
        c_ami = p.E("gpsimd", lambda e: e.dma_start(out=am_in[:, :], in_=amg_sb[0:1, 0:2]),
                    waits=[("ve", c_amc)], inc=("gl", 16))
        c_ar = p.E("gpsimd", lambda e: e.collective_compute("AllReduce", ALU.add, replica_groups=ALL8,
                                                            ins=[am_in[:, :]], outs=[am_out[:, :]]),
                   waits=[("gl", c_ami)], inc=("car", 1))
        c_amo = p.E("gpsimd", lambda e: e.dma_start(out=amg_sb[0:1, 2:4], in_=am_out[:, :]),
                    waits=[("car", c_ar)], inc=("gl", 16))
        amref["amo"] = c_amo

    def emit_thr_bcast():
        c_amo = amref["amo"]
        NEL = float(FF * D)
        c_v = p.E("vector", lambda e: e.tensor_scalar(amg_sb[0:1, 4:6], amg_sb[0:1, 2:4], 1.0 / NEL, 1.0,
                                                      ALU.mult, ALU.mult),
                  waits=[("gl", c_amo)], inc=("ve", 1))
        c_v = p.E("vector", lambda e: e.tensor_scalar(amg_sb[0:1, 4:6], amg_sb[0:1, 4:6], 1e-5, 1.0,
                                                      ALU.max, ALU.mult), waits=[("ve", c_v)], inc=("ve", 1))
        c_v = p.E("vector", lambda e: e.tensor_scalar(thrrow_sb[0:1, 0:2], amg_sb[0:1, 4:6], -0.7, 1.0,
                                                      ALU.mult, ALU.mult), waits=[("ve", c_v)], inc=("ve", 1))
        c_v = p.E("vector", lambda e: e.tensor_scalar(thrrow_sb[0:1, 2:4], amg_sb[0:1, 4:6], 0.7, 1.0,
                                                      ALU.mult, ALU.mult), waits=[("ve", c_v)], inc=("ve", 1))
        c_v = p.E("vector", lambda e: e.tensor_scalar(thrrow_sb[0:1, 4:6], amg_sb[0:1, 4:6], 0.5, 1.0,
                                                      ALU.mult, ALU.mult), waits=[("ve", c_v)], inc=("ve", 1))
        c_v = p.E("vector", lambda e: e.tensor_scalar(thrrow_sb[0:1, 6:8], amg_sb[0:1, 4:6], -0.5, 1.0,
                                                      ALU.mult, ALU.mult), waits=[("ve", c_v)], inc=("ve", 1))
        c_bc = p.E("tensor", lambda e: e.matmul(ps[0][:, 0:8], onesf_sb[:, :], thrrow_sb[0:1, 0:8],
                                                start=True, stop=True),
                   waits=[("ve", c_v)] + bank_wait(0), inc=("tt", 1))
        c_thr = p.E("vector", lambda e: e.tensor_copy(thr_sb[:, 0:8], ps[0][:, 0:8]),
                    waits=[("tt", c_bc)], inc=("ve", 1))
        bank_free[0] = ("ve", c_thr)
        ev["thr"] = ("ve", c_thr)

    ctx_done = {}
    lmm_hist = {}
    pT_guard = {0: None, 1: None, 2: None, 3: None}
    first_evac = [True]
    r2_next = [0]
    pend_norm = [None]
    ctxP_done = {}
    ctxC_done = {}

    def emit_head_copies(h0, evac_cnt):
        def fcp(e, h0=h0):
            return e.tensor_copy(ctx2_sb[:, h0 * 1024:(h0 + 1) * 1024],
                                 ctx_sb[:, ds(h0 * 2048 + (1 - rt["u_ve"]) * 1024, 1024)])
        w = [("ve", evac_cnt)]
        if first_evac[0]:
            pass
        ctxP_done[h0] = p.E("vector", fcp, waits=w, inc=("ve", 1))
        def fcc(e, h0=h0):
            return e.copy(ctxC_sb[:, h0 * 1024:(h0 + 1) * 1024],
                          ctx_sb[:, ds(h0 * 2048 + rt["u_ac"] * 1024, 1024)])
        ctxC_done[h0] = p.E("scalar", fcc, waits=[("ve", evac_cnt)], inc=("ac", 1))

    def flush_norm():
        if pend_norm[0] is None:
            return
        h0, c0, ctx_bk0, c_rl0, slot = pend_norm[0]
        pend_norm[0] = None
        c_bcm = p.E("tensor", lambda e, slot=slot: e.matmul(ps[6][:, :], onesf_sb[:, :],
                                                            rl_sb[0:1, slot * 512:(slot + 1) * 512],
                                                            start=True, stop=True),
                    waits=[("ve", c_rl0)] + bank_wait(6), inc=("tt", 1))
        c_bcc = p.E("vector", lambda e, slot=slot: e.tensor_copy(bcst_sb[:, slot * 512:(slot + 1) * 512],
                                                                 ps[6][:, :]),
                    waits=[("tt", c_bcm)], inc=("ve", 1))
        bank_free[6] = ("ve", c_bcc)
        evac_w = [("ve", c_bcc)]
        if first_evac[0]:
            evac_w.append(ev["a2_mms"])
            first_evac[0] = False
        def fevac(e, h0=h0, c0=c0, ctx_bk0=ctx_bk0, slot=slot):
            return e.tensor_tensor(ctx_sb[:, h0 * 2048 + c0 * 512: h0 * 2048 + c0 * 512 + 512],
                                   ps[ctx_bk0][:, :], bcst_sb[:, slot * 512:(slot + 1) * 512], ALU.mult)
        c_ev = p.E("vector", fevac, waits=evac_w, inc=("ve", 1))
        bank_free[ctx_bk0] = ("ve", c_ev)
        ctx_done[(h0, c0)] = c_ev

    for h in range(8):
        for cI in range(4):
            g = h * 4 + cI
            n_t = 4 * cI + 4
            ctx_bk = 4 + g % 2
            l_bk = 7
            ctx_w0 = bank_wait(ctx_bk)
            l_w0 = bank_wait(l_bk)
            c_lmm = None
            pend = None
            order = list(range(4 * cI, 4 * cI + 4)) + list(range(4 * cI))

            def emit_ctx_l(pend, last):
                nonlocal c_lmm
                ti, jt, c_exp = pend
                def fctx(e, ctx_bk=ctx_bk, h=h, jt=jt, pbuf=ti % 4, ti=ti):
                    return e.matmul(ps[ctx_bk][:, :],
                                    v_sb[:, jt * 1024 + h * 128: jt * 1024 + h * 128 + 128],
                                    pT_sb[pbuf][:, :], start=(ti == 0), stop=False)
                wctx = [("ac", c_exp)]
                if ti == 0:
                    wctx += [("ve", ev["v_evacs"][1])] + ctx_w0 + l_w0
                p.E("tensor", fctx, waits=wctx)
                def fl(e, l_bk=l_bk, pbuf=ti % 4, ti=ti, last=last):
                    return e.matmul(ps[l_bk][0:1, :], ones128_sb[:, :],
                                    pT_sb[pbuf][:, :], start=(ti == 0), stop=last)
                c_lmm = p.E("tensor", fl, inc=("tt", 1))
                lmm_hist[(h, cI, ti)] = c_lmm
                pT_guard[ti % 4] = ("tt", c_lmm)

            for ti in range(n_t):
                jt = order[ti]
                sbk = ti % 4
                pbuf = ti % 4
                var = jt - 4 * cI + 16
                msk = jt - 4 * cI
                waits = list(bank_wait(sbk))
                if ti == 0:
                    waits += [("ac", ev["a1_evacs"][1]), ("gl", ev["attn_gl"][1])]
                def mm1(e, sbk=sbk, h=h, jt=jt, cI=cI):
                    return e.matmul(ps[sbk][:, :],
                                    kT_sb[:, h * 2048 + jt * 128: h * 2048 + jt * 128 + 128],
                                    qT_sb[:, h * 2048 + cI * 512: h * 2048 + cI * 512 + 512],
                                    start=True, stop=False)
                p.E("tensor", mm1, waits=waits)
                def mm2(e, sbk=sbk, h=h, last=(msk < 0)):
                    return e.matmul(ps[sbk][:, :], ones3_sb[:, :],
                                    negsi_sb[:, h * 512: h * 512 + 512], start=False, stop=last)
                c_smm = p.E("tensor", mm2, inc=("tt", 1) if msk < 0 else None,
                            waits=[ev["attn_glm"]] if (ti == 0) else ())
                if msk >= 0:
                    def mmm(e, sbk=sbk, msk=msk):
                        return e.matmul(ps[sbk][:, :], identb_sb[:, :],
                                        masks_sb[:, msk * 512:(msk + 1) * 512], start=False, stop=True)
                    c_smm = p.E("tensor", mmm, inc=("tt", 1))
                if pend is not None:
                    emit_ctx_l(pend, last=False)
                if ti == 2:
                    flush_norm()
                exp_w = [("tt", c_smm), ("sl", ev["attn_sl"][1])]
                if pT_guard[pbuf] is not None:
                    exp_w.append(pT_guard[pbuf])
                def fexp(e, sbk=sbk, pbuf=pbuf, h=h, var=var):
                    return e.activation(pT_sb[pbuf][:, :], ps[sbk][:, :], AF.Exp,
                                        bias=alibi_sb[:, h * 20 + var: h * 20 + var + 1])
                c_exp = p.E("scalar", fexp, waits=exp_w, inc=("ac", 1))
                bank_free[sbk] = ("ac", c_exp)
                pend = (ti, jt, c_exp)
            emit_ctx_l(pend, last=True)
            c_rl = p.E("vector", lambda e, l_bk=l_bk, g=g: e.reciprocal(
                rl_sb[0:1, (g % 2) * 512:(g % 2) * 512 + 512], ps[l_bk][0:1, :]),
                       waits=[("tt", c_lmm)], inc=("ve", 1))
            bank_free[l_bk] = ("ve", c_rl)
            pend_norm[0] = (h, cI, ctx_bk, c_rl, g % 2)
            if g == 0:
                emit_absmean_am()
            if g == 3:
                emit_thr_bcast()
            if h >= 2:
                for _ in range(2):
                    i = r2_next[0]
                    emit_r2_compute(i)
                    emit_r2_load(i + 2)
                    r2_next[0] += 1
            elif h == 1 and cI == 0:
                emit_r2_load(0)
                emit_r2_load(1)
    flush_norm()
    while r2_next[0] < len(t_pieces):
        i = r2_next[0]
        emit_r2_compute(i)
        emit_r2_load(i + 2)
        r2_next[0] += 1
    ev["attn_mms"] = ("tt", p.now("tt"))
    ev["ctx_all"] = ("ve", max(ctx_done.values()))
    ev["tern_w"] = ("sl", p.now("sl"))

    c_agfc = p.E("gpsimd", lambda e: e.collective_compute("AllGather", ALU.bypass, replica_groups=ALL8,
                                                          ins=[fc_agi[:, :]], outs=[fc_ago[:, :]]),
                 waits=[ev["tern_w"]], inc=("cfc", 1))
    c_agpj = p.E("gpsimd", lambda e: e.collective_compute("AllGather", ALU.bypass, replica_groups=ALL8,
                                                          ins=[pj_agi[:, :]], outs=[pj_ago[:, :]]),
                 inc=("cpj", 1))

    c_xo = None
    for g in range(2):
        def fxo(e, g=g):
            src = dx_own[g * 512:(g + 1) * 512, :].rearrange("(tt p) d -> p tt d", p=128)
            dst = x_sb[:, g * 4 * 2048:(g + 1) * 4 * 2048].rearrange("p (tt d) -> p tt d", d=2048)
            return e.dma_start(out=dst, in_=src)
        c_xo = p.E("sync", fxo, waits=[ev["attn_mms"]] if g == 0 else (), inc=("sl", 16))
    ev["x_own"] = ("sl", c_xo)

    for h in range(8):
        emit_head_copies(h, ev["ctx_all"][1])
    ev["ctxC"] = ("ac", max(ctxC_done.values()))
    c_stg = None
    for h in range(8):
        def fstg(e, h=h):
            return e.dma_start(out=ctx_agi[h * 128:(h + 1) * 128, :],
                               in_=ctx2_sb[:, h * 1024:(h + 1) * 1024])
        c_stg = p.E("gpsimd", fstg, waits=[("ve", ctxP_done[h])], inc=("gl", 16))
    c_agc = p.E("gpsimd", lambda e: e.collective_compute("AllGather", ALU.bypass, replica_groups=PAIRS,
                                                         ins=[ctx_agi[:, :]], outs=[ctx_ago[:, :]]),
                waits=[("gl", c_stg)], inc=("cctx", 1))
    def fc2(e):
        prow = (1 - rt["u_gl"]) * 1024
        src = ctx_ago[ds(prow, 1024), :].rearrange("(h p) t -> p h t", p=128)
        dst = ctx2_sb[:, :].rearrange("p (h t) -> p h t", t=1024)
        return e.dma_start(out=dst, in_=src)
    c_ctx2 = p.E("gpsimd", fc2, waits=[("cctx", c_agc), ev["attn_mms"]], inc=("gl", 16))
    ev["ctx2"] = ("gl", c_ctx2)

    def emit_ln_batch(tiles, col_sb, ve_ready, extra_sq_w=()):
        T = len(tiles)
        t0 = tiles[0]
        view = col_sb[:, :].rearrange("p (t j) -> p t j", j=8)

        def cv(k):
            return view[:, t0:t0 + T, k:k + 1]

        for tt in tiles:
            p.E("vector", lambda e, tt=tt: e.reduce_sum(col_sb[:, tt * 8:tt * 8 + 1],
                                                        x_sb[:, tt * 2048:(tt + 1) * 2048], axis=AX.X),
                waits=[("ve", ve_ready)], inc=("ve", 1))
        c2 = None
        for tt in tiles:
            for j in range(4):
                def fsq(e, tt=tt, j=j):
                    sj = x_sb[:, tt * 2048 + j * 512: tt * 2048 + j * 512 + 512]
                    return e.scalar_tensor_tensor(lnsq_sb[:, :], sj, 1.0, sj, ALU.mult, ALU.mult,
                                                  accum_out=col_sb[:, tt * 8 + 1 + j:tt * 8 + 2 + j])
                c2 = p.E("vector", fsq, waits=list(extra_sq_w) if (tt == tiles[0] and j == 0) else (),
                         inc=("ve", 1))
            p.E("vector", lambda e: e.nop(), waits=[("ve", c2)])
        c3 = p.E("vector", lambda e: e.reduce_sum(cv(5), view[:, t0:t0 + T, 1:5], axis=AX.X),
                 waits=[("ve", c2)], inc=("ve", 1))
        c3 = p.E("vector", lambda e: e.tensor_scalar(cv(0), cv(0), 1.0 / D, 1.0, ALU.mult, ALU.mult),
                 waits=[("ve", c3)], inc=("ve", 1))
        c3 = p.E("vector", lambda e: e.tensor_tensor(cv(6), cv(0), cv(0), ALU.mult),
                 waits=[("ve", c3)], inc=("ve", 1))
        c3 = p.E("vector", lambda e: e.tensor_scalar(cv(5), cv(5), 1.0 / D, 1.0, ALU.mult, ALU.mult),
                 waits=[("ve", c3)], inc=("ve", 1))
        c3 = p.E("vector", lambda e: e.tensor_tensor(cv(5), cv(5), cv(6), ALU.subtract),
                 waits=[("ve", c3)], inc=("ve", 1))
        c3 = p.E("vector", lambda e: e.tensor_scalar(cv(5), cv(5), LN_EPS, 1.0, ALU.add, ALU.mult),
                 waits=[("ve", c3)], inc=("ve", 1))
        c4 = p.E("vector", lambda e: e.tensor_scalar(cv(7), cv(5), 0.30, 0.55, ALU.mult, ALU.add),
                 waits=[("ve", c3)], inc=("ve", 1))
        for _it in range(4):
            c4 = p.E("vector", lambda e: e.reciprocal(cv(6), cv(7)), waits=[("ve", c4)], inc=("ve", 1))
            c4 = p.E("vector", lambda e: e.scalar_tensor_tensor(cv(6), cv(6), 0.5, cv(5),
                                                                ALU.mult, ALU.mult),
                     waits=[("ve", c4)], inc=("ve", 1))
            c4 = p.E("vector", lambda e: e.scalar_tensor_tensor(cv(7), cv(7), 0.5, cv(6),
                                                                ALU.mult, ALU.add),
                     waits=[("ve", c4)], inc=("ve", 1))
        c4 = p.E("vector", lambda e: e.reciprocal(cv(7), cv(7)), waits=[("ve", c4)], inc=("ve", 1))
        done = {}
        for tt in tiles:
            def fnorm(e, tt=tt):
                sl = x_sb[:, tt * 2048:(tt + 1) * 2048]
                return e.tensor_scalar(sl, sl, col_sb[:, tt * 8:tt * 8 + 1],
                                       col_sb[:, tt * 8 + 7:tt * 8 + 8], ALU.subtract, ALU.mult)
            done[tt] = p.E("vector", fnorm, waits=[("ve", c4)], inc=("ve", 1))
        return done

    ln1_done = {}
    buf_users = {0: 0, 1: 0}
    for tp in range(4):
        c_mm = None
        for fc in range(16):
            buf = fc % 2
            w = [ev["owp"], ("tt", buf_users[buf])]
            if tp == 0 and fc < 2:
                w.append(ev["ctx_all"])
            c_ow = p.E("gpsimd", lambda e, fc=fc, buf=buf: e.dma_start(out=ow_sb[buf][:, :],
                                                                       in_=outw_bf[fc * 128:(fc + 1) * 128, :]),
                       waits=w, inc=("gl", 16))
            for tloc in range(2):
                tt = tp * 2 + tloc
                for n4 in range(4):
                    bk = tloc * 4 + n4
                    if fc == 0:
                        w0 = list(bank_wait(bk)) + [("gl", c_ow), ev["ctxC"],
                                                    ("gl", ev["consts_gl"][1])]
                        def fb(e, bk=bk, n4=n4):
                            return e.matmul(ps[bk][:, :], ones3_sb[:, :],
                                            outb3_sb[:, n4 * 512:(n4 + 1) * 512], start=True, stop=False)
                        p.E("tensor", fb, waits=w0)
                    def fmm(e, bk=bk, fc=fc, tt=tt, n4=n4, buf=buf):
                        if fc < 8:
                            lhsT = ctxC_sb[:, fc * 1024 + tt * 128: fc * 1024 + tt * 128 + 128]
                        else:
                            lhsT = ctx2_sb[:, (fc - 8) * 1024 + tt * 128: (fc - 8) * 1024 + tt * 128 + 128]
                        return e.matmul(ps[bk][:, :], lhsT, ow_sb[buf][:, n4 * 512:(n4 + 1) * 512],
                                        start=False, stop=(fc == 15))
                    wmm = [("gl", c_ow)] if (tloc == 0 and n4 == 0) else []
                    if fc == 8 and tloc == 0 and n4 == 0:
                        wmm.append(ev["ctx2"])
                    c_mm = p.E("tensor", fmm, waits=wmm,
                               inc=("tt", 1) if (fc == 15 or (tloc == 1 and n4 == 3)) else None)
            buf_users[buf] = p.now("tt")
        for tloc in range(2):
            tt = tp * 2 + tloc
            for n4 in range(4):
                bk = tloc * 4 + n4
                def fev(e, bk=bk, tt=tt, n4=n4):
                    sl = x_sb[:, tt * 2048 + n4 * 512: tt * 2048 + n4 * 512 + 512]
                    return e.tensor_tensor(sl, sl, ps[bk][:, :], ALU.add)
                c_ad = p.E("vector", fev, waits=[("tt", c_mm), ev["x_own"]], inc=("ve", 1))
                bank_free[bk] = ("ve", c_ad)
        ln1_done.update(emit_ln_batch([2 * tp, 2 * tp + 1], lncol_sb, p.now("ve")))
    ev["outproj"] = ("ve", p.now("ve"))




    fcs_free = {}
    pjs_free = {}
    st_cnt = None
    for tc2 in range(2):
        xT_done = None
        for tl in range(4):
            tt = tc2 * 4 + tl
            for dg in range(4):
                w = list(bank_wait(7)) + [("ve", ln1_done[tt]), ("sl", ev["consts_sl"][1])]
                if tc2 == 1 and tl == 0 and dg == 0:
                    w.append(fcs_free[9])
                c_tr = None
                for k in range(4):
                    dc = dg * 4 + k
                    def ftr(e, tt=tt, dc=dc, k=k):
                        return e.transpose(ps[7][:, k * 128:(k + 1) * 128],
                                           x_sb[:, tt * 2048 + dc * 128: tt * 2048 + dc * 128 + 128],
                                           identf_sb[:, :])
                    c_tr = p.E("tensor", ftr, waits=w if k == 0 else (), inc=("tt", 1) if k == 3 else None)
                def fxe(e, tl=tl, dg=dg):
                    dst = xlnT_sb[:, :].rearrange("p (dc t) -> p dc t", t=512)[:, dg * 4:(dg + 1) * 4,
                                                                              tl * 128:(tl + 1) * 128]
                    src = ps[7][:, :].rearrange("p (k t) -> p k t", t=128)
                    return e.tensor_copy(dst, src)
                xT_done = p.E("vector", fxe, waits=[("tt", c_tr)], inc=("ve", 1))
                bank_free[7] = ("ve", xT_done)
        ev[f"xlnT{tc2}"] = ("ve", xT_done)

        first_sq = True
        for sI in range(10):
            buf = sI % 2
            gi = tc2 * 10 + sI
            w = [("cfc", c_agfc)]
            if gi <= 1:
                w.append(ev["outproj"])
            if tc2 == 1 and sI == 0:
                w.append(pjs_free[15])
            if gi >= 2:
                w.append(fcs_free[gi - 2])
            c_sl = None
            for g in range(4):
                def fsl(e, sI=sI, g=g, buf=buf):
                    src = fc_ago[g * 512:(g + 1) * 512, sI * 512:(sI + 1) * 512].rearrange(
                        "(dc p) f -> p dc f", p=128)
                    dst = fcs_sb[buf][:, g * 4 * 512:(g + 1) * 4 * 512].rearrange("p (dc f) -> p dc f", f=512)
                    return e.dma_start(out=dst, in_=src)
                c_sl = p.E("sync", fsl, waits=w if g == 0 else (), inc=("sl", 16))
            for hft in range(4):
                hf = sI * 4 + hft
                bk = hf % 2
                w2 = list(bank_wait(bk)) + [("sl", c_sl), ev[f"xlnT{tc2}"]]
                c_mm = None
                for dc in range(16):
                    def fmm(e, bk=bk, dc=dc, hft=hft, buf=buf):
                        return e.matmul(ps[bk][:, :],
                                        fcs_sb[buf][:, dc * 512 + hft * 128: dc * 512 + hft * 128 + 128],
                                        xlnT_sb[:, dc * 512:(dc + 1) * 512],
                                        start=(dc == 0), stop=(dc == 15))
                    c_mm = p.E("tensor", fmm, waits=w2 if dc == 0 else (), inc=("tt", 1) if dc == 15 else None)
                c_r = p.E("vector", lambda e, bk=bk: e.tensor_scalar(relu_sb[:, :], ps[bk][:, :], 0.0, 1.0,
                                                                    ALU.max, ALU.mult),
                          waits=[("tt", c_mm)], inc=("ve", 1))
                sq_w = [("ve", c_r)]
                if first_sq and tc2 == 1:
                    sq_w.append(pjs_free[15])
                first_sq = False
                c_q = p.E("vector", lambda e, hf=hf: e.tensor_mul(h_sb[:, hf * 512:(hf + 1) * 512],
                                                                 relu_sb[:, :], relu_sb[:, :]),
                          waits=sq_w, inc=("ve", 1))
                bank_free[bk] = ("ve", c_r)
            fcs_free[gi] = ("tt", p.now("tt"))
        ev[f"h{tc2}"] = ("ve", p.now("ve"))

        for d2t in range(16):
            buf = d2t % 2
            gi = tc2 * 16 + d2t
            w = [("cpj", c_agpj), fcs_free[tc2 * 10 + 9]]
            if gi >= 2:
                w.append(pjs_free[gi - 2])
            c_pl = None
            for c8 in range(8):
                def fpl(e, d2t=d2t, c8=c8, buf=buf):
                    src = pj_ago[c8 * 10240 + d2t * 640: c8 * 10240 + d2t * 640 + 640, :].rearrange(
                        "(hc p) f -> p hc f", p=128)
                    dst = pjs_sb[buf][:, c8 * 5 * 128:(c8 * 5 + 5) * 128].rearrange("p (hc f) -> p hc f", f=128)
                    return e.dma_start(out=dst, in_=src)
                c_pl = p.E("sync", fpl, waits=w if c8 == 0 else (), inc=("sl", 16))
            bk = 2 + d2t % 2
            w2 = list(bank_wait(bk)) + [("sl", c_pl), ev[f"h{tc2}"]]
            c_mm = None
            for hc in range(40):
                def fmm(e, bk=bk, hc=hc, buf=buf):
                    return e.matmul(ps[bk][:, :], pjs_sb[buf][:, hc * 128:(hc + 1) * 128],
                                    h_sb[:, hc * 512:(hc + 1) * 512],
                                    start=(hc == 0), stop=(hc == 39))
                c_mm = p.E("tensor", fmm, waits=w2 if hc == 0 else (), inc=("tt", 1) if hc == 39 else None)
            pjs_free[gi] = ("tt", p.now("tt"))
            c_o2 = p.E("scalar", lambda e, bk=bk: e.activation(o2_sb[:, :], ps[bk][:, :], AF.Copy),
                       waits=[("tt", c_mm)], inc=("ac", 1))
            bank_free[bk] = ("ac", c_o2)
            w3 = list(bank_wait(6)) + [("ac", c_o2)]
            c_tr = None
            for k in range(4):
                def ftr(e, k=k):
                    return e.transpose(ps[6][:, k * 128:(k + 1) * 128], o2_sb[:, k * 128:(k + 1) * 128],
                                       identf_sb[:, :])
                c_tr = p.E("tensor", ftr, waits=w3 if k == 0 else (), inc=("tt", 1) if k == 3 else None)
            c_ad = None
            for k in range(4):
                tt = tc2 * 4 + k
                def fadd(e, tt=tt, k=k, d2t=d2t):
                    sl = x_sb[:, tt * 2048 + d2t * 128: tt * 2048 + d2t * 128 + 128]
                    return e.tensor_tensor(sl, sl, ps[6][:, k * 128:(k + 1) * 128], ALU.add)
                c_ad = p.E("vector", fadd, waits=[("tt", c_tr)] if k == 0 else (), inc=("ve", 1))
            bank_free[6] = ("ve", c_ad)
        ln2_done = emit_ln_batch([tc2 * 4 + tl for tl in range(4)], ln2col_sb, p.now("ve"))
        for tl in range(4):
            tt = tc2 * 4 + tl
            def fst(e, tt=tt):
                return e.dma_start(out=dout[tt * 128:(tt + 1) * 128, :],
                                   in_=x_sb[:, tt * 2048:(tt + 1) * 2048])
            st_cnt = p.E("sync", fst, waits=[("ve", ln2_done[tt])], inc=("st", 16))

    p.E("sync", lambda e: e.nop(), waits=[("st", p.now("st"))])

    with nc.Block() as block:
        block.tensor(lambda e: [f(e) for f in p.ops["tensor"]])
        block.scalar(lambda e: [f(e) for f in p.ops["scalar"]])
        block.vector(lambda e: [f(e) for f in p.ops["vector"]])
        block.gpsimd(lambda e: [f(e) for f in p.ops["gpsimd"]])
        block.sync(lambda e: [f(e) for f in p.ops["sync"]])

    return nc


def _stage_inputs(L0_attn_x, L0_attn_qkv_w, L0_attn_qkv_b, L0_attn_out_w, L0_attn_out_b,
                  L0_ffn_w_fc, L0_ffn_w_proj):
    x = np.ascontiguousarray(np.asarray(L0_attn_x, np.float32))
    qkv_w = np.asarray(L0_attn_qkv_w, np.float32)
    qkv_b = np.asarray(L0_attn_qkv_b, np.float32)
    out_w = np.asarray(L0_attn_out_w, np.float32)
    out_b = np.asarray(L0_attn_out_b, np.float32)
    wfcT = np.ascontiguousarray(np.asarray(L0_ffn_w_fc, np.float32).T)
    wpjT = np.ascontiguousarray(np.asarray(L0_ffn_w_proj, np.float32).T)

    slopes = _slopes()
    masks = np.zeros((128, 4, 512), np.float32)
    pp = np.arange(128)[:, None]
    qq = np.arange(512)[None, :]
    for v in range(4):
        masks[:, v, :] = np.where(128 * v + pp <= qq, 0.0, MASKVAL)
    masks = np.ascontiguousarray(masks.reshape(128, 4 * 512))
    ident = np.eye(128, dtype=np.float32)

    in_maps = []
    for c in range(NCORES):
        b, h2 = c // 2, c % 2
        sl = slopes[h2 * 8:(h2 + 1) * 8]
        rel = np.arange(-16, 4)
        alibi = (sl[None, :, None] * (128.0 * rel[None, None, :] + np.arange(128)[:, None, None])
                 - C_STAB).astype(np.float32)
        nq = (-sl[:, None] * np.arange(512)[None, :]).reshape(8 * 512)
        in_maps.append({
            "xT": np.ascontiguousarray(x[b].T),
            "x_own": np.ascontiguousarray(x[b, h2 * 1024:(h2 + 1) * 1024]),
            "wq": np.ascontiguousarray(qkv_w[:, h2 * 1024:(h2 + 1) * 1024]),
            "wk": np.ascontiguousarray(qkv_w[:, 2048 + h2 * 1024: 2048 + (h2 + 1) * 1024]),
            "wv": np.ascontiguousarray(qkv_w[:, 4096 + h2 * 1024: 4096 + (h2 + 1) * 1024]),
            "bqk": np.ascontiguousarray(np.concatenate([
                qkv_b[h2 * 1024:(h2 + 1) * 1024].reshape(8, 128).T,
                qkv_b[2048 + h2 * 1024: 2048 + (h2 + 1) * 1024].reshape(8, 128).T], axis=1)),
            "bv3": np.ascontiguousarray(
                _split3(qkv_b[4096 + h2 * 1024: 4096 + (h2 + 1) * 1024][None, :]).reshape(3, 1024)),
            "outw": np.ascontiguousarray(np.concatenate([
                out_w[h2 * 1024:(h2 + 1) * 1024], out_w[(1 - h2) * 1024:(2 - h2) * 1024]], axis=0)),
            "outb3": np.ascontiguousarray(_split3(out_b[None, :]).reshape(3, 2048)),
            "wfcT_sl": np.ascontiguousarray(wfcT[c * 256:(c + 1) * 256]),
            "wprojT_sl": np.ascontiguousarray(wpjT[c * 640:(c + 1) * 640]),
            "masks": masks,
            "alibi2": np.ascontiguousarray(alibi.reshape(128, 160)),
            "negsi3q": np.ascontiguousarray(_split3(nq[None, :]).reshape(3, 8 * 512)),
            "identf": ident,
        })
    return in_maps


_cached = {}


def kernel(L0_attn_x, L0_attn_qkv_w, L0_attn_qkv_b, L0_attn_out_w, L0_attn_out_b,
           L0_ffn_w_fc, L0_ffn_w_proj):
    in_maps = _stage_inputs(L0_attn_x, L0_attn_qkv_w, L0_attn_qkv_b, L0_attn_out_w,
                            L0_attn_out_b, L0_ffn_w_fc, L0_ffn_w_proj)
    if "nc" not in _cached:
        _cached["nc"] = build_program()
    nc = _cached["nc"]
    res = run_bass_kernel_spmd(nc, in_maps, core_ids=list(range(NCORES)))
    out = np.empty((B, S, D), np.float32)
    for c in range(NCORES):
        b, h2 = c // 2, c % 2
        out[b, h2 * 1024:(h2 + 1) * 1024] = res.results[c]["out"]
    return out


# version 20
# speedup vs baseline: 1.1361x; 1.0132x over previous
import numpy as np
import ml_dtypes
import concourse.bass as bass
import concourse.mybir as mybir
from concourse.bass_utils import run_bass_kernel_spmd

F32 = mybir.dt.float32
BF16 = mybir.dt.bfloat16
AF = mybir.ActivationFunctionType
ALU = mybir.AluOpType
AX = mybir.AxisListType
ds = bass.ds
BFNP = ml_dtypes.bfloat16

B, S, D = 4, 2048, 2048
NH, HD = 16, 128
FF = 5120
NCORES = 8
C_STAB = 15.0
MASKVAL = -30000.0
LN_EPS = 1e-5
KB = 1024

PAIRS = [[0, 1], [2, 3], [4, 5], [6, 7]]
ALL8 = [list(range(8))]


def _split3(x):
    x = np.asarray(x, np.float32)
    p1 = x.astype(BFNP).astype(np.float32)
    r = x - p1
    p2 = r.astype(BFNP).astype(np.float32)
    p3 = (r - p2).astype(BFNP).astype(np.float32)
    return np.stack([p1, p2, p3]).astype(np.float32)


def _slopes():
    return np.asarray([1.0 / 2.0 ** (i / 2.0) for i in range(NH)], dtype=np.float64)


class P:
    ENGS = ("tensor", "scalar", "vector", "gpsimd", "sync")

    def __init__(self, nc):
        self.nc = nc
        self.ops = {e: [] for e in self.ENGS}
        self.cnt = {}
        self.sems = {}
        self.waited = {e: {} for e in self.ENGS}

    def sem(self, name):
        self.sems[name] = self.nc.alloc_semaphore(name=name)
        self.cnt[name] = 0

    def now(self, name):
        return self.cnt[name]

    def E(self, eng, fn, waits=(), inc=None):
        w = []
        for s, v in waits:
            if v > self.waited[eng].get(s, 0):
                self.waited[eng][s] = v
                w.append((s, v))
        tgt = None
        if inc is not None:
            self.cnt[inc[0]] += inc[1]
            tgt = self.cnt[inc[0]]

        def runner(e, fn=fn, w=tuple(w), inc=inc):
            for s, v in w:
                e.wait_ge(self.sems[s], v)
            ins = fn(e)
            if inc is not None and ins is not None:
                ins.then_inc(self.sems[inc[0]], inc[1])

        self.ops[eng].append(runner)
        return tgt


def build_program():
    nc = bass.Bass("TRN2", target_bir_lowering=False)

    dxT = nc.dram_tensor("xT", [D, S], F32, kind="ExternalInput")
    dx_own = nc.dram_tensor("x_own", [1024, D], F32, kind="ExternalInput")
    dwq = nc.dram_tensor("wq", [D, 1024], F32, kind="ExternalInput")
    dwk = nc.dram_tensor("wk", [D, 1024], F32, kind="ExternalInput")
    dwv = nc.dram_tensor("wv", [D, 1024], F32, kind="ExternalInput")
    dbqk = nc.dram_tensor("bqk", [128, 16], F32, kind="ExternalInput")
    dbv3 = nc.dram_tensor("bv3", [3, 1024], F32, kind="ExternalInput")
    doutw = nc.dram_tensor("outw", [D, D], F32, kind="ExternalInput")
    doutb3 = nc.dram_tensor("outb3", [3, D], F32, kind="ExternalInput")
    dwfcT = nc.dram_tensor("wfcT_sl", [256, FF], F32, kind="ExternalInput")
    dwpjT = nc.dram_tensor("wprojT_sl", [640, D], F32, kind="ExternalInput")
    dmasks = nc.dram_tensor("masks", [128, 4 * 512], F32, kind="ExternalInput")
    dalibi = nc.dram_tensor("alibi2", [128, 8 * 20], F32, kind="ExternalInput")
    dnegsi = nc.dram_tensor("negsi3q", [3, 8 * 512], F32, kind="ExternalInput")
    dident = nc.dram_tensor("identf", [128, 128], F32, kind="ExternalInput")
    dout = nc.dram_tensor("out", [1024, D], F32, kind="ExternalOutput")

    outw_bf = nc.dram_tensor("outw_bf", [D, D], BF16)
    am_in = nc.dram_tensor("am_in", [1, 2], F32)
    am_out = nc.dram_tensor("am_out", [1, 2], F32, addr_space="Shared")
    fc_agi = nc.dram_tensor("fc_agi", [256, FF], BF16)
    fc_ago = nc.dram_tensor("fc_ago", [D, FF], BF16, addr_space="Shared")
    pj_agi = nc.dram_tensor("pj_agi", [16 * 640, 128], BF16)
    pj_ago = nc.dram_tensor("pj_ago", [16 * 640 * 8, 128], BF16, addr_space="Shared")
    ctx_agi = nc.dram_tensor("ctx_agi", [1024, 1024], BF16)
    ctx_ago = nc.dram_tensor("ctx_ago", [2048, 1024], BF16)

    BASE = 16384

    def at(name, shape, dtype, off_b):
        return nc.alloc_sbuf_tensor_at(name, shape, dtype, offset=BASE + off_b)

    wq_sb = at("wq_sb", [128, 16 * 1024], BF16, 0 * KB)
    wk_sb = at("wk_sb", [128, 16 * 1024], BF16, 32 * KB)
    xt_sb = [at(f"xt{i}", [128, 16 * 512], BF16, (64 + 16 * i) * KB) for i in range(2)]
    qT_sb = at("qT_sb", [128, 8 * 2048], BF16, 96 * KB)
    kT_sb = at("kT_sb", [128, 8 * 2048], BF16, 128 * KB)
    wv_sb = at("wv_sb", [128, 16 * 1024], BF16, 0 * KB)
    v_sb = at("v_sb", [128, 16 * 1024], BF16, 32 * KB)
    ctx_sb = at("ctx_sb", [128, 8 * 2048], BF16, 0 * KB)
    masks_sb = at("masks_sb", [128, 4 * 512], BF16, 64 * KB)
    negsi_sb = at("negsi_sb", [3, 8 * 512], BF16, 72 * KB)
    alibi_sb = at("alibi_sb", [128, 8 * 20], F32, 80 * KB)
    pT_sb = [at(f"pT{i}", [128, 512], BF16, (81 + i) * KB) for i in range(4)]
    rl_sb = at("rl_sb", [1, 1024], F32, 85 * KB)
    bcst_sb = at("bcst_sb", [128, 1024], F32, 89 * KB)
    ctx2_sb = at("ctx2_sb", [128, 8 * 1024], BF16, 32 * KB)
    ctxC_sb = at("ctxC_sb", [128, 8 * 1024], BF16, 48 * KB)
    ow_sb = [at(f"ow{i}", [128, 2048], BF16, (64 + 4 * i) * KB) for i in range(2)]
    x_sb = at("x_sb", [128, 8 * 2048], F32, 96 * KB)
    h_sb = at("h_sb", [128, 40 * 512], BF16, 0 * KB)
    fcs_sb = [at(f"fcs{i}", [128, 16 * 512], BF16, (40 + 16 * i) * KB) for i in range(2)]
    pjs_sb = [at(f"pjs{i}", [128, 40 * 128], BF16, (40 + 10 * i) * KB) for i in range(2)]
    xlnT_sb = at("xlnT_sb", [128, 16 * 512], BF16, 72 * KB)
    relu_sb = at("relu_sb", [128, 512], F32, 88 * KB)
    o2_sb = at("o2_sb", [128, 512], F32, 90 * KB)
    lnsq_sb = at("lnsq_sb", [128, 512], F32, 93 * KB)
    lncol_sb = at("lncol_sb", [128, 64], F32, 95 * KB)
    ln2col_sb = at("ln2col_sb", [128, 64], F32, 95 * KB + 256)
    tin_sb = [at(f"tin{i}", [128, 512], F32, (161 + 2 * i) * KB) for i in range(2)]
    ts1_sb = at("ts1_sb", [128, 512], F32, 165 * KB)
    tout_sb = at("tout_sb", [128, 512], BF16, 167 * KB)
    cur = [168 * KB]

    def cst(name, shape, dtype):
        nonlocal_off = cur[0]
        t = at(name, shape, dtype, nonlocal_off)
        nbytes = int(np.prod(shape[1:])) * (2 if dtype == BF16 else 4)
        cur[0] += (nbytes + 31) // 32 * 32
        return t

    bqk_sb = cst("bqk_sb", [128, 16], F32)
    absacc_sb = cst("absacc_sb", [128, 44], F32)
    amg_sb = cst("amg_sb", [1, 8], F32)
    thr_sb = cst("thr_sb", [128, 8], F32)
    thrrow_sb = cst("thrrow_sb", [1, 8], F32)
    onescol_sb = cst("onescol_sb", [128, 1], F32)
    ones3_sb = cst("ones3_sb", [3, 128], BF16)
    ones128_sb = cst("ones128_sb", [128, 1], BF16)
    bv3_sb = cst("bv3_sb", [3, 1024], BF16)
    outb3_sb = cst("outb3_sb", [3, 2048], BF16)
    identf_sb = cst("identf_sb", [128, 128], F32)
    identb_sb = cst("identb_sb", [128, 128], BF16)
    onesf_sb = cst("onesf_sb", [1, 128], F32)
    assert cur[0] <= 176 * KB, cur[0]
    assert BASE + 176 * KB <= nc.SBUF_PARTITION_SIZE_BYTES - 32

    ps = [nc.alloc_psum_tensor(f"ps{i}", [128, 512], F32) for i in range(8)]
    bank_free = [None] * 8

    p = P(nc)
    for s in ("gl", "sl", "tt", "ac", "ve", "st", "gp", "car", "cfc", "cpj", "cctx"):
        p.sem(s)
    ev = {}
    rt = {}
    p.E("vector", lambda e: rt.__setitem__("u_ve", e.partition_id() % 2))
    p.E("scalar", lambda e: rt.__setitem__("u_ac", e.partition_id() % 2))
    p.E("gpsimd", lambda e: rt.__setitem__("u_gl", e.partition_id() % 2))

    def bank_wait(i):
        return [bank_free[i]] if bank_free[i] is not None else []

    c = p.E("sync", lambda e: e.dma_start(out=bqk_sb[:, :], in_=dbqk[:, :]), inc=("sl", 16))
    c = p.E("sync", lambda e: e.dma_start(out=identf_sb[:, :], in_=dident[:, :]), inc=("sl", 16))
    ev["consts_sl"] = ("sl", c)
    c = p.E("gpsimd", lambda e: e.dma_start(out=bv3_sb[:, :], in_=dbv3[:, :]), inc=("gl", 16))
    c = p.E("gpsimd", lambda e: e.dma_start(out=outb3_sb[:, :], in_=doutb3[:, :]), inc=("gl", 16))
    cgl = c
    c = p.E("gpsimd", lambda e: e.memset(ones3_sb[:, :], 1.0), inc=("gp", 1))
    c = p.E("gpsimd", lambda e: e.memset(ones128_sb[:, :], 1.0), inc=("gp", 1))
    c = p.E("gpsimd", lambda e: e.memset(onesf_sb[:, :], 1.0), inc=("gp", 1))
    c = p.E("gpsimd", lambda e: e.memset(onescol_sb[:, :], 1.0), inc=("gp", 1))
    ev["consts_gl"] = ("gl", cgl)
    ev["consts_gp"] = ("gp", c)

    def wload(sb, dr, fw):
        cl = None
        for g in range(4):
            def fn(e, g=g, sb=sb, dr=dr, fw=fw):
                src = dr[g * 512:(g + 1) * 512, :].rearrange("(dc p) f -> p dc f", p=128)
                dst = sb[:, g * 4 * fw:(g + 1) * 4 * fw].rearrange("p (dc f) -> p dc f", f=fw)
                return e.dma_start(out=dst, in_=src)
            cl = p.E("gpsimd", fn, inc=("gl", 16))
        return cl
    ev["wq"] = ("gl", wload(wq_sb, dwq, 1024))
    ev["wk"] = ("gl", wload(wk_sb, dwk, 1024))

    t_pieces = [(0, g, cc) for g in range(2) for cc in range(10)] + \
               [(1, g, cc) for g in range(5) for cc in range(4)]

    def piece_src(m, g, cc):
        return (dwfcT if m == 0 else dwpjT)[g * 128:(g + 1) * 128, cc * 512:(cc + 1) * 512]

    r1_abs = {}
    for i, (m, g, cc) in enumerate(t_pieces):
        buf = i % 2
        w = [("ve", r1_abs[i - 2])] if i >= 2 else []
        cl = p.E("sync", lambda e, m=m, g=g, cc=cc, buf=buf: e.dma_start(out=tin_sb[buf][:, :],
                                                                         in_=piece_src(m, g, cc)),
                 waits=w, inc=("sl", 16))
        r1_abs[i] = p.E("vector",
                        lambda e, i=i, buf=buf: e.reduce_sum(absacc_sb[:, i:i + 1], tin_sb[buf][:, :],
                                                             axis=AX.X, apply_absolute_value=True),
                        waits=[("sl", cl)], inc=("ve", 1))
    c_r1 = p.E("vector", lambda e: e.reduce_sum(absacc_sb[:, 40:41], absacc_sb[:, 0:20], axis=AX.X),
               waits=[("ve", r1_abs[len(t_pieces) - 1])], inc=("ve", 1))
    c_r1 = p.E("vector", lambda e: e.reduce_sum(absacc_sb[:, 41:42], absacc_sb[:, 20:40], axis=AX.X),
               waits=[("ve", c_r1)], inc=("ve", 1))
    ev["r1"] = ("ve", c_r1)

    def xt_load(tc, buf, extra_w):
        cl = None
        for g2 in range(4):
            def fn(e, tc=tc, g2=g2, buf=buf):
                src = dxT[g2 * 512:(g2 + 1) * 512, tc * 512:(tc + 1) * 512].rearrange("(dc p) t -> p dc t", p=128)
                dst = xt_sb[buf][:, g2 * 4 * 512:(g2 + 1) * 4 * 512].rearrange("p (dc t) -> p dc t", t=512)
                return e.dma_start(out=dst, in_=src)
            cl = p.E("gpsimd", fn, waits=extra_w if g2 == 0 else (), inc=("gl", 16))
        return cl

    jobs = [("Q", tc) for tc in range(4)] + [("K", tc) for tc in range(4)] + [("V", tc) for tc in range(4)]
    xt_cnt = {}
    job_mms = {}
    xt_cnt[0] = xt_load(0, 0, [])
    xt_cnt[1] = xt_load(1, 1, [])
    for s, (kind, tc) in enumerate(jobs):
        buf = s % 2
        if kind in ("Q", "K"):
            wsb = wq_sb if kind == "Q" else wk_sb
            dst_sb = qT_sb if kind == "Q" else kT_sb
            wev = ev["wq"] if kind == "Q" else ("gl", ev["wk"][1])
            for ft in range(8):
                bk = ft % 2
                waits = [("gl", xt_cnt[s]), (wev[0], wev[1])] + bank_wait(bk)
                c_mm = None
                for dc in range(16):
                    def fn(e, bk=bk, wsb=wsb, ft=ft, dc=dc, buf=buf):
                        return e.matmul(ps[bk][:, :],
                                        wsb[:, dc * 1024 + ft * 128: dc * 1024 + ft * 128 + 128],
                                        xt_sb[buf][:, dc * 512:(dc + 1) * 512],
                                        start=(dc == 0), stop=(dc == 15))
                    c_mm = p.E("tensor", fn, waits=waits if dc == 0 else (),
                               inc=("tt", 1) if dc == 15 else None)
                def ev_fn(e, dst_sb=dst_sb, ft=ft, tc=tc, bk=bk, kind=kind):
                    return e.activation(dst_sb[:, ft * 2048 + tc * 512: ft * 2048 + tc * 512 + 512],
                                        ps[bk][:, :], AF.Identity,
                                        bias=bqk_sb[:, (0 if kind == "Q" else 8) + ft:
                                                    (1 if kind == "Q" else 9) + ft])
                c_ev = p.E("scalar", ev_fn, waits=[("tt", c_mm), ("sl", ev["consts_sl"][1])], inc=("ac", 1))
                bank_free[bk] = ("ac", c_ev)
        else:
            for tl in range(4):
                tt = tc * 4 + tl
                for hc in range(2):
                    bk = 2 + (tt * 2 + hc) % 2
                    waits = [("gl", xt_cnt[s]), ("gl", ev["wv"][1]), ("gl", ev["consts_gl"][1]), ("gp", ev["consts_gp"][1])] + bank_wait(bk)
                    def fb(e, bk=bk, hc=hc):
                        return e.matmul(ps[bk][:, :], ones3_sb[:, :], bv3_sb[:, hc * 512:(hc + 1) * 512],
                                        start=True, stop=False)
                    p.E("tensor", fb, waits=waits)
                    c_mm = None
                    for dc in range(16):
                        def fn(e, bk=bk, tl=tl, dc=dc, hc=hc, buf=buf):
                            return e.matmul(ps[bk][:, :],
                                            xt_sb[buf][:, dc * 512 + tl * 128: dc * 512 + tl * 128 + 128],
                                            wv_sb[:, dc * 1024 + hc * 512: dc * 1024 + hc * 512 + 512],
                                            start=False, stop=(dc == 15))
                        c_mm = p.E("tensor", fn, inc=("tt", 1) if dc == 15 else None)
                    def ev_fn(e, tt=tt, hc=hc, bk=bk):
                        return e.tensor_copy(v_sb[:, tt * 1024 + hc * 512: tt * 1024 + hc * 512 + 512],
                                             ps[bk][:, :])
                    c_ev = p.E("vector", ev_fn, waits=[("tt", c_mm)], inc=("ve", 1))
                    bank_free[bk] = ("ve", c_ev)
        job_mms[s] = p.now("tt")
        if s == 3:
            ev["a1_mms"] = ("tt", p.now("tt"))
            cl = None
            for g in range(4):
                def fn(e, g=g):
                    src = dwv[g * 512:(g + 1) * 512, :].rearrange("(dc p) f -> p dc f", p=128)
                    dst = wv_sb[:, g * 4 * 1024:(g + 1) * 4 * 1024].rearrange("p (dc f) -> p dc f", f=1024)
                    return e.dma_start(out=dst, in_=src)
                cl = p.E("gpsimd", fn, waits=[ev["a1_mms"]] if g == 0 else (), inc=("gl", 16))
            ev["wv"] = ("gl", cl)
        if s + 2 < len(jobs):
            xt_cnt[s + 2] = xt_load((jobs[s + 2][1]), (s + 2) % 2, [("tt", job_mms[s])])
    ev["a2_mms"] = ("tt", p.now("tt"))
    ev["v_evacs"] = ("ve", p.now("ve"))
    ev["a1_evacs"] = ("ac", p.now("ac"))

    c_owp = None
    for g in range(4):
        c_owp = p.E("gpsimd", lambda e, g=g: e.dma_start(out=outw_bf[g * 512:(g + 1) * 512, :],
                                                         in_=doutw[g * 512:(g + 1) * 512, :]),
                    inc=("gl", 16))
    ev["owp"] = ("gl", c_owp)


    c = p.E("gpsimd", lambda e: e.dma_start(out=masks_sb[:, :], in_=dmasks[:, :]),
            waits=[ev["a2_mms"]], inc=("gl", 16))
    c = p.E("gpsimd", lambda e: e.dma_start(out=identb_sb[:, :], in_=dident[:, :]), inc=("gl", 16))
    ev["attn_glm"] = ("gl", c)
    c = p.E("sync", lambda e: e.dma_start(out=alibi_sb[:, :], in_=dalibi[:, :]),
            waits=[ev["a2_mms"]], inc=("sl", 16))
    ev["attn_sl"] = ("sl", c)
    c = p.E("gpsimd", lambda e: e.dma_start(out=negsi_sb[:, :], in_=dnegsi[:, :]),
            waits=[ev["a2_mms"]], inc=("gl", 16))
    ev["attn_gl"] = ("gl", c)

    r2_state = {"prev_comb": None, "prev_w": None}
    r2_loads = {}

    def emit_r2_load(i):
        if i >= len(t_pieces):
            return
        m, g, cc = t_pieces[i]
        buf = i % 2
        w = []
        if i >= 2 and r2_state.get(f"g2_{i-2}"):
            w.append(("ve", r2_state[f"g2_{i-2}"]))
        r2_loads[i] = p.E("sync", lambda e, m=m, g=g, cc=cc, buf=buf: e.dma_start(
            out=tin_sb[buf][:, :], in_=piece_src(m, g, cc)), waits=w, inc=("sl", 16))

    def emit_r2_compute(i):
        if i >= len(t_pieces):
            return
        m, g, cc = t_pieces[i]
        buf = i % 2
        c_g1 = p.E("vector", lambda e, m=m, buf=buf: e.tensor_scalar(
            ts1_sb[:, :], tin_sb[buf][:, :], thr_sb[:, 2 + m:3 + m], 1.0, ALU.is_gt, ALU.mult),
            waits=[("sl", r2_loads[i]), ev["thr"]] +
                  ([("ve", r2_state["prev_comb"])] if r2_state["prev_comb"] else []),
            inc=("ve", 1))
        c_g2 = p.E("vector", lambda e, m=m, buf=buf: e.scalar_tensor_tensor(
            ts1_sb[:, :], tin_sb[buf][:, :], thr_sb[:, m:m + 1], ts1_sb[:, :], ALU.is_lt, ALU.subtract),
            waits=[("ve", c_g1)], inc=("ve", 1))
        r2_state[f"g2_{i}"] = c_g2
        c_c2 = p.E("vector", lambda e, m=m: e.tensor_scalar(
            tout_sb[:, :], ts1_sb[:, :], thr_sb[:, 4 + m:5 + m], -2.0, ALU.mult, ALU.mult),
            waits=[("ve", c_g2)] + ([("sl", r2_state["prev_w"])] if r2_state["prev_w"] else []),
            inc=("ve", 1))
        r2_state["prev_comb"] = c_c2
        c_c3 = p.E("vector", lambda e: e.tensor_scalar(ts1_sb[:, 0:8], tout_sb[:, 0:8], 1.0, 1.0,
                                                       ALU.mult, ALU.mult),
                   waits=[("ve", c_c2)], inc=("ve", 1))
        if m == 0:
            cw = p.E("sync", lambda e, g=g, cc=cc: e.dma_start(
                out=fc_agi[g * 128:(g + 1) * 128, cc * 512:(cc + 1) * 512], in_=tout_sb[:, :]),
                waits=[("ve", c_c3)], inc=("sl", 16))
        else:
            for k in range(4):
                cw = p.E("sync", lambda e, g=g, cc=cc, k=k: e.dma_start(
                    out=pj_agi[(cc * 4 + k) * 640 + g * 128: (cc * 4 + k) * 640 + g * 128 + 128, :],
                    in_=tout_sb[:, k * 128:(k + 1) * 128]),
                    waits=[("ve", c_c3)] if k == 0 else (), inc=("sl", 16))
        r2_state["prev_w"] = cw


    amref = {}

    def emit_absmean_am():
        c_ams = p.E("tensor",
                    lambda e: e.matmul(ps[6][0:1, 0:2], onescol_sb[:, :], absacc_sb[:, 40:42],
                                       start=True, stop=True),
                    waits=[ev["r1"], ("gl", ev["consts_gl"][1]), ("gp", ev["consts_gp"][1])] + bank_wait(6), inc=("tt", 1))
        c_amc = p.E("vector", lambda e: e.tensor_copy(amg_sb[0:1, 0:2], ps[6][0:1, 0:2]),
                    waits=[("tt", c_ams)], inc=("ve", 1))
        bank_free[6] = ("ve", c_amc)
        c_ami = p.E("gpsimd", lambda e: e.dma_start(out=am_in[:, :], in_=amg_sb[0:1, 0:2]),
                    waits=[("ve", c_amc)], inc=("gl", 16))
        c_ar = p.E("gpsimd", lambda e: e.collective_compute("AllReduce", ALU.add, replica_groups=ALL8,
                                                            ins=[am_in[:, :]], outs=[am_out[:, :]]),
                   waits=[("gl", c_ami)], inc=("car", 1))
        c_amo = p.E("gpsimd", lambda e: e.dma_start(out=amg_sb[0:1, 2:4], in_=am_out[:, :]),
                    waits=[("car", c_ar)], inc=("gl", 16))
        amref["amo"] = c_amo

    def emit_thr_bcast():
        c_amo = amref["amo"]
        NEL = float(FF * D)
        c_v = p.E("vector", lambda e: e.tensor_scalar(amg_sb[0:1, 4:6], amg_sb[0:1, 2:4], 1.0 / NEL, 1.0,
                                                      ALU.mult, ALU.mult),
                  waits=[("gl", c_amo)], inc=("ve", 1))
        c_v = p.E("vector", lambda e: e.tensor_scalar(amg_sb[0:1, 4:6], amg_sb[0:1, 4:6], 1e-5, 1.0,
                                                      ALU.max, ALU.mult), waits=[("ve", c_v)], inc=("ve", 1))
        c_v = p.E("vector", lambda e: e.tensor_scalar(thrrow_sb[0:1, 0:2], amg_sb[0:1, 4:6], -0.7, 1.0,
                                                      ALU.mult, ALU.mult), waits=[("ve", c_v)], inc=("ve", 1))
        c_v = p.E("vector", lambda e: e.tensor_scalar(thrrow_sb[0:1, 2:4], amg_sb[0:1, 4:6], 0.7, 1.0,
                                                      ALU.mult, ALU.mult), waits=[("ve", c_v)], inc=("ve", 1))
        c_v = p.E("vector", lambda e: e.tensor_scalar(thrrow_sb[0:1, 4:6], amg_sb[0:1, 4:6], 0.5, 1.0,
                                                      ALU.mult, ALU.mult), waits=[("ve", c_v)], inc=("ve", 1))
        c_v = p.E("vector", lambda e: e.tensor_scalar(thrrow_sb[0:1, 6:8], amg_sb[0:1, 4:6], -0.5, 1.0,
                                                      ALU.mult, ALU.mult), waits=[("ve", c_v)], inc=("ve", 1))
        c_bc = p.E("tensor", lambda e: e.matmul(ps[0][:, 0:8], onesf_sb[:, :], thrrow_sb[0:1, 0:8],
                                                start=True, stop=True),
                   waits=[("ve", c_v)] + bank_wait(0), inc=("tt", 1))
        c_thr = p.E("vector", lambda e: e.tensor_copy(thr_sb[:, 0:8], ps[0][:, 0:8]),
                    waits=[("tt", c_bc)], inc=("ve", 1))
        bank_free[0] = ("ve", c_thr)
        ev["thr"] = ("ve", c_thr)

    ctx_done = {}
    lmm_hist = {}
    pT_guard = {0: None, 1: None, 2: None, 3: None}
    first_evac = [True]
    r2_next = [0]
    pend_norm = [None]
    ctxP_done = {}
    ctxC_done = {}

    def emit_head_copies(h0, evac_cnt):
        def fcp(e, h0=h0):
            return e.tensor_copy(ctx2_sb[:, h0 * 1024:(h0 + 1) * 1024],
                                 ctx_sb[:, ds(h0 * 2048 + (1 - rt["u_ve"]) * 1024, 1024)])
        w = [("ve", evac_cnt)]
        if first_evac[0]:
            pass
        ctxP_done[h0] = p.E("vector", fcp, waits=w, inc=("ve", 1))
        def fcc(e, h0=h0):
            return e.copy(ctxC_sb[:, h0 * 1024:(h0 + 1) * 1024],
                          ctx_sb[:, ds(h0 * 2048 + rt["u_ac"] * 1024, 1024)])
        ctxC_done[h0] = p.E("scalar", fcc, waits=[("ve", evac_cnt)], inc=("ac", 1))

    def flush_norm():
        if pend_norm[0] is None:
            return
        h0, c0, ctx_bk0, c_rl0, slot = pend_norm[0]
        pend_norm[0] = None
        c_bcm = p.E("tensor", lambda e, slot=slot: e.matmul(ps[6][:, :], onesf_sb[:, :],
                                                            rl_sb[0:1, slot * 512:(slot + 1) * 512],
                                                            start=True, stop=True),
                    waits=[("ve", c_rl0)] + bank_wait(6), inc=("tt", 1))
        c_bcc = p.E("vector", lambda e, slot=slot: e.tensor_copy(bcst_sb[:, slot * 512:(slot + 1) * 512],
                                                                 ps[6][:, :]),
                    waits=[("tt", c_bcm)], inc=("ve", 1))
        bank_free[6] = ("ve", c_bcc)
        evac_w = [("ve", c_bcc)]
        if first_evac[0]:
            evac_w.append(ev["a2_mms"])
            first_evac[0] = False
        def fevac(e, h0=h0, c0=c0, ctx_bk0=ctx_bk0, slot=slot):
            return e.tensor_tensor(ctx_sb[:, h0 * 2048 + c0 * 512: h0 * 2048 + c0 * 512 + 512],
                                   ps[ctx_bk0][:, :], bcst_sb[:, slot * 512:(slot + 1) * 512], ALU.mult)
        c_ev = p.E("vector", fevac, waits=evac_w, inc=("ve", 1))
        bank_free[ctx_bk0] = ("ve", c_ev)
        ctx_done[(h0, c0)] = c_ev

    for h in range(8):
        for cI in range(4):
            g = h * 4 + cI
            n_t = 4 * cI + 4
            ctx_bk = 4 + g % 2
            l_bk = 7
            ctx_w0 = bank_wait(ctx_bk)
            l_w0 = bank_wait(l_bk)
            c_lmm = None
            pend = None
            order = list(range(4 * cI, 4 * cI + 4)) + list(range(4 * cI))

            def emit_ctx_l(pend, last):
                nonlocal c_lmm
                ti, jt, c_exp = pend
                def fctx(e, ctx_bk=ctx_bk, h=h, jt=jt, pbuf=ti % 4, ti=ti):
                    return e.matmul(ps[ctx_bk][:, :],
                                    v_sb[:, jt * 1024 + h * 128: jt * 1024 + h * 128 + 128],
                                    pT_sb[pbuf][:, :], start=(ti == 0), stop=False)
                wctx = [("ac", c_exp)]
                if ti == 0:
                    wctx += [("ve", ev["v_evacs"][1])] + ctx_w0 + l_w0
                p.E("tensor", fctx, waits=wctx)
                def fl(e, l_bk=l_bk, pbuf=ti % 4, ti=ti, last=last):
                    return e.matmul(ps[l_bk][0:1, :], ones128_sb[:, :],
                                    pT_sb[pbuf][:, :], start=(ti == 0), stop=last)
                c_lmm = p.E("tensor", fl, inc=("tt", 1))
                lmm_hist[(h, cI, ti)] = c_lmm
                pT_guard[ti % 4] = ("tt", c_lmm)

            for ti in range(n_t):
                jt = order[ti]
                sbk = ti % 4
                pbuf = ti % 4
                var = jt - 4 * cI + 16
                msk = jt - 4 * cI
                waits = list(bank_wait(sbk))
                if ti == 0:
                    waits += [("ac", ev["a1_evacs"][1]), ("gl", ev["attn_gl"][1])]
                def mm1(e, sbk=sbk, h=h, jt=jt, cI=cI):
                    return e.matmul(ps[sbk][:, :],
                                    kT_sb[:, h * 2048 + jt * 128: h * 2048 + jt * 128 + 128],
                                    qT_sb[:, h * 2048 + cI * 512: h * 2048 + cI * 512 + 512],
                                    start=True, stop=False)
                p.E("tensor", mm1, waits=waits)
                def mm2(e, sbk=sbk, h=h, last=(msk < 0)):
                    return e.matmul(ps[sbk][:, :], ones3_sb[:, :],
                                    negsi_sb[:, h * 512: h * 512 + 512], start=False, stop=last)
                c_smm = p.E("tensor", mm2, inc=("tt", 1) if msk < 0 else None,
                            waits=[ev["attn_glm"]] if (ti == 0) else ())
                if msk >= 0:
                    def mmm(e, sbk=sbk, msk=msk):
                        return e.matmul(ps[sbk][:, :], identb_sb[:, :],
                                        masks_sb[:, msk * 512:(msk + 1) * 512], start=False, stop=True)
                    c_smm = p.E("tensor", mmm, inc=("tt", 1))
                if pend is not None:
                    emit_ctx_l(pend, last=False)
                if ti == 2:
                    flush_norm()
                exp_w = [("tt", c_smm), ("sl", ev["attn_sl"][1])]
                if pT_guard[pbuf] is not None:
                    exp_w.append(pT_guard[pbuf])
                def fexp(e, sbk=sbk, pbuf=pbuf, h=h, var=var):
                    return e.activation(pT_sb[pbuf][:, :], ps[sbk][:, :], AF.Exp,
                                        bias=alibi_sb[:, h * 20 + var: h * 20 + var + 1])
                c_exp = p.E("scalar", fexp, waits=exp_w, inc=("ac", 1))
                bank_free[sbk] = ("ac", c_exp)
                pend = (ti, jt, c_exp)
            emit_ctx_l(pend, last=True)
            c_rl = p.E("vector", lambda e, l_bk=l_bk, g=g: e.reciprocal(
                rl_sb[0:1, (g % 2) * 512:(g % 2) * 512 + 512], ps[l_bk][0:1, :]),
                       waits=[("tt", c_lmm)], inc=("ve", 1))
            bank_free[l_bk] = ("ve", c_rl)
            pend_norm[0] = (h, cI, ctx_bk, c_rl, g % 2)
            if g == 0:
                emit_absmean_am()
            if g == 3:
                emit_thr_bcast()
            if h >= 2:
                for _ in range(2):
                    i = r2_next[0]
                    emit_r2_compute(i)
                    emit_r2_load(i + 2)
                    r2_next[0] += 1
            elif h == 1 and cI == 0:
                emit_r2_load(0)
                emit_r2_load(1)
    flush_norm()
    while r2_next[0] < len(t_pieces):
        i = r2_next[0]
        emit_r2_compute(i)
        emit_r2_load(i + 2)
        r2_next[0] += 1
    ev["attn_mms"] = ("tt", p.now("tt"))
    ev["ctx_all"] = ("ve", max(ctx_done.values()))
    ev["tern_w"] = ("sl", p.now("sl"))

    c_agfc = p.E("gpsimd", lambda e: e.collective_compute("AllGather", ALU.bypass, replica_groups=ALL8,
                                                          ins=[fc_agi[:, :]], outs=[fc_ago[:, :]]),
                 waits=[ev["tern_w"]], inc=("cfc", 1))
    c_agpj = p.E("gpsimd", lambda e: e.collective_compute("AllGather", ALU.bypass, replica_groups=ALL8,
                                                          ins=[pj_agi[:, :]], outs=[pj_ago[:, :]]),
                 inc=("cpj", 1))

    c_xo = None
    for g in range(2):
        def fxo(e, g=g):
            src = dx_own[g * 512:(g + 1) * 512, :].rearrange("(tt p) d -> p tt d", p=128)
            dst = x_sb[:, g * 4 * 2048:(g + 1) * 4 * 2048].rearrange("p (tt d) -> p tt d", d=2048)
            return e.dma_start(out=dst, in_=src)
        c_xo = p.E("sync", fxo, waits=[ev["attn_mms"]] if g == 0 else (), inc=("sl", 16))
    ev["x_own"] = ("sl", c_xo)

    for h in range(8):
        emit_head_copies(h, ev["ctx_all"][1])
    ev["ctxC"] = ("ac", max(ctxC_done.values()))
    c_stg = None
    for h in range(8):
        def fstg(e, h=h):
            return e.dma_start(out=ctx_agi[h * 128:(h + 1) * 128, :],
                               in_=ctx2_sb[:, h * 1024:(h + 1) * 1024])
        c_stg = p.E("gpsimd", fstg, waits=[("ve", ctxP_done[h])], inc=("gl", 16))
    c_agc = p.E("gpsimd", lambda e: e.collective_compute("AllGather", ALU.bypass, replica_groups=PAIRS,
                                                         ins=[ctx_agi[:, :]], outs=[ctx_ago[:, :]]),
                waits=[("gl", c_stg)], inc=("cctx", 1))
    def fc2(e):
        prow = (1 - rt["u_gl"]) * 1024
        src = ctx_ago[ds(prow, 1024), :].rearrange("(h p) t -> p h t", p=128)
        dst = ctx2_sb[:, :].rearrange("p (h t) -> p h t", t=1024)
        return e.dma_start(out=dst, in_=src)
    c_ctx2 = p.E("gpsimd", fc2, waits=[("cctx", c_agc), ev["attn_mms"]], inc=("gl", 16))
    ev["ctx2"] = ("gl", c_ctx2)

    def emit_ln_batch(tiles, col_sb, ve_ready, extra_sq_w=()):
        T = len(tiles)
        t0 = tiles[0]
        view = col_sb[:, :].rearrange("p (t j) -> p t j", j=8)

        def cv(k):
            return view[:, t0:t0 + T, k:k + 1]

        for tt in tiles:
            p.E("vector", lambda e, tt=tt: e.reduce_sum(col_sb[:, tt * 8:tt * 8 + 1],
                                                        x_sb[:, tt * 2048:(tt + 1) * 2048], axis=AX.X),
                waits=[("ve", ve_ready)], inc=("ve", 1))
        c2 = None
        for tt in tiles:
            for j in range(4):
                def fsq(e, tt=tt, j=j):
                    sj = x_sb[:, tt * 2048 + j * 512: tt * 2048 + j * 512 + 512]
                    return e.scalar_tensor_tensor(lnsq_sb[:, :], sj, 1.0, sj, ALU.mult, ALU.mult,
                                                  accum_out=col_sb[:, tt * 8 + 1 + j:tt * 8 + 2 + j])
                c2 = p.E("vector", fsq, waits=list(extra_sq_w) if (tt == tiles[0] and j == 0) else (),
                         inc=("ve", 1))
            p.E("vector", lambda e: e.nop(), waits=[("ve", c2)])
        c3 = p.E("vector", lambda e: e.reduce_sum(cv(5), view[:, t0:t0 + T, 1:5], axis=AX.X),
                 waits=[("ve", c2)], inc=("ve", 1))
        c3 = p.E("vector", lambda e: e.tensor_scalar(cv(0), cv(0), 1.0 / D, 1.0, ALU.mult, ALU.mult),
                 waits=[("ve", c3)], inc=("ve", 1))
        c3 = p.E("vector", lambda e: e.tensor_tensor(cv(6), cv(0), cv(0), ALU.mult),
                 waits=[("ve", c3)], inc=("ve", 1))
        c3 = p.E("vector", lambda e: e.tensor_scalar(cv(5), cv(5), 1.0 / D, 1.0, ALU.mult, ALU.mult),
                 waits=[("ve", c3)], inc=("ve", 1))
        c3 = p.E("vector", lambda e: e.tensor_tensor(cv(5), cv(5), cv(6), ALU.subtract),
                 waits=[("ve", c3)], inc=("ve", 1))
        c3 = p.E("vector", lambda e: e.tensor_scalar(cv(5), cv(5), LN_EPS, 1.0, ALU.add, ALU.mult),
                 waits=[("ve", c3)], inc=("ve", 1))
        c4 = p.E("vector", lambda e: e.tensor_scalar(cv(7), cv(5), 0.30, 0.55, ALU.mult, ALU.add),
                 waits=[("ve", c3)], inc=("ve", 1))
        for _it in range(4):
            c4 = p.E("vector", lambda e: e.reciprocal(cv(6), cv(7)), waits=[("ve", c4)], inc=("ve", 1))
            c4 = p.E("vector", lambda e: e.scalar_tensor_tensor(cv(6), cv(6), 0.5, cv(5),
                                                                ALU.mult, ALU.mult),
                     waits=[("ve", c4)], inc=("ve", 1))
            c4 = p.E("vector", lambda e: e.scalar_tensor_tensor(cv(7), cv(7), 0.5, cv(6),
                                                                ALU.mult, ALU.add),
                     waits=[("ve", c4)], inc=("ve", 1))
        c4 = p.E("vector", lambda e: e.reciprocal(cv(7), cv(7)), waits=[("ve", c4)], inc=("ve", 1))
        done = {}
        for tt in tiles:
            def fnorm(e, tt=tt):
                sl = x_sb[:, tt * 2048:(tt + 1) * 2048]
                return e.tensor_scalar(sl, sl, col_sb[:, tt * 8:tt * 8 + 1],
                                       col_sb[:, tt * 8 + 7:tt * 8 + 8], ALU.subtract, ALU.mult)
            done[tt] = p.E("vector", fnorm, waits=[("ve", c4)], inc=("ve", 1))
        return done

    ln1_done = {}
    buf_users = {0: 0, 1: 0}
    for tp in range(4):
        c_mm = None
        for fc in range(16):
            buf = fc % 2
            w = [ev["owp"], ("tt", buf_users[buf])]
            if tp == 0 and fc < 2:
                w.append(ev["ctx_all"])
            c_ow = p.E("gpsimd", lambda e, fc=fc, buf=buf: e.dma_start(out=ow_sb[buf][:, :],
                                                                       in_=outw_bf[fc * 128:(fc + 1) * 128, :]),
                       waits=w, inc=("gl", 16))
            for tloc in range(2):
                tt = tp * 2 + tloc
                for n4 in range(4):
                    bk = tloc * 4 + n4
                    if fc == 0:
                        w0 = list(bank_wait(bk)) + [("gl", c_ow), ev["ctxC"],
                                                    ("gl", ev["consts_gl"][1]), ("gp", ev["consts_gp"][1])]
                        def fb(e, bk=bk, n4=n4):
                            return e.matmul(ps[bk][:, :], ones3_sb[:, :],
                                            outb3_sb[:, n4 * 512:(n4 + 1) * 512], start=True, stop=False)
                        p.E("tensor", fb, waits=w0)
                    def fmm(e, bk=bk, fc=fc, tt=tt, n4=n4, buf=buf):
                        if fc < 8:
                            lhsT = ctxC_sb[:, fc * 1024 + tt * 128: fc * 1024 + tt * 128 + 128]
                        else:
                            lhsT = ctx2_sb[:, (fc - 8) * 1024 + tt * 128: (fc - 8) * 1024 + tt * 128 + 128]
                        return e.matmul(ps[bk][:, :], lhsT, ow_sb[buf][:, n4 * 512:(n4 + 1) * 512],
                                        start=False, stop=(fc == 15))
                    wmm = [("gl", c_ow)] if (tloc == 0 and n4 == 0) else []
                    if fc == 8 and tloc == 0 and n4 == 0:
                        wmm.append(ev["ctx2"])
                    c_mm = p.E("tensor", fmm, waits=wmm,
                               inc=("tt", 1) if (fc == 15 or (tloc == 1 and n4 == 3)) else None)
            buf_users[buf] = p.now("tt")
        for tloc in range(2):
            tt = tp * 2 + tloc
            for n4 in range(4):
                bk = tloc * 4 + n4
                def fev(e, bk=bk, tt=tt, n4=n4):
                    sl = x_sb[:, tt * 2048 + n4 * 512: tt * 2048 + n4 * 512 + 512]
                    return e.tensor_tensor(sl, sl, ps[bk][:, :], ALU.add)
                c_ad = p.E("vector", fev, waits=[("tt", c_mm), ev["x_own"]], inc=("ve", 1))
                bank_free[bk] = ("ve", c_ad)
        ln1_done.update(emit_ln_batch([2 * tp, 2 * tp + 1], lncol_sb, p.now("ve")))
    ev["outproj"] = ("ve", p.now("ve"))




    fcs_free = {}
    pjs_free = {}
    st_cnt = None
    for tc2 in range(2):
        xT_done = None
        for tl in range(4):
            tt = tc2 * 4 + tl
            for dg in range(4):
                w = list(bank_wait(7)) + [("ve", ln1_done[tt]), ("sl", ev["consts_sl"][1])]
                if tc2 == 1 and tl == 0 and dg == 0:
                    w.append(fcs_free[9])
                c_tr = None
                for k in range(4):
                    dc = dg * 4 + k
                    def ftr(e, tt=tt, dc=dc, k=k):
                        return e.transpose(ps[7][:, k * 128:(k + 1) * 128],
                                           x_sb[:, tt * 2048 + dc * 128: tt * 2048 + dc * 128 + 128],
                                           identf_sb[:, :])
                    c_tr = p.E("tensor", ftr, waits=w if k == 0 else (), inc=("tt", 1) if k == 3 else None)
                def fxe(e, tl=tl, dg=dg):
                    dst = xlnT_sb[:, :].rearrange("p (dc t) -> p dc t", t=512)[:, dg * 4:(dg + 1) * 4,
                                                                              tl * 128:(tl + 1) * 128]
                    src = ps[7][:, :].rearrange("p (k t) -> p k t", t=128)
                    return e.tensor_copy(dst, src)
                xT_done = p.E("vector", fxe, waits=[("tt", c_tr)], inc=("ve", 1))
                bank_free[7] = ("ve", xT_done)
        ev[f"xlnT{tc2}"] = ("ve", xT_done)

        first_sq = True
        for sI in range(10):
            buf = sI % 2
            gi = tc2 * 10 + sI
            w = [("cfc", c_agfc)]
            if gi <= 1:
                w.append(ev["outproj"])
            if tc2 == 1 and sI == 0:
                w.append(pjs_free[15])
            if gi >= 2:
                w.append(fcs_free[gi - 2])
            c_sl = None
            for g in range(4):
                def fsl(e, sI=sI, g=g, buf=buf):
                    src = fc_ago[g * 512:(g + 1) * 512, sI * 512:(sI + 1) * 512].rearrange(
                        "(dc p) f -> p dc f", p=128)
                    dst = fcs_sb[buf][:, g * 4 * 512:(g + 1) * 4 * 512].rearrange("p (dc f) -> p dc f", f=512)
                    return e.dma_start(out=dst, in_=src)
                c_sl = p.E("sync", fsl, waits=w if g == 0 else (), inc=("sl", 16))
            for hft in range(4):
                hf = sI * 4 + hft
                bk = hf % 2
                w2 = list(bank_wait(bk)) + [("sl", c_sl), ev[f"xlnT{tc2}"]]
                c_mm = None
                for dc in range(16):
                    def fmm(e, bk=bk, dc=dc, hft=hft, buf=buf):
                        return e.matmul(ps[bk][:, :],
                                        fcs_sb[buf][:, dc * 512 + hft * 128: dc * 512 + hft * 128 + 128],
                                        xlnT_sb[:, dc * 512:(dc + 1) * 512],
                                        start=(dc == 0), stop=(dc == 15))
                    c_mm = p.E("tensor", fmm, waits=w2 if dc == 0 else (), inc=("tt", 1) if dc == 15 else None)
                c_r = p.E("vector", lambda e, bk=bk: e.tensor_scalar(relu_sb[:, :], ps[bk][:, :], 0.0, 1.0,
                                                                    ALU.max, ALU.mult),
                          waits=[("tt", c_mm)], inc=("ve", 1))
                sq_w = [("ve", c_r)]
                if first_sq and tc2 == 1:
                    sq_w.append(pjs_free[15])
                first_sq = False
                c_q = p.E("vector", lambda e, hf=hf: e.tensor_mul(h_sb[:, hf * 512:(hf + 1) * 512],
                                                                 relu_sb[:, :], relu_sb[:, :]),
                          waits=sq_w, inc=("ve", 1))
                bank_free[bk] = ("ve", c_r)
            fcs_free[gi] = ("tt", p.now("tt"))
        ev[f"h{tc2}"] = ("ve", p.now("ve"))

        for d2t in range(16):
            buf = d2t % 2
            gi = tc2 * 16 + d2t
            w = [("cpj", c_agpj), fcs_free[tc2 * 10 + 9]]
            if gi >= 2:
                w.append(pjs_free[gi - 2])
            c_pl = None
            for c8 in range(8):
                def fpl(e, d2t=d2t, c8=c8, buf=buf):
                    src = pj_ago[c8 * 10240 + d2t * 640: c8 * 10240 + d2t * 640 + 640, :].rearrange(
                        "(hc p) f -> p hc f", p=128)
                    dst = pjs_sb[buf][:, c8 * 5 * 128:(c8 * 5 + 5) * 128].rearrange("p (hc f) -> p hc f", f=128)
                    return e.dma_start(out=dst, in_=src)
                c_pl = p.E("sync", fpl, waits=w if c8 == 0 else (), inc=("sl", 16))
            bk = 2 + d2t % 2
            w2 = list(bank_wait(bk)) + [("sl", c_pl), ev[f"h{tc2}"]]
            c_mm = None
            for hc in range(40):
                def fmm(e, bk=bk, hc=hc, buf=buf):
                    return e.matmul(ps[bk][:, :], pjs_sb[buf][:, hc * 128:(hc + 1) * 128],
                                    h_sb[:, hc * 512:(hc + 1) * 512],
                                    start=(hc == 0), stop=(hc == 39))
                c_mm = p.E("tensor", fmm, waits=w2 if hc == 0 else (), inc=("tt", 1) if hc == 39 else None)
            pjs_free[gi] = ("tt", p.now("tt"))
            c_o2 = p.E("scalar", lambda e, bk=bk: e.activation(o2_sb[:, :], ps[bk][:, :], AF.Copy),
                       waits=[("tt", c_mm)], inc=("ac", 1))
            bank_free[bk] = ("ac", c_o2)
            w3 = list(bank_wait(6)) + [("ac", c_o2)]
            c_tr = None
            for k in range(4):
                def ftr(e, k=k):
                    return e.transpose(ps[6][:, k * 128:(k + 1) * 128], o2_sb[:, k * 128:(k + 1) * 128],
                                       identf_sb[:, :])
                c_tr = p.E("tensor", ftr, waits=w3 if k == 0 else (), inc=("tt", 1) if k == 3 else None)
            c_ad = None
            for k in range(4):
                tt = tc2 * 4 + k
                def fadd(e, tt=tt, k=k, d2t=d2t):
                    sl = x_sb[:, tt * 2048 + d2t * 128: tt * 2048 + d2t * 128 + 128]
                    return e.tensor_tensor(sl, sl, ps[6][:, k * 128:(k + 1) * 128], ALU.add)
                c_ad = p.E("vector", fadd, waits=[("tt", c_tr)] if k == 0 else (), inc=("ve", 1))
            bank_free[6] = ("ve", c_ad)
        ln2_done = emit_ln_batch([tc2 * 4 + tl for tl in range(4)], ln2col_sb, p.now("ve"))
        for tl in range(4):
            tt = tc2 * 4 + tl
            def fst(e, tt=tt):
                return e.dma_start(out=dout[tt * 128:(tt + 1) * 128, :],
                                   in_=x_sb[:, tt * 2048:(tt + 1) * 2048])
            st_cnt = p.E("sync", fst, waits=[("ve", ln2_done[tt])], inc=("st", 16))

    p.E("sync", lambda e: e.nop(), waits=[("st", p.now("st"))])

    with nc.Block() as block:
        block.tensor(lambda e: [f(e) for f in p.ops["tensor"]])
        block.scalar(lambda e: [f(e) for f in p.ops["scalar"]])
        block.vector(lambda e: [f(e) for f in p.ops["vector"]])
        block.gpsimd(lambda e: [f(e) for f in p.ops["gpsimd"]])
        block.sync(lambda e: [f(e) for f in p.ops["sync"]])

    return nc


def _stage_inputs(L0_attn_x, L0_attn_qkv_w, L0_attn_qkv_b, L0_attn_out_w, L0_attn_out_b,
                  L0_ffn_w_fc, L0_ffn_w_proj):
    x = np.ascontiguousarray(np.asarray(L0_attn_x, np.float32))
    qkv_w = np.asarray(L0_attn_qkv_w, np.float32)
    qkv_b = np.asarray(L0_attn_qkv_b, np.float32)
    out_w = np.asarray(L0_attn_out_w, np.float32)
    out_b = np.asarray(L0_attn_out_b, np.float32)
    wfcT = np.ascontiguousarray(np.asarray(L0_ffn_w_fc, np.float32).T)
    wpjT = np.ascontiguousarray(np.asarray(L0_ffn_w_proj, np.float32).T)

    slopes = _slopes()
    masks = np.zeros((128, 4, 512), np.float32)
    pp = np.arange(128)[:, None]
    qq = np.arange(512)[None, :]
    for v in range(4):
        masks[:, v, :] = np.where(128 * v + pp <= qq, 0.0, MASKVAL)
    masks = np.ascontiguousarray(masks.reshape(128, 4 * 512))
    ident = np.eye(128, dtype=np.float32)

    in_maps = []
    for c in range(NCORES):
        b, h2 = c // 2, c % 2
        sl = slopes[h2 * 8:(h2 + 1) * 8]
        rel = np.arange(-16, 4)
        alibi = (sl[None, :, None] * (128.0 * rel[None, None, :] + np.arange(128)[:, None, None])
                 - C_STAB).astype(np.float32)
        nq = (-sl[:, None] * np.arange(512)[None, :]).reshape(8 * 512)
        in_maps.append({
            "xT": np.ascontiguousarray(x[b].T),
            "x_own": np.ascontiguousarray(x[b, h2 * 1024:(h2 + 1) * 1024]),
            "wq": np.ascontiguousarray(qkv_w[:, h2 * 1024:(h2 + 1) * 1024]),
            "wk": np.ascontiguousarray(qkv_w[:, 2048 + h2 * 1024: 2048 + (h2 + 1) * 1024]),
            "wv": np.ascontiguousarray(qkv_w[:, 4096 + h2 * 1024: 4096 + (h2 + 1) * 1024]),
            "bqk": np.ascontiguousarray(np.concatenate([
                qkv_b[h2 * 1024:(h2 + 1) * 1024].reshape(8, 128).T,
                qkv_b[2048 + h2 * 1024: 2048 + (h2 + 1) * 1024].reshape(8, 128).T], axis=1)),
            "bv3": np.ascontiguousarray(
                _split3(qkv_b[4096 + h2 * 1024: 4096 + (h2 + 1) * 1024][None, :]).reshape(3, 1024)),
            "outw": np.ascontiguousarray(np.concatenate([
                out_w[h2 * 1024:(h2 + 1) * 1024], out_w[(1 - h2) * 1024:(2 - h2) * 1024]], axis=0)),
            "outb3": np.ascontiguousarray(_split3(out_b[None, :]).reshape(3, 2048)),
            "wfcT_sl": np.ascontiguousarray(wfcT[c * 256:(c + 1) * 256]),
            "wprojT_sl": np.ascontiguousarray(wpjT[c * 640:(c + 1) * 640]),
            "masks": masks,
            "alibi2": np.ascontiguousarray(alibi.reshape(128, 160)),
            "negsi3q": np.ascontiguousarray(_split3(nq[None, :]).reshape(3, 8 * 512)),
            "identf": ident,
        })
    return in_maps


_cached = {}


def kernel(L0_attn_x, L0_attn_qkv_w, L0_attn_qkv_b, L0_attn_out_w, L0_attn_out_b,
           L0_ffn_w_fc, L0_ffn_w_proj):
    in_maps = _stage_inputs(L0_attn_x, L0_attn_qkv_w, L0_attn_qkv_b, L0_attn_out_w,
                            L0_attn_out_b, L0_ffn_w_fc, L0_ffn_w_proj)
    if "nc" not in _cached:
        _cached["nc"] = build_program()
    nc = _cached["nc"]
    res = run_bass_kernel_spmd(nc, in_maps, core_ids=list(range(NCORES)))
    out = np.empty((B, S, D), np.float32)
    for c in range(NCORES):
        b, h2 = c // 2, c % 2
        out[b, h2 * 1024:(h2 + 1) * 1024] = res.results[c]["out"]
    return out
